# version 01
# baseline (speedup 1.0000x reference)
import contextlib
import numpy as np

import concourse.bass as bass
import concourse.mybir as mybir
from concourse.bass_utils import run_bass_kernel_spmd

F32 = mybir.dt.float32
BF16 = mybir.dt.bfloat16
I32 = mybir.dt.int32
AF = mybir.ActivationFunctionType
ALU = mybir.AluOpType
AX = mybir.AxisListType

B, L, D = 4, 2048, 2048
H, HD, C = 16, 128, 64
GLA_SCALE = float(HD) ** -0.5



class Prog:

    ENGINES = ("sync", "scalar", "vector", "tensor", "gpsimd")

    def __init__(self, nc, es):
        self.nc = nc
        self.es = es
        self.ops = []
        self.last_w = {}
        self.readers = {}

    def add(self, eng, fn, reads=(), writes=(), dma=False):
        i = len(self.ops)
        deps = set()
        for t in reads:
            j = self.last_w.get(t)
            if j is not None:
                deps.add(j)
        for t in writes:
            j = self.last_w.get(t)
            if j is not None:
                deps.add(j)
            for r in self.readers.get(t, ()):
                deps.add(r)
        deps.discard(i)
        self.ops.append([eng, fn, deps, dma, 0])
        for t in reads:
            self.readers.setdefault(t, []).append(i)
        for t in writes:
            self.last_w[t] = i
            self.readers[t] = []
        return i

    def dma(self, eng, out, in_, reads=(), writes=(), transpose=False):
        if transpose:
            fn = lambda e, o=out, s=in_: e.dma_start_transpose(o, s)
        else:
            fn = lambda e, o=out, s=in_: e.dma_start(out=o, in_=s)
        return self.add(eng, fn, reads=reads, writes=writes, dma=True)

    def finalize(self):
        nc, es = self.nc, self.es
        n = len(self.ops)
        needed = set()
        for eng, fn, deps, dma, _ in self.ops:
            for d in deps:
                if self.ops[d][0] == "tensor" and eng == "tensor":
                    continue
                needed.add(d)
        R = 8
        csem = {e: es.enter_context(nc.semaphore(f"c_{e}")) for e in self.ENGINES}
        dsem = {e: [es.enter_context(nc.semaphore(f"d_{e}{j}")) for j in range(R)]
                for e in self.ENGINES}
        ctot = {e: 0 for e in self.ENGINES}
        dcnt = {e: 0 for e in self.ENGINES}
        target = [None] * n
        prewait = [None] * n
        for i, op in enumerate(self.ops):
            eng, fn, deps, dma, _ = op
            if dma:
                slot = dcnt[eng] % R
                uses = dcnt[eng] // R
                dcnt[eng] += 1
                target[i] = ("d", (eng, slot), 16 * (uses + 1))
                op[4] = 16
                if uses > 0:
                    prewait[i] = ((eng, slot), 16 * uses)
            elif i in needed:
                ctot[eng] += 1
                target[i] = ("c", eng, ctot[eng])
                op[4] = 1
        waits = [[] for _ in range(n)]
        known = {e: {} for e in self.ENGINES}
        for i, (eng, fn, deps, dma, _) in enumerate(self.ops):
            req = {}
            if prewait[i] is not None:
                key, cnt = ("d", prewait[i][0]), prewait[i][1]
                req[key] = cnt
            for d in deps:
                if self.ops[d][0] == "tensor" and eng == "tensor":
                    continue
                tg = target[d]
                assert tg is not None
                key = (tg[0], tg[1])
                req[key] = max(req.get(key, 0), tg[2])
            k = known[eng]
            for key in sorted(req, key=repr):
                cnt = req[key]
                if k.get(key, -1) >= cnt:
                    continue
                k[key] = cnt
                waits[i].append((key, cnt))
        blk = es.enter_context(nc.Block())
        streams = {e: [] for e in self.ENGINES}
        for i, (eng, fn, deps, dma, inc) in enumerate(self.ops):
            streams[eng].append((fn, waits[i], inc, target[i]))

        def sem_of(kind, ident):
            if kind == "c":
                return csem[ident]
            return dsem[ident[0]][ident[1]]

        def mk_runner(items):
            def run(e):
                for fn, ws, inc, tg in items:
                    for (kind, ident), cnt in ws:
                        e.wait_ge(sem_of(kind, ident), cnt)
                    instr = fn(e)
                    if inc:
                        instr.then_inc(sem_of(tg[0], tg[1]), inc)
            return run

        for e in self.ENGINES:
            if streams[e]:
                getattr(blk, e)(mk_runner(streams[e]))


def ts(i, sz):
    return slice(i * sz, (i + 1) * sz)



SB = 512
NSB = L // 2 // SB


def build_A():
    nc = bass.Bass("TRN2", target_bir_lowering=False, debug=False)
    es = contextlib.ExitStack()
    es.__enter__()
    dr = {}
    for nm, shp, dt, kind in [
        ("xp", [1024, D], F32, "ExternalInput"),
        ("xo", [1024, D], F32, "ExternalInput"),
        ("wq", [D, D], F32, "ExternalInput"),
        ("wk", [D, D], F32, "ExternalInput"),
        ("wv", [D, D], F32, "ExternalInput"),
        ("wgk1", [D, 16], F32, "ExternalInput"),
        ("wgk2", [16, D], F32, "ExternalInput"),
        ("bgk2c", [128, 16], F32, "ExternalInput"),
        ("eyebf", [128, 128], BF16, "ExternalInput"),
        ("mask64", [128, C], F32, "ExternalInput"),
        ("wfcT", [D, 4 * D], F32, "ExternalInput"),
        ("wprojT", [4 * D, D], F32, "ExternalInput"),
        ("og", [1024, D], F32, "ExternalOutput"),
        ("gabs_fc", [128, 1], F32, "ExternalOutput"),
        ("gabs_proj", [128, 1], F32, "ExternalOutput"),
    ]:
        dr[nm] = nc.dram_tensor(nm, shp, dt, kind=kind).ap()

    sb = {}

    def A(nm, shp, dt=F32):
        sb[nm] = es.enter_context(nc.sbuf_tensor(nm, shp, dt))
        return sb[nm]

    A("xin", [128, 2 * D])
    A("xln", [128, 2 * D], BF16)
    A("stat", [128, 16])
    A("xhT", [128, 16 * SB], BF16)
    A("eye", [128, 128], BF16)
    A("mask", [128, C])
    A("wqs0", [128, 16 * 128], BF16); A("wqs1", [128, 16 * 128], BF16)
    A("wks0", [128, 16 * 128], BF16); A("wks1", [128, 16 * 128], BF16)
    A("wvs0", [128, 16 * 128], BF16); A("wvs1", [128, 16 * 128], BF16)
    A("wgk1s", [128, 16 * 16], BF16)
    A("wgk2s", [16, D], BF16)
    A("bgk2", [128, 16])
    A("nbgk2", [128, 16])
    A("qb", [128, 16 * SB], BF16)
    A("kb", [128, 16 * SB], BF16)
    A("vb", [128, 4 * D], BF16)
    A("gk1b", [16, SB], BF16)
    A("lsa", [128, SB]); A("lsb", [128, SB]); A("lsc", [128, SB])
    A("ell", [128, 8 * 16 * C])
    A("smask", [128, 16 * C])
    A("gpp", [128, 16 * C])
    A("dd", [128, 16 * C])
    A("ea", [128, 16 * C]); A("eb", [128, 16 * C])
    A("estate", [128, 16])
    A("qgb", [128, 16 * C], BF16); A("qhb", [128, 16 * C], BF16)
    A("khb", [128, 16 * C], BF16)
    A("kwb", [128, 16 * 128], BF16)
    A("atm", [128, C], BF16)
    A("kwc", [128, 128], BF16)
    A("S", [128, H * 128])
    A("Sb", [128, 128], BF16)
    A("ms", [128, 2])
    A("ogst", [128, D])
    ps = {}
    for nm in ("pj0", "pj1", "gk", "at", "o", "ds"):
        ps[nm] = es.enter_context(nc.psum_tensor("ps_" + nm, [128, 512], F32))
    ps["tr"] = es.enter_context(nc.psum_tensor("ps_tr", [128, 512], BF16))

    p = Prog(nc, es)

    p.dma("sync", sb["eye"][:], dr["eyebf"], writes=["eye"])
    p.dma("sync", sb["mask"][:], dr["mask64"], writes=["mask"])
    p.dma("sync", sb["bgk2"][:], dr["bgk2c"], writes=["bgk2"])
    p.dma("gpsimd", sb["wgk2s"][:], dr["wgk2"], writes=["wgk2s"])
    p.dma("gpsimd", sb["wgk1s"].ap().rearrange("p (kt o) -> p kt o", kt=16),
          dr["wgk1"].rearrange("(kt p) o -> p kt o", p=128), writes=["wgk1s"])
    p.add("vector", lambda e: e.tensor_scalar_mul(sb["nbgk2"][:], sb["bgk2"][:], -1.0),
          reads=["bgk2"], writes=["nbgk2"])
    p.add("vector", lambda e: e.memset(sb["smask"][:], 1.0), writes=["smask"])
    p.add("vector", lambda e: e.memset(sb["smask"][:, 0:16 * C:C], 0.0),
          reads=["smask"], writes=["smask"])
    p.add("vector", lambda e: e.memset(sb["S"][:], 0.0), writes=["S"])
    p.add("vector", lambda e: e.memset(sb["kwb"][:], 0.0), writes=["kwb"])
    p.add("vector", lambda e: e.memset(sb["atm"][:], 0.0), writes=["atm"])

    ell3 = sb["ell"].ap().rearrange("p (c h t) -> p c h t", c=8, h=16, t=C)

    def superblock(sbi, pref, xdr, row0):
        for tt in range(4):
            xt_ = sb["xin"][:, ts(tt % 2, D)]
            xtok = ("xin", tt % 2)
            p.dma("sync", xt_, xdr[row0 + tt * 128: row0 + (tt + 1) * 128, :],
                  writes=[xtok])
            st = sb["stat"]
            c0, c1, c2, c3 = (st[:, 4 * tt + j:4 * tt + j + 1] for j in range(4))
            p.add("vector", lambda e, x=xt_, o=c0: e.reduce_sum(o, x, AX.X),
                  reads=[xtok], writes=[("st0", tt)])
            p.add("scalar", lambda e, x=xt_, a=c1: e.activation(
                out=sb["ogst"][:], in_=x, func=AF.Square, accum_out=a),
                reads=[xtok], writes=[("st1", tt), "ogst"])
            p.add("vector", lambda e, a=c2, b=c0: e.tensor_scalar_mul(a, b, 1.0 / D),
                  reads=[("st0", tt)], writes=[("stm", tt)])
            p.add("vector", lambda e, a=c1: e.tensor_scalar(
                a, a, 1.0 / D, 1e-5, ALU.mult, ALU.add),
                reads=[("st1", tt)], writes=[("st1", tt)])
            p.add("vector", lambda e, a=c3, b=c2: e.scalar_tensor_tensor(
                a, b, 1.0, b, ALU.mult, ALU.mult),
                reads=[("stm", tt)], writes=[("stmm", tt)])
            p.add("vector", lambda e, a=c1, b=c3: e.tensor_sub(a, a, b),
                  reads=[("st1", tt), ("stmm", tt)], writes=[("st1", tt)])
            p.add("scalar", lambda e, a=c3, b=c1: e.activation(
                out=a, in_=b, func=AF.Sqrt),
                reads=[("st1", tt)], writes=[("stsq", tt)])
            p.add("vector", lambda e, a=c1, b=c3: e.reciprocal(a, b),
                  reads=[("stsq", tt)], writes=[("strs", tt)])
            p.add("vector", lambda e, x=xt_, tt=tt, m=c2, r=c1: e.tensor_scalar(
                sb["xln"][:, ts(tt % 2, D)], x, m, r, ALU.subtract, ALU.mult),
                reads=[xtok, ("stm", tt), ("strs", tt)], writes=[("xln", tt % 2)])
            for dt_ in range(16):
                p.add("tensor", lambda e, dt_=dt_, tt=tt: e.transpose(
                    ps["tr"][:, 0:128],
                    sb["xln"][:, dt_ * 128 + (tt % 2) * D: dt_ * 128 + (tt % 2) * D + 128],
                    sb["eye"][:]),
                    reads=[("xln", tt % 2), "eye"], writes=["ps_tr"])
                p.add("scalar", lambda e, dt_=dt_, tt=tt: e.copy(
                    sb["xhT"][:, dt_ * SB + tt * 128: dt_ * SB + (tt + 1) * 128],
                    ps["tr"][:, 0:128]),
                    reads=["ps_tr"], writes=[("xhT", dt_, tt)])
        xhT_toks = [("xhT", d, t) for d in range(16) for t in range(4)]
        for (wnm, bufs, dst) in (("wq", ("wqs0", "wqs1"), "qb"),
                                 ("wk", ("wks0", "wks1"), "kb")):
            if pref and wnm == "wq":
                continue
            for oi in range(16):
                wbuf = bufs[oi % 2]
                p.dma("gpsimd", sb[wbuf].ap().rearrange("p (kt o) -> p kt o", kt=16),
                      dr[wnm][:, ts(oi, 128)].rearrange("(kt p) o -> p kt o", p=128),
                      writes=[(wbuf,)])
                for ki in range(16):
                    p.add("tensor", lambda e, ki=ki, wbuf=wbuf: e.matmul(
                        ps["pj0"][:, 0:SB], sb[wbuf][:, ts(ki, 128)],
                        sb["xhT"][:, ts(ki, SB)], start=(ki == 0), stop=(ki == 15)),
                        reads=[(wbuf,)] + xhT_toks, writes=["ps_pj0"])
                p.add("scalar", lambda e, oi=oi, dst=dst: e.copy(
                    sb[dst][:, ts(oi, SB)], ps["pj0"][:, 0:SB]),
                    reads=["ps_pj0"], writes=[(dst, oi)])
        for ki in range(16):
            p.add("tensor", lambda e, ki=ki: e.matmul(
                ps["gk"][0:16, 0:SB], sb["wgk1s"][:, ts(ki, 16)],
                sb["xhT"][:, ts(ki, SB)], start=(ki == 0), stop=(ki == 15)),
                reads=["wgk1s"] + xhT_toks, writes=["ps_gk"])
        p.add("vector", lambda e: e.tensor_copy(sb["gk1b"][:], ps["gk"][0:16, 0:SB]),
              reads=["ps_gk"], writes=["gk1b"])
        for h in range(16):
            p.add("tensor", lambda e, h=h: e.matmul(
                ps["pj1"][:, 0:SB], sb["wgk2s"][0:16, ts(h, 128)], sb["gk1b"][:],
                start=True, stop=True),
                reads=["wgk2s", "gk1b"], writes=["ps_pj1"])
            bcol = sb["bgk2"][:, h:h + 1]
            nbcol = sb["nbgk2"][:, h:h + 1]
            p.add("scalar", lambda e, b=bcol: e.activation(
                out=sb["lsa"][:], in_=ps["pj1"][:, 0:SB], func=AF.Abs, bias=b),
                reads=["ps_pj1", "bgk2"], writes=["lsa"])
            p.add("scalar", lambda e: e.activation(
                out=sb["lsb"][:], in_=sb["lsa"][:], func=AF.Exp, scale=-1.0),
                reads=["lsa"], writes=["lsb"])
            p.add("scalar", lambda e: e.activation(
                out=sb["lsc"][:], in_=sb["lsb"][:], func=AF.Ln, bias=1.0),
                reads=["lsb"], writes=["lsc"])
            p.add("scalar", lambda e, nb=nbcol: e.activation(
                out=sb["lsa"][:], in_=ps["pj1"][:, 0:SB], func=AF.Relu,
                bias=nb, scale=-1.0),
                reads=["ps_pj1", "nbgk2"], writes=["lsa"])
            out3 = ell3[:, :, h, :]
            in3a = sb["lsa"].ap().rearrange("p (c t) -> p c t", c=8, t=C)
            in3b = sb["lsc"].ap().rearrange("p (c t) -> p c t", c=8, t=C)
            p.add("vector", lambda e, o=out3, a=in3a, b=in3b: e.tensor_add(o, a, b),
                  reads=["lsa", "lsc"], writes=[("ell", h)])
        for ni in range(16):
            wv_buf = ("wvs0", "wvs1")[ni % 2]
            p.dma("gpsimd", sb[wv_buf].ap().rearrange("p (kt o) -> p kt o", kt=16),
                  dr["wv"][:, ts(ni, 128)].rearrange("(kt p) o -> p kt o", p=128),
                  writes=[(wv_buf,)])
            for tt in range(4):
                for ki in range(16):
                    p.add("tensor", lambda e, tt=tt, ki=ki, wv_buf=wv_buf: e.matmul(
                        ps["pj1"][:, 0:128],
                        sb["xhT"][:, ki * SB + tt * 128: ki * SB + (tt + 1) * 128],
                        sb[wv_buf][:, ts(ki, 128)], start=(ki == 0), stop=(ki == 15)),
                        reads=[(wv_buf,)] + xhT_toks, writes=["ps_pj1"])
                p.add("scalar", lambda e, ni=ni, tt=tt: e.copy(
                    sb["vb"][:, tt * D + ni * 128: tt * D + (ni + 1) * 128],
                    ps["pj1"][:, 0:128]),
                    reads=["ps_pj1"], writes=[("vb", tt, ni)])
        vb_toks = [("vb", t, n) for t in range(4) for n in range(16)]
        r16 = lambda t_: sb[t_].ap().rearrange("p (h t) -> p h t", h=16, t=C)
        for ch in range(8):
            tt, par = ch // 2, ch % 2
            r0 = par * 64
            p.add("vector", lambda e, ch=ch: e.tensor_tensor_scan(
                sb["gpp"][:], sb["smask"][:], sb["ell"][:, ts(ch, 16 * C)],
                0.0, ALU.mult, ALU.add),
                reads=["smask"] + [("ell", h) for h in range(16)], writes=["gpp"])
            gb = r16("gpp")
            gb0 = gb[:, :, 0:1].broadcast_to([128, 16, C])
            gbl = gb[:, :, C - 1:C].broadcast_to([128, 16, C])
            qb_toks = [("qb", oi) for oi in range(16)]
            kb_toks = [("kb", oi) for oi in range(16)]
            ch_sl = lambda t_: sb[t_].ap().rearrange(
                "p (h t) -> p h t", h=16, t=SB)[:, :, ch * C:(ch + 1) * C]
            if not pref:
                p.add("gpsimd", lambda e, a=gb, b=gb0: e.tensor_sub(r16("dd"), a, b),
                      reads=["gpp"], writes=["dd"])
                p.add("scalar", lambda e: e.activation(
                    out=sb["ea"][:], in_=sb["dd"][:], func=AF.Exp, scale=-1.0 / 16),
                    reads=["dd"], writes=["ea"])
                p.add("scalar", lambda e: e.activation(
                    out=sb["eb"][:], in_=sb["dd"][:], func=AF.Exp, scale=1.0 / 16),
                    reads=["dd"], writes=["eb"])
                p.add("vector", lambda e, q=ch_sl("qb"): e.scalar_tensor_tensor(
                    r16("qhb"), q, GLA_SCALE, r16("ea"), ALU.mult, ALU.mult),
                    reads=qb_toks + ["ea"], writes=["qhb"])
                p.add("gpsimd", lambda e, k=ch_sl("kb"): e.tensor_mul(
                    r16("khb"), k, r16("eb")),
                    reads=kb_toks + ["eb"], writes=["khb"])
            p.add("gpsimd", lambda e, a=gb, b=gbl: e.tensor_sub(r16("dd"), a, b),
                  reads=["gpp", "ea", "eb"] if not pref else ["gpp"], writes=["dd"])
            p.add("scalar", lambda e: e.activation(
                out=sb["ea"][:], in_=sb["dd"][:], func=AF.Exp, scale=1.0 / 16),
                reads=["dd"], writes=["ea"])
            kw3out = sb["kwb"].ap().rearrange("p (h t) -> p h t", h=16, t=128)[
                :, :, r0:r0 + 64]
            p.add("gpsimd", lambda e, k=ch_sl("kb"), o=kw3out: e.tensor_mul(
                o, k, r16("ea")),
                reads=kb_toks + ["ea", "kwb"], writes=["kwb"])
            if not pref:
                p.add("scalar", lambda e: e.activation(
                    out=sb["eb"][:], in_=sb["gpp"][:], func=AF.Exp, scale=-1.0 / 16),
                    reads=["gpp", "khb"], writes=["eb"])
                p.add("vector", lambda e, q=ch_sl("qb"): e.scalar_tensor_tensor(
                    r16("qgb"), q, GLA_SCALE, r16("eb"), ALU.mult, ALU.mult),
                    reads=qb_toks + ["eb"], writes=["qgb"])
            p.add("scalar", lambda e: e.activation(
                out=sb["estate"][:], in_=sb["gpp"][:, C - 1:16 * C:C], func=AF.Exp,
                scale=-1.0 / 16),
                reads=["gpp"], writes=["estate"])
            for h in range(16):
                vsl = sb["vb"][r0:r0 + 64, tt * D + h * 128: tt * D + (h + 1) * 128]
                if not pref:
                    p.add("vector", lambda e, h=h: e.tensor_copy(
                        sb["Sb"][:], sb["S"][:, ts(h, 128)]),
                        reads=[("S", h)], writes=["Sb"])
                    p.add("tensor", lambda e, h=h, r0=r0: e.matmul(
                        ps["at"][r0:r0 + 64, 0:64], sb["khb"][:, ts(h, C)],
                        sb["qhb"][:, ts(h, C)], start=True, stop=True,
                        tile_position=(0, r0)),
                        reads=["khb", "qhb"], writes=["ps_at"])
                    p.add("vector", lambda e, r0=r0: e.tensor_mul(
                        sb["atm"][r0:r0 + 64, :], ps["at"][r0:r0 + 64, 0:64],
                        sb["mask"][r0:r0 + 64, :]),
                        reads=["ps_at", "mask", "atm"], writes=["atm"])
                    p.add("tensor", lambda e, h=h, r0=r0: e.matmul(
                        ps["o"][r0:r0 + 64, 0:128], sb["qgb"][:, ts(h, C)],
                        sb["Sb"][:], start=True, stop=False, tile_position=(0, r0)),
                        reads=["qgb", "Sb"], writes=["ps_o"])
                    p.add("tensor", lambda e, r0=r0, v=vsl: e.matmul(
                        ps["o"][r0:r0 + 64, 0:128], sb["atm"][r0:r0 + 64, :],
                        v, start=False, stop=True, tile_position=(r0, r0)),
                        reads=["atm"] + vb_toks, writes=["ps_o"])
                p.add("tensor", lambda e, h=h: e.transpose(
                    ps["tr"][:, 0:128], sb["kwb"][:, ts(h, 128)], sb["eye"][:]),
                    reads=["kwb", "eye"], writes=["ps_tr"])
                p.add("scalar", lambda e, r0=r0: e.copy(
                    sb["kwc"][r0:r0 + 64, :], ps["tr"][r0:r0 + 64, 0:128]),
                    reads=["ps_tr"], writes=["kwc"])
                p.add("tensor", lambda e, r0=r0, v=vsl: e.matmul(
                    ps["ds"][:, 0:128], sb["kwc"][r0:r0 + 64, :], v,
                    start=True, stop=True, tile_position=(r0, 0)),
                    reads=["kwc"] + vb_toks, writes=["ps_ds"])
                p.add("vector", lambda e, h=h: e.scalar_tensor_tensor(
                    sb["S"][:, ts(h, 128)], sb["S"][:, ts(h, 128)],
                    sb["estate"][:, h:h + 1], ps["ds"][:, 0:128], ALU.mult, ALU.add),
                    reads=[("S", h), "estate", "ps_ds"], writes=[("S", h)])
                if not pref:
                    p.add("scalar", lambda e, r0=r0: e.activation(
                        out=sb["lsa"][r0:r0 + 64, 0:128], in_=ps["o"][r0:r0 + 64, 0:128],
                        func=AF.Square, accum_out=sb["ms"][r0:r0 + 64, 0:1]),
                        reads=["ps_o"], writes=["lsa", "ms0"])
                    p.add("vector", lambda e, r0=r0: e.tensor_scalar(
                        sb["ms"][r0:r0 + 64, 1:2], sb["ms"][r0:r0 + 64, 0:1],
                        1.0 / 128, 1e-5, ALU.mult, ALU.add),
                        reads=["ms0"], writes=["ms1"])
                    p.add("vector", lambda e, r0=r0: e.reciprocal(
                        sb["ms"][r0:r0 + 64, 0:1], sb["ms"][r0:r0 + 64, 1:2]),
                        reads=["ms1"], writes=["ms0"])
                    p.add("scalar", lambda e, r0=r0: e.activation(
                        out=sb["ms"][r0:r0 + 64, 1:2], in_=sb["ms"][r0:r0 + 64, 0:1],
                        func=AF.Sqrt),
                        reads=["ms0"], writes=["ms1"])
                    p.add("vector", lambda e, h=h, r0=r0: e.tensor_scalar_mul(
                        sb["ogst"][r0:r0 + 64, ts(h, 128)],
                        ps["o"][r0:r0 + 64, 0:128], sb["ms"][r0:r0 + 64, 1:2]),
                        reads=["ps_o", "ms1"], writes=["ogst"])
            if (not pref) and par == 1:
                p.dma("gpsimd", dr["og"][row0 + tt * 128: row0 + (tt + 1) * 128, :],
                      sb["ogst"][:],
                      reads=["ogst"], writes=[("og_out", sbi, tt)])

    A2 = lambda nm, shp, dt=F32: A(nm, shp, dt)
    A2("gw0", [128, 512]); A2("gw1", [128, 512])
    A2("gacc", [128, 4])
    p.add("vector", lambda e: e.memset(sb["gacc"][:], 0.0), writes=["gacc"])
    for gi, (wnm, onm, acol) in enumerate(
            [("wfcT", "gabs_fc", 0), ("wprojT", "gabs_proj", 1)]):
        nrow, ncol = dr[wnm].shape
        nt = (nrow // 128) * (ncol // 512)
        cpr = ncol // 512
        for t_ in range(nt):
            r, c_ = t_ // cpr, t_ % cpr
            buf = ("gw0", "gw1")[t_ % 2]
            p.dma("sync", sb[buf][:],
                  dr[wnm][r * 128:(r + 1) * 128, c_ * 512:(c_ + 1) * 512],
                  writes=[(buf,)])
            p.add("vector", lambda e, buf=buf: e.tensor_reduce(
                sb["gacc"][:, 3:4], sb[buf][:], AX.X, ALU.add,
                apply_absolute_value=True),
                reads=[(buf,)], writes=["gacc3"])
            p.add("vector", lambda e, acol=acol: e.tensor_add(
                sb["gacc"][:, acol:acol + 1], sb["gacc"][:, acol:acol + 1],
                sb["gacc"][:, 3:4]),
                reads=["gacc3", ("gacca", acol)], writes=[("gacca", acol)])
        p.dma("gpsimd", dr[onm], sb["gacc"][:, acol:acol + 1],
              reads=[("gacca", acol)], writes=[("gout", gi)])

    for i in range(NSB):
        superblock(i, True, dr["xp"], i * SB)
    for i in range(NSB):
        superblock(NSB + i, False, dr["xo"], i * SB)

    p.add("gpsimd", lambda e: e.nop(),
          reads=[("og_out", NSB + s, t) for s in range(NSB) for t in range(4)]
                + [("gout", 0), ("gout", 1)])
    p.finalize()
    return nc, es




def build_B():
    nc = bass.Bass("TRN2", target_bir_lowering=False, debug=False)
    es = contextlib.ExitStack()
    es.__enter__()
    dr = {}
    for nm, shp, dt, kind in [
        ("x", [1024, D], F32, "ExternalInput"),
        ("og", [1024, D], F32, "ExternalInput"),
        ("wg", [D, D], F32, "ExternalInput"),
        ("wo", [D, D], F32, "ExternalInput"),
        ("gnormc", [128, 1], F32, "ExternalInput"),
        ("wfcT", [D, 4 * D], F32, "ExternalInput"),
        ("wprojT", [4 * D, D], F32, "ExternalInput"),
        ("w_inT", [D, D], F32, "ExternalInput"),
        ("binrow", [1, D], F32, "ExternalInput"),
        ("gfc", [128, 1], F32, "ExternalInput"),
        ("gproj", [128, 1], F32, "ExternalInput"),
        ("eyebf", [128, 128], BF16, "ExternalInput"),
        ("x3", [1024, D], F32, "ExternalOutput"),
        ("h", [1024, D], F32, "ExternalOutput"),
        ("tfc", [D, 4 * D], BF16, "Internal"),
        ("tproj", [4 * D, D], BF16, "Internal"),
        ("x2s", [1024, D], F32, "Internal"),
    ]:
        dr[nm] = nc.dram_tensor(nm, shp, dt, kind=kind).ap()

    sb = {}

    def A(nm, shp, dt=F32):
        sb[nm] = es.enter_context(nc.sbuf_tensor(nm, shp, dt))
        return sb[nm]

    A("eye", [128, 128], BF16)
    A("ones", [128, 1])
    A("half", [128, 1]); A("nhalf", [128, 1])
    A("gcol", [128, 4])
    A("gnc", [128, 1])
    A("cell", [1, 4])
    A("wt0", [128, 512]); A("wt1", [128, 512])
    A("sg1", [128, 512], BF16); A("sg2", [128, 512], BF16)
    A("tau0", [128, 512], BF16); A("tau1", [128, 512], BF16)
    A("xt0", [128, D])
    A("stat", [128, 24])
    A("xln", [128, D], BF16)
    A("onesb", [1, 128], BF16)
    A("xhT", [128, 16 * 1024], BF16)
    A("wstr0", [128, 16 * 512], BF16); A("wstr1", [128, 16 * 512], BF16)
    A("ogt", [128, 2 * D])
    A("sgt", [128, 512])
    A("oggT", [128, 16 * 256], BF16)
    A("ao", [128, 2 * D])
    A("dmy", [128, D])
    A("hst", [128, 4 * D])
    A("hT", [128, 64 * 128], BF16)
    A("x2t", [128, D])
    A("o2", [128, D])
    A("bin", [1, D], BF16)
    ps = {}
    for nm in ("p0", "p1", "c"):
        ps[nm] = es.enter_context(nc.psum_tensor("psB_" + nm, [128, 512], F32))
    ps["tr"] = es.enter_context(nc.psum_tensor("psB_tr", [128, 512], BF16))

    p = Prog(nc, es)
    p.dma("sync", sb["eye"][:], dr["eyebf"], writes=["eye"])
    p.dma("sync", sb["gcol"][:, 0:1], dr["gfc"], writes=["gin0"])
    p.dma("sync", sb["gcol"][:, 2:3], dr["gproj"], writes=["gin2"])
    p.dma("sync", sb["gnc"][:], dr["gnormc"], writes=["gnc"])
    p.dma("gpsimd", sb["bin"][:], dr["binrow"], writes=["bin"])
    p.add("vector", lambda e: e.memset(sb["ones"][:], 1.0), writes=["ones"])
    p.add("vector", lambda e: e.memset(sb["onesb"][:], 1.0), writes=["onesb"])
    p.add("vector", lambda e: e.memset(sb["half"][:], 0.5), writes=["half"])
    p.add("vector", lambda e: e.memset(sb["nhalf"][:], -0.5), writes=["nhalf"])

    NW = float(D * 4 * D)
    for gi, incol in enumerate((0, 2)):
        gtok = "gin0" if incol == 0 else "gin2"
        p.add("tensor", lambda e, incol=incol: e.matmul(
            ps["c"][0:1, 0:1], sb["gcol"][:, incol:incol + 1], sb["ones"][:],
            start=True, stop=True),
            reads=[gtok, "ones"], writes=["ps_c"])
        p.add("scalar", lambda e, gi=gi: e.copy(
            sb["cell"][0:1, gi * 2:gi * 2 + 1], ps["c"][0:1, 0:1]),
            reads=["ps_c"], writes=[("cl", gi, 0)])
        p.add("vector", lambda e, gi=gi: e.tensor_scalar(
            sb["cell"][0:1, gi * 2 + 1:gi * 2 + 2],
            sb["cell"][0:1, gi * 2:gi * 2 + 1], 1.0 / NW, 1e-6, ALU.mult, ALU.add),
            reads=[("cl", gi, 0)], writes=[("cl", gi, 1)])
        p.add("vector", lambda e, gi=gi: e.reciprocal(
            sb["cell"][0:1, gi * 2 + 1:gi * 2 + 2],
            sb["cell"][0:1, gi * 2 + 1:gi * 2 + 2]),
            reads=[("cl", gi, 1)], writes=[("cl", gi, 1)])
        p.add("vector", lambda e, gi=gi: e.tensor_scalar_mul(
            sb["cell"][0:1, gi * 2:gi * 2 + 1],
            sb["cell"][0:1, gi * 2:gi * 2 + 1], 0.5 / NW),
            reads=[("cl", gi, 0)], writes=[("cl", gi, 0)])
        p.add("tensor", lambda e, gi=gi: e.matmul(
            ps["c"][:, 0:2], sb["ones"][0:1, 0:1].broadcast_to([1, 128]),
            sb["cell"][0:1, gi * 2:gi * 2 + 2], start=True, stop=True),
            reads=[("cl", gi, 0), ("cl", gi, 1), "ones"], writes=["ps_c"])
        p.add("scalar", lambda e, incol=incol: e.copy(
            sb["gcol"][:, incol:incol + 2], ps["c"][:, 0:2]),
            reads=["ps_c"], writes=[("g2", incol), gtok])

    def ternarize(wnm, onm, rgcol):
        nrow, ncol = dr[wnm].shape
        for r in range(nrow // 128):
            for c_ in range(ncol // 512):
                buf, tau = (("wt0", "tau0"),
                            ("wt1", "tau1"))[(r * (ncol // 512) + c_) % 2]
                p.dma("sync", sb[buf][:],
                      dr[wnm][r * 128:(r + 1) * 128, c_ * 512:(c_ + 1) * 512],
                      writes=[(buf,)])
                p.add("scalar", lambda e, buf=buf, rgcol=rgcol: e.activation(
                    out=sb["sg1"][:], in_=sb[buf][:], func=AF.Sign,
                    bias=sb["nhalf"][:, 0:1], scale=sb["gcol"][:, rgcol:rgcol + 1]),
                    reads=[(buf,), "nhalf", ("g2", rgcol - 1)], writes=["sg1"])
                p.add("scalar", lambda e, buf=buf, rgcol=rgcol: e.activation(
                    out=sb["sg2"][:], in_=sb[buf][:], func=AF.Sign,
                    bias=sb["half"][:, 0:1], scale=sb["gcol"][:, rgcol:rgcol + 1]),
                    reads=[(buf,), "half", ("g2", rgcol - 1)], writes=["sg2"])
                p.add("vector", lambda e, tau=tau: e.tensor_add(
                    sb[tau][:], sb["sg1"][:], sb["sg2"][:]),
                    reads=["sg1", "sg2"], writes=[(tau,)])
                p.dma("gpsimd",
                      dr[onm][r * 128:(r + 1) * 128, c_ * 512:(c_ + 1) * 512],
                      sb[tau][:], reads=[(tau,)], writes=[("tw", onm)])
    ternarize("wfcT", "tfc", 1)
    ternarize("wprojT", "tproj", 3)

    def ln_tile(xt_, xtoks, eps, outsl, outtok, sidx):
        if not isinstance(xtoks, list):
            xtoks = [xtoks]
        st = sb["stat"]
        c0, c1, c2, c3 = (st[:, 4 * sidx + j:4 * sidx + j + 1] for j in range(4))
        nf = 1.0 / xt_.shape[1]
        p.add("vector", lambda e, x=xt_, o=c0: e.reduce_sum(o, x, AX.X),
              reads=xtoks, writes=[("s0", sidx)])
        p.add("scalar", lambda e, x=xt_, a=c1: e.activation(
            out=sb["dmy"][:, 0:x.shape[1]], in_=x, func=AF.Square, accum_out=a),
            reads=xtoks, writes=[("s1", sidx), "lndmy"])
        p.add("vector", lambda e, a=c2, b=c0, nf=nf: e.tensor_scalar_mul(a, b, nf),
              reads=[("s0", sidx)], writes=[("sm", sidx)])
        p.add("vector", lambda e, a=c1, nf=nf, eps=eps: e.tensor_scalar(
            a, a, nf, eps, ALU.mult, ALU.add),
            reads=[("s1", sidx)], writes=[("s1", sidx)])
        p.add("vector", lambda e, a=c3, b=c2: e.scalar_tensor_tensor(
            a, b, 1.0, b, ALU.mult, ALU.mult),
            reads=[("sm", sidx)], writes=[("smm", sidx)])
        p.add("vector", lambda e, a=c1, b=c3: e.tensor_sub(a, a, b),
              reads=[("s1", sidx), ("smm", sidx)], writes=[("s1", sidx)])
        p.add("scalar", lambda e, a=c3, b=c1: e.activation(out=a, in_=b, func=AF.Sqrt),
              reads=[("s1", sidx)], writes=[("ssq", sidx)])
        p.add("vector", lambda e, a=c1, b=c3: e.reciprocal(a, b),
              reads=[("ssq", sidx)], writes=[("srs", sidx)])
        p.add("vector", lambda e, x=xt_, m=c2, r=c1, o=outsl: e.tensor_scalar(
            o, x, m, r, ALU.subtract, ALU.mult),
            reads=xtoks + [("sm", sidx), ("srs", sidx)], writes=[outtok])

    def tile_transposes(ntile, src_fn, srctoks, dst_fn, dsttok_fn):
        for dt_ in range(ntile):
            p.add("tensor", lambda e, dt_=dt_: e.transpose(
                ps["tr"][:, 0:128], src_fn(dt_), sb["eye"][:]),
                reads=srctoks + ["eye"], writes=["ps_tr"])
            p.add("scalar", lambda e, dt_=dt_: e.copy(dst_fn(dt_), ps["tr"][:, 0:128]),
                  reads=["ps_tr"], writes=[dsttok_fn(dt_)])

    for tt in range(8):
        xt_ = sb["xt0"][:]
        xtok = "xld"
        p.dma("sync", xt_, dr["x"][tt * 128:(tt + 1) * 128, :], writes=[xtok])
        ln_tile(xt_, xtok, 1e-5, sb["xln"][:], "xln", tt % 4)
        tile_transposes(
            16, lambda d: sb["xln"][:, ts(d, 128)], ["xln"],
            lambda d, tt=tt: sb["xhT"][:, d * 1024 + tt * 128: d * 1024 + (tt + 1) * 128],
            lambda d, tt=tt: ("xhT", tt))
    xhT_toks = [("xhT", t) for t in range(8)]

    for sbi in range(4):
        for tt in range(2):
            p.dma("sync", sb["ogt"][:, ts(tt, D)],
                  dr["og"][sbi * 256 + tt * 128: sbi * 256 + (tt + 1) * 128, :],
                  reads=["lndmy2"], writes=[("og4", tt)])
        for oc in range(4):
            wbuf = ("wstr0", "wstr1")[oc % 2]
            p.dma("gpsimd", sb[wbuf].ap().rearrange("p (kt o) -> p kt o", kt=16),
                  dr["wg"][:, ts(oc, 512)].rearrange("(kt p) o -> p kt o", p=128),
                  writes=[(wbuf,)])
            for tt in range(2):
                gt = sbi * 2 + tt
                for ki in range(16):
                    p.add("tensor", lambda e, ki=ki, wbuf=wbuf, gt=gt: e.matmul(
                        ps["p0"][:, 0:512],
                        sb["xhT"][:, ki * 1024 + gt * 128: ki * 1024 + (gt + 1) * 128],
                        sb[wbuf][:, ts(ki, 512)], start=(ki == 0), stop=(ki == 15)),
                        reads=[(wbuf,)] + xhT_toks, writes=["ps_p0"])
                p.add("scalar", lambda e: e.activation(
                    out=sb["sgt"][:], in_=ps["p0"][:, 0:512], func=AF.Silu),
                    reads=["ps_p0"], writes=["sgt"])
                p.add("vector", lambda e, tt=tt, oc=oc: e.tensor_mul(
                    sb["ao"].ap().bitcast(BF16)[:, tt * D + oc * 512:
                                                tt * D + (oc + 1) * 512],
                    sb["ogt"][:, tt * D + oc * 512: tt * D + (oc + 1) * 512],
                    sb["sgt"][:]),
                    reads=["sgt", ("og4", tt)], writes=[("aox",)])
        ogg_toks = [("aox",)]
        tile_transposes(
            32, lambda d: sb["ao"].ap().bitcast(BF16)[
                :, (d % 16) * 128 + (d // 16) * D:
                (d % 16) * 128 + (d // 16) * D + 128],
            ogg_toks,
            lambda d: sb["oggT"][:, (d % 16) * 256 + (d // 16) * 128:
                                 (d % 16) * 256 + (d // 16) * 128 + 128],
            lambda d: ("oggT", d % 16))
        oggT_toks = [("oggT", d) for d in range(16)]
        for oc in range(4):
            wbuf = ("wstr0", "wstr1")[oc % 2]
            p.dma("gpsimd", sb[wbuf].ap().rearrange("p (kt o) -> p kt o", kt=16),
                  dr["wo"][:, ts(oc, 512)].rearrange("(kt p) o -> p kt o", p=128),
                  writes=[(wbuf,)])
            p.add("vector", lambda e, wbuf=wbuf: e.tensor_scalar_mul(
                sb[wbuf][:], sb[wbuf][:], sb["gnc"][:, 0:1]),
                reads=[(wbuf,), "gnc"], writes=[(wbuf,)])
            for tt in range(2):
                for ki in range(16):
                    p.add("tensor", lambda e, ki=ki, wbuf=wbuf, tt=tt: e.matmul(
                        ps["p1"][:, 0:512],
                        sb["oggT"][:, ki * 256 + tt * 128: ki * 256 + (tt + 1) * 128],
                        sb[wbuf][:, ts(ki, 512)], start=(ki == 0), stop=(ki == 15)),
                        reads=[(wbuf,)] + oggT_toks, writes=["ps_p1"])
                p.add("scalar", lambda e, tt=tt, oc=oc: e.copy(
                    sb["ao"][:, tt * D + oc * 512: tt * D + (oc + 1) * 512],
                    ps["p1"][:, 0:512]),
                    reads=["ps_p1"], writes=[("aox",), ("aow", tt, oc)])
        for tt in range(2):
            gt = sbi * 2 + tt
            xt_ = sb["xt0"][:]
            xtok = "xld"
            p.dma("sync", xt_, dr["x"][gt * 128:(gt + 1) * 128, :], writes=[xtok])
            aosl = sb["ao"][:, ts(tt, D)]
            ao_toks = [("aow", tt, o) for o in range(4)] + [("aox",)]
            ln_tile(aosl, ao_toks, 1e-5, sb["x2t"][:], "x2w", tt)
            p.add("vector", lambda e, x=xt_: e.tensor_add(
                sb["x2t"][:], sb["x2t"][:], x),
                reads=["x2w", xtok], writes=["x2w"])
            p.dma("gpsimd", dr["x2s"][gt * 128:(gt + 1) * 128, :], sb["x2t"][:],
                  reads=["x2w"], writes=[("x2s", gt)])

    for gt in range(8):
        xt_ = sb["xt0"][:]
        xtok = "xld"
        p.dma("sync", xt_, dr["x2s"][gt * 128:(gt + 1) * 128, :],
              reads=[("x2s", gt)], writes=[xtok])
        ln_tile(xt_, xtok, 1e-6, sb["xln"][:], "xln", 0)
        tile_transposes(
            16, lambda d: sb["xln"][:, ts(d, 128)], ["xln"],
            lambda d: sb["oggT"][:, ts(d, 128)][:, 0:128] if False else
                      sb["oggT"][:, d * 128: (d + 1) * 128],
            lambda d: ("oggT", d))
        x2T_toks = [("oggT", d) for d in range(16)]
        for oc in range(16):
            wbuf = ("wstr0", "wstr1")[oc % 2]
            p.dma("sync", sb[wbuf].ap().rearrange("p (kt o) -> p kt o", kt=16),
                  dr["tfc"][:, ts(oc, 512)].rearrange("(kt p) o -> p kt o", p=128),
                  reads=[("tw", "tfc")], writes=[(wbuf,)])
            for ki in range(16):
                p.add("tensor", lambda e, ki=ki, wbuf=wbuf: e.matmul(
                    ps["p0"][:, 0:512], sb["oggT"][:, ts(ki, 128)],
                    sb[wbuf][:, ts(ki, 512)], start=(ki == 0), stop=(ki == 15)),
                    reads=[(wbuf,)] + x2T_toks, writes=["ps_p0"])
            p.add("scalar", lambda e, oc=oc: e.activation(
                out=sb["hst"][:, ts(oc, 512)], in_=ps["p0"][:, 0:512], func=AF.Gelu,
                scale=sb["gcol"][:, 0:1]),
                reads=["ps_p0", ("g2", 0)], writes=[("hs", oc)])
        hs_toks = [("hs", o) for o in range(16)]
        st = sb["stat"]
        c0, c1, c2, c3 = (st[:, 16 + j:16 + j + 1] for j in range(4))
        nf = 1.0 / (4 * D)
        p.add("vector", lambda e, o=c0: e.reduce_sum(o, sb["hst"][:], AX.X),
              reads=hs_toks, writes=["h0"])
        for hc in range(4):
            p.add("scalar", lambda e, hc=hc, a=c1: e.activation(
                out=sb["dmy"][:, 0:D],
                in_=sb["hst"][:, hc * D:(hc + 1) * D],
                func=AF.Square,
                accum_out=(sb["stat"][:, 20 + hc:21 + hc] if hc > 0 else a)),
                reads=hs_toks, writes=[("hsq4", hc), "lndmy"])
        for hc in range(1, 4):
            p.add("vector", lambda e, hc=hc, a=c1: e.tensor_add(
                a, a, sb["stat"][:, 20 + hc:21 + hc]),
                reads=[("hsq4", 0), ("hsq4", hc)], writes=[("hsq4", 0)])
        p.add("vector", lambda e, a=c1: e.tensor_copy(a, a),
              reads=[("hsq4", 0)], writes=["h1"])
        p.add("vector", lambda e, a=c2, b=c0, nf=nf: e.tensor_scalar_mul(a, b, nf),
              reads=["h0"], writes=["hm"])
        p.add("vector", lambda e, a=c1, nf=nf: e.tensor_scalar(
            a, a, nf, 1e-6, ALU.mult, ALU.add), reads=["h1"], writes=["h1"])
        p.add("vector", lambda e, a=c3, b=c2: e.scalar_tensor_tensor(
            a, b, 1.0, b, ALU.mult, ALU.mult), reads=["hm"], writes=["hmm"])
        p.add("vector", lambda e, a=c1, b=c3: e.tensor_sub(a, a, b),
              reads=["h1", "hmm"], writes=["h1"])
        p.add("scalar", lambda e, a=c3, b=c1: e.activation(out=a, in_=b, func=AF.Sqrt),
              reads=["h1"], writes=["hsq"])
        p.add("vector", lambda e, a=c1, b=c3: e.reciprocal(a, b),
              reads=["hsq"], writes=["hrs"])
        for dt_ in range(64):
            p.add("vector", lambda e, dt_=dt_, m=c2, r=c1: e.tensor_scalar(
                sb["xln"][:, 0:128], sb["hst"][:, ts(dt_, 128)], m, r,
                ALU.subtract, ALU.mult),
                reads=hs_toks + ["hm", "hrs", "lndmy"], writes=["xln"])
            p.add("tensor", lambda e: e.transpose(
                ps["tr"][:, 0:128], sb["xln"][:, 0:128], sb["eye"][:]),
                reads=["xln", "eye"], writes=["ps_tr"])
            p.add("scalar", lambda e, dt_=dt_: e.copy(
                sb["hT"][:, ts(dt_, 128)], ps["tr"][:, 0:128]),
                reads=["ps_tr"], writes=[("hT", dt_)])
        hT_toks = [("hT", d) for d in range(64)]
        for oc in range(4):
            for kg in range(4):
                wbuf = ("wstr0", "wstr1")[kg % 2]
                p.dma("sync", sb[wbuf].ap().rearrange("p (kt o) -> p kt o", kt=16),
                      dr["tproj"][kg * 2048:(kg + 1) * 2048, ts(oc, 512)].rearrange(
                          "(kt p) o -> p kt o", p=128),
                      reads=[("tw", "tproj")], writes=[(wbuf,)])
                for k2 in range(16):
                    ki = kg * 16 + k2
                    p.add("tensor", lambda e, ki=ki, k2=k2, wbuf=wbuf, kg=kg: e.matmul(
                        ps["p1"][:, 0:512], sb["hT"][:, ts(ki, 128)],
                        sb[wbuf][:, ts(k2, 512)],
                        start=(ki == 0), stop=(ki == 63)),
                        reads=[(wbuf,)] + hT_toks, writes=["ps_p1"])
            p.add("scalar", lambda e, oc=oc: e.activation(
                out=sb["o2"][:, ts(oc, 512)], in_=ps["p1"][:, 0:512],
                func=AF.Identity, scale=sb["gcol"][:, 2:3]),
                reads=["ps_p1", ("g2", 2)], writes=[("o2w", oc)])
        o2_toks = [("o2w", o) for o in range(4)]
        ln_tile(sb["o2"][:], o2_toks, 1e-5, sb["x2t"][:], "x2w", 1)
        p.add("vector", lambda e: e.tensor_add(sb["x2t"][:], sb["x2t"][:], sb["xt0"][:]),
              reads=["x2w", "xld"], writes=["x2w"])
        p.dma("gpsimd", dr["x3"][gt * 128:(gt + 1) * 128, :], sb["x2t"][:],
              reads=["x2w"], writes=[("x3out", gt)])
        ln_tile(sb["x2t"][:], "x2w", 1e-5, sb["xln"][:], "xln", 2)
        tile_transposes(
            16, lambda d: sb["xln"][:, ts(d, 128)], ["xln"],
            lambda d, gt=gt: sb["xhT"][:, d * 1024 + gt * 128: d * 1024 + (gt + 1) * 128],
            lambda d, gt=gt: ("xhT", gt))
    x3T_toks = [("xhT", t) for t in range(8)]
    for oc in range(4):
        wbuf = ("wstr0", "wstr1")[oc % 2]
        p.dma("gpsimd", sb[wbuf].ap().rearrange("p (kt o) -> p kt o", kt=16),
              dr["w_inT"][:, ts(oc, 512)].rearrange("(kt p) o -> p kt o", p=128),
              writes=[(wbuf,)])
        for tt in range(8):
            for ki in range(16):
                p.add("tensor", lambda e, ki=ki, wbuf=wbuf, tt=tt: e.matmul(
                    ps["p0"][:, 0:512],
                    sb["xhT"][:, ki * 1024 + tt * 128: ki * 1024 + (tt + 1) * 128],
                    sb[wbuf][:, ts(ki, 512)], start=(ki == 0), stop=False),
                    reads=[(wbuf,)] + x3T_toks, writes=["ps_p0"])
            p.add("tensor", lambda e, oc=oc: e.matmul(
                ps["p0"][:, 0:512], sb["onesb"][0:1, :],
                sb["bin"][0:1, ts(oc, 512)], start=False, stop=True),
                reads=["bin", "onesb"], writes=["ps_p0"])
            p.add("scalar", lambda e: e.activation(
                out=sb["sgt"][:], in_=ps["p0"][:, 0:512], func=AF.Gelu),
                reads=["ps_p0"], writes=["sgt"])
            p.dma("gpsimd", dr["h"][tt * 128:(tt + 1) * 128, ts(oc, 512)],
                  sb["sgt"][:], reads=["sgt"], writes=[("hout", tt, oc)])
    p.add("gpsimd", lambda e: e.nop(),
          reads=[("x3out", g) for g in range(8)]
                + [("hout", t, o) for t in range(8) for o in range(4)])
    p.finalize()
    return nc, es


def run_launch_B(x_half, og_half, Wg, Wo, gnorm_w, w_fc, w_proj, w_in, b_in,
                 gfc, gproj):
    nc, es = build_B()
    eyebf = _bf16(np.eye(128, dtype=np.float32))
    wfcT = np.ascontiguousarray(w_fc.T)
    wprojT = np.ascontiguousarray(w_proj.T)
    w_inT = np.ascontiguousarray(w_in.T)
    gnormc = np.ascontiguousarray(gnorm_w.reshape(128, 1))
    binrow = np.ascontiguousarray(b_in.reshape(1, D))
    in_maps = []
    for core in range(8):
        in_maps.append(dict(
            x=np.ascontiguousarray(x_half[core]),
            og=np.ascontiguousarray(og_half[core]),
            wg=Wg, wo=Wo, gnormc=gnormc, wfcT=wfcT, wprojT=wprojT,
            w_inT=w_inT, binrow=binrow,
            gfc=gfc, gproj=gproj, eyebf=eyebf))
    res = run_bass_kernel_spmd(nc, in_maps, core_ids=list(range(8)))
    es.__exit__(None, None, None)
    x3 = np.stack([res.results[c]["x3"] for c in range(8)])
    h = np.stack([res.results[c]["h"] for c in range(8)])
    return x3, h



DI, DS = 4096, 16
NCT = DI // 128
CTG = 4


def build_D(stop_after=None):
    nc = bass.Bass("TRN2", target_bir_lowering=False, debug=False)
    es = contextlib.ExitStack()
    es.__enter__()
    dr = {}
    for nm, shp, dt, kind in [
        ("hT", [D, L], F32, "ExternalInput"),
        ("in_wT", [D, 2 * DI], F32, "ExternalInput"),
        ("convw", [128, NCT * 4], F32, "ExternalInput"),
        ("convb", [128, NCT], F32, "ExternalInput"),
        ("xprojT", [DI, 160], F32, "ExternalInput"),
        ("dtwT", [128, DI], F32, "ExternalInput"),
        ("dtb", [128, NCT], F32, "ExternalInput"),
        ("alog", [128, NCT * DS], F32, "ExternalInput"),
        ("dparm", [128, NCT], F32, "ExternalInput"),
        ("outwT", [DI, D], F32, "ExternalInput"),
        ("w1T", [D, 4 * D], F32, "ExternalInput"),
        ("w2T", [4 * D, D], F32, "ExternalInput"),
        ("pidx", [32, 1], F32, "ExternalInput"),
        ("mo", [L, D], F32, "ExternalOutput"),
        ("qw1", [D, 4 * D], BF16, "ExternalOutput"),
        ("qw2", [4 * D, D], BF16, "ExternalOutput"),
        ("sc1", [128, 1], F32, "ExternalOutput"),
        ("sc2", [128, 1], F32, "ExternalOutput"),
        ("xis", [DI, L], F32, "Internal"),
        ("zss", [DI, L], F32, "Internal"),
        ("ygs", [DI, L], BF16, "Internal"),
        ("obw", [DI, D], BF16, "Internal"),
    ]:
        dr[nm] = nc.dram_tensor(nm, shp, dt, kind=kind).ap()

    sb = {}

    def A(nm, shp, dt=F32):
        sb[nm] = es.enter_context(nc.sbuf_tensor(nm, shp, dt))
        return sb[nm]

    A("hTb", [128, 16 * L], BF16)
    A("arena0", [128, 4 * L])
    A("y4", [128, CTG * L])
    A("dx4", [128, CTG * L])
    A("hsc", [128, L])
    A("crep", [128, L])
    A("xibf", [128, L], BF16)
    A("wct0", [128, 16 * 128], BF16); A("wct1", [128, 16 * 128], BF16)
    A("xpj0", [128, 160], BF16); A("xpj1", [128, 160], BF16)
    A("bc", [32, L])
    A("dtrb", [128, L], BF16)
    A("acol", [128, NCT * DS])
    A("lsa", [128, 512])
    A("onescol", [128, 1])
    A("half", [128, 1]); A("nhalf", [128, 1])
    A("qacc", [128, 4])
    A("qcell", [1, 4])
    A("qcol", [128, 4])
    A("selm", [32, 260])
    for nm in ("convw", "convb", "dtb", "dparm"):
        A(nm + "s", list(dr[nm].shape), F32)
    ps = {}
    for nm in ("x0", "x1", "d0", "d1", "d2", "d3", "c"):
        ps[nm] = es.enter_context(nc.psum_tensor("psD_" + nm, [128, 512], F32))
    ps["bcp"] = es.enter_context(nc.psum_tensor("psD_bcp", [32, 512], F32))

    crb = sb["crep"].ap().bitcast(BF16)
    sb["wt0"] = sb["hsc"][:, 0:512]; sb["wt1"] = sb["hsc"][:, 512:1024]
    sb["qdmy"] = sb["hsc"][:, 1024:1536]
    sb["sg1"] = crb[:, 0:512]; sb["sg2"] = crb[:, 512:1024]
    sb["tau0"] = crb[:, 1024:1536]; sb["tau1"] = crb[:, 1536:2048]
    ar0 = sb["arena0"]
    xz_sl = ar0[:, 0:L]
    cv_sl = ar0[:, L:2 * L]
    da_sl = [ar0[:, 0:L], ar0[:, L:2 * L]]
    dbx_sl = ar0[:, 2 * L:3 * L]
    zst_sl = ar0[:, 2 * L:3 * L]
    htm_sl = ar0[:, 3 * L:4 * L]
    sb["lsc"] = ar0[:, 3 * L:3 * L + 512]
    sb["lsb"] = ar0[:, 3 * L + 512:3 * L + 1024]
    hTbv = sb["hTb"].ap().bitcast(F32)
    xi4_sl = lambda ci: hTbv[:, ci * L:(ci + 1) * L]
    dt4_sl = lambda ci: hTbv[:, (4 + ci) * L:(5 + ci) * L]
    ygres = sb["hTb"]
    obstr = sb["dx4"].ap().bitcast(BF16)

    p = Prog(nc, es)
    p.add("vector", lambda e: e.memset(sb["onescol"][:], 1.0), writes=["onescol"])
    p.add("vector", lambda e: e.memset(sb["half"][:], 0.5), writes=["half"])
    p.add("vector", lambda e: e.memset(sb["nhalf"][:], -0.5), writes=["nhalf"])
    p.add("vector", lambda e: e.memset(sb["qacc"][:], 0.0), writes=["qa0", "qa1", "qac3"])
    for nm in ("convw", "convb", "dtb", "dparm"):
        p.dma("sync", sb[nm + "s"][:], dr[nm], writes=[nm])
    p.dma("sync", sb["selm"][:, 256:257], dr["pidx"], writes=["pidx"])
    p.dma("sync", sb["hsc"][:, 0:NCT * DS], dr["alog"], writes=["alog", "hsc"])
    p.add("scalar", lambda e: e.activation(
        out=sb["acol"][:], in_=sb["hsc"][:, 0:NCT * DS], func=AF.Exp),
        reads=["alog", "hsc"], writes=["acol"])
    p.add("vector", lambda e: e.tensor_scalar_mul(sb["acol"][:], sb["acol"][:], -1.0),
          reads=["acol"], writes=["acol"])

    def qstats(wnm, ai):
        nrow, ncol = dr[wnm].shape
        for t_ in range((nrow // 128) * (ncol // 512)):
            r, c_ = t_ // (ncol // 512), t_ % (ncol // 512)
            buf = ("wt0", "wt1")[t_ % 2]
            p.dma("scalar", sb[buf],
                  dr[wnm][r * 128:(r + 1) * 128, c_ * 512:(c_ + 1) * 512],
                  writes=[(buf,), "hsc"])
            p.add("scalar", lambda e, buf=buf: e.activation(
                out=sb["qdmy"], in_=sb[buf], func=AF.Square,
                accum_out=sb["qacc"][:, 3:4]),
                reads=[(buf,)], writes=["hsc", "qac3"])
            p.add("vector", lambda e, ai=ai: e.tensor_add(
                sb["qacc"][:, ai:ai + 1], sb["qacc"][:, ai:ai + 1],
                sb["qacc"][:, 3:4]),
                reads=["qac3", "qa%d" % ai], writes=["qa%d" % ai])
    qstats("w1T", 0)
    qstats("w2T", 1)
    NW = float(D * 4 * D)
    for gi in range(2):
        p.add("tensor", lambda e, gi=gi: e.matmul(
            ps["c"][0:1, 0:1], sb["qacc"][:, gi:gi + 1], sb["onescol"][:],
            start=True, stop=True),
            reads=["qa%d" % gi, "onescol"], writes=["ps_c"])
        p.add("scalar", lambda e, gi=gi: e.copy(
            sb["qcell"][0:1, gi * 2:gi * 2 + 1], ps["c"][0:1, 0:1]),
            reads=["ps_c"], writes=[("qc", gi, 0)])
        p.add("vector", lambda e, gi=gi: e.tensor_scalar(
            sb["qcell"][0:1, gi * 2:gi * 2 + 1],
            sb["qcell"][0:1, gi * 2:gi * 2 + 1], 1.0 / NW, 1e-5, ALU.mult, ALU.max),
            reads=[("qc", gi, 0)], writes=[("qc", gi, 0)])
        p.add("scalar", lambda e, gi=gi: e.activation(
            out=sb["qcell"][0:1, gi * 2 + 1:gi * 2 + 2],
            in_=sb["qcell"][0:1, gi * 2:gi * 2 + 1], func=AF.Sqrt),
            reads=[("qc", gi, 0)], writes=[("qc", gi, 1)])
        p.add("vector", lambda e, gi=gi: e.reciprocal(
            sb["qcell"][0:1, gi * 2:gi * 2 + 1],
            sb["qcell"][0:1, gi * 2 + 1:gi * 2 + 2]),
            reads=[("qc", gi, 1)], writes=[("qc", gi, 0)])
        p.add("vector", lambda e, gi=gi: e.tensor_scalar_mul(
            sb["qcell"][0:1, gi * 2 + 1:gi * 2 + 2],
            sb["qcell"][0:1, gi * 2 + 1:gi * 2 + 2], 0.5),
            reads=[("qc", gi, 1)], writes=[("qc", gi, 1)])
        p.add("tensor", lambda e, gi=gi: e.matmul(
            ps["c"][:, 0:2], sb["onescol"][0:1, 0:1].broadcast_to([1, 128]),
            sb["qcell"][0:1, gi * 2:gi * 2 + 2], start=True, stop=True),
            reads=[("qc", gi, 0), ("qc", gi, 1), "onescol"], writes=["ps_c"])
        p.add("scalar", lambda e, gi=gi: e.copy(
            sb["qcol"][:, gi * 2:gi * 2 + 2], ps["c"][:, 0:2]),
            reads=["ps_c"], writes=[("qr", gi)])
    p.dma("gpsimd", dr["sc1"], sb["qcol"][:, 1:2], reads=[("qr", 0)],
          writes=["sc1o"])
    p.dma("gpsimd", dr["sc2"], sb["qcol"][:, 3:4], reads=[("qr", 1)],
          writes=["sc2o"])

    def qquant(wnm, onm, gi):
        nrow, ncol = dr[wnm].shape
        for t_ in range((nrow // 128) * (ncol // 512)):
            r, c_ = t_ // (ncol // 512), t_ % (ncol // 512)
            buf, tau = (("wt0", "tau0"), ("wt1", "tau1"))[t_ % 2]
            p.dma("scalar", sb[buf],
                  dr[wnm][r * 128:(r + 1) * 128, c_ * 512:(c_ + 1) * 512],
                  writes=[(buf,), "hsc"])
            p.add("scalar", lambda e, buf=buf, gi=gi: e.activation(
                out=sb["sg1"], in_=sb[buf], func=AF.Sign,
                bias=sb["nhalf"][:, 0:1], scale=sb["qcol"][:, gi * 2:gi * 2 + 1]),
                reads=[(buf,), "nhalf", ("qr", gi)], writes=["sg1", "crep"])
            p.add("scalar", lambda e, buf=buf, gi=gi: e.activation(
                out=sb["sg2"], in_=sb[buf], func=AF.Sign,
                bias=sb["half"][:, 0:1], scale=sb["qcol"][:, gi * 2:gi * 2 + 1]),
                reads=[(buf,), "half", ("qr", gi)], writes=["sg2", "crep"])
            p.add("vector", lambda e, tau=tau: e.tensor_add(
                sb[tau], sb["sg1"], sb["sg2"]),
                reads=["sg1", "sg2"], writes=[(tau,), "crep"])
            p.dma("gpsimd",
                  dr[onm][r * 128:(r + 1) * 128, c_ * 512:(c_ + 1) * 512],
                  sb[tau], reads=[(tau,)], writes=[("qw", onm)])
    qquant("w1T", "qw1", 0)
    qquant("w2T", "qw2", 1)

    for t_ in range((DI // 128) * (D // 512)):
        r, c_ = t_ // (D // 512), t_ % (D // 512)
        tau = ("tau0", "tau1")[t_ % 2]
        p.dma("gpsimd", sb[tau],
              dr["outwT"][r * 128:(r + 1) * 128, c_ * 512:(c_ + 1) * 512],
              writes=[(tau,), "crep"])
        p.dma("gpsimd", dr["obw"][r * 128:(r + 1) * 128, c_ * 512:(c_ + 1) * 512],
              sb[tau], reads=[(tau,)], writes=[("obw",)])

    for q in range(4):
        p.dma("gpsimd",
              sb["hTb"].ap().rearrange("p (ft t) -> p ft t", ft=16)[:, q * 4:(q + 1) * 4, :],
              dr["hT"].rearrange("(ft p) t -> p ft t", p=128)[:, q * 4:(q + 1) * 4, :],
              writes=[("hTb", q)])
    hTb_toks = [("hTb", q) for q in range(4)]

    for ct in range(NCT):
        wbuf = "wct0"
        p.dma("gpsimd", sb[wbuf].ap().rearrange("p (kt o) -> p kt o", kt=16),
              dr["in_wT"][:, ct * 128:(ct + 1) * 128].rearrange(
                  "(kt p) o -> p kt o", p=128),
              writes=[(wbuf,)])
        for tc in range(4):
            xb = ("x0", "x1")[tc % 2]
            for ki in range(16):
                p.add("tensor", lambda e, ki=ki, tc=tc, xb=xb, wbuf=wbuf: e.matmul(
                    ps[xb][:, 0:512], sb[wbuf][:, ts(ki, 128)],
                    sb["hTb"][:, ki * L + tc * 512: ki * L + (tc + 1) * 512],
                    start=(ki == 0), stop=(ki == 15)),
                    reads=[(wbuf,)] + hTb_toks, writes=["ps_" + xb]),
            p.add("scalar", lambda e, tc=tc, xb=xb: e.copy(
                xz_sl[:, ts(tc, 512)], ps[xb][:, 0:512]),
                reads=["ps_" + xb], writes=[("xz", tc)])
        xztoks = [("xz", tc) for tc in range(4)]
        w3c = sb["convws"][:, ct * 4 + 3: ct * 4 + 4]
        p.add("vector", lambda e, w3c=w3c, ct=ct: e.tensor_scalar(
            cv_sl, xz_sl, w3c, sb["convbs"][:, ct:ct + 1], ALU.mult, ALU.add),
            reads=xztoks + ["convw", "convb"], writes=["cv"])
        for kk in (2, 1, 0):
            sh = 3 - kk
            wc = sb["convws"][:, ct * 4 + kk: ct * 4 + kk + 1]
            p.add("vector", lambda e, wc=wc, sh=sh: e.scalar_tensor_tensor(
                cv_sl[:, sh:L], xz_sl[:, 0:L - sh], wc, cv_sl[:, sh:L],
                ALU.mult, ALU.add),
                reads=xztoks + ["cv", "convw"], writes=["cv"])
        p.add("scalar", lambda e: e.activation(out=xz_sl, in_=cv_sl, func=AF.Silu),
              reads=["cv"], writes=[("xz", 0), ("xz", 1), ("xz", 2), ("xz", 3)])
        p.dma("gpsimd", dr["xis"][ct * 128:(ct + 1) * 128, :], xz_sl,
              reads=xztoks, writes=[("xis", ct)])
        p.add("vector", lambda e: e.tensor_copy(sb["xibf"][:], xz_sl),
              reads=xztoks, writes=["xibf"])
        xpj = ("xpj0", "xpj1")[ct % 2]
        p.dma("gpsimd", sb[xpj][:], dr["xprojT"][ct * 128:(ct + 1) * 128, :],
              writes=[(xpj,)])
        for tc in range(4):
            p.add("tensor", lambda e, tc=tc, xpj=xpj, ct=ct: e.matmul(
                ps["d%d" % tc][:, 0:512], sb[xpj][:, 0:128],
                sb["xibf"][:, ts(tc, 512)], start=(ct == 0), stop=(ct == NCT - 1)),
                reads=[(xpj,), "xibf"], writes=["ps_d%d" % tc])
            p.add("tensor", lambda e, tc=tc, xpj=xpj: e.matmul(
                ps["bcp"][0:32, 0:512], sb[xpj][:, 128:160],
                sb["xibf"][:, ts(tc, 512)], start=True, stop=True),
                reads=[(xpj,), "xibf"], writes=["ps_bcp"])
            if ct == 0:
                p.add("vector", lambda e, tc=tc: e.tensor_copy(
                    sb["bc"][:, ts(tc, 512)], ps["bcp"][0:32, 0:512]),
                    reads=["ps_bcp"], writes=[("bc", tc)])
            else:
                p.add("vector", lambda e, tc=tc: e.tensor_add(
                    sb["bc"][:, ts(tc, 512)], sb["bc"][:, ts(tc, 512)],
                    ps["bcp"][0:32, 0:512]),
                    reads=["ps_bcp", ("bc", tc)], writes=[("bc", tc)])
        p.dma("gpsimd", sb["wct1"].ap().rearrange("p (kt o) -> p kt o", kt=16),
              dr["in_wT"][:, DI + ct * 128: DI + (ct + 1) * 128].rearrange(
                  "(kt p) o -> p kt o", p=128),
              writes=[("wct1",)])
        for tc in range(4):
            xb = ("x0", "x1")[tc % 2]
            for ki in range(16):
                p.add("tensor", lambda e, ki=ki, tc=tc, xb=xb: e.matmul(
                    ps[xb][:, 0:512], sb["wct1"][:, ts(ki, 128)],
                    sb["hTb"][:, ki * L + tc * 512: ki * L + (tc + 1) * 512],
                    start=(ki == 0), stop=(ki == 15)),
                    reads=[("wct1",)] + hTb_toks, writes=["ps_" + xb]),
            p.add("scalar", lambda e, tc=tc, xb=xb: e.activation(
                out=cv_sl[:, ts(tc, 512)], in_=ps[xb][:, 0:512], func=AF.Silu),
                reads=["ps_" + xb], writes=["cv"])
        p.dma("gpsimd", dr["zss"][ct * 128:(ct + 1) * 128, :], cv_sl,
              reads=["cv"], writes=[("zss", ct)])
    for tc in range(4):
        p.add("scalar", lambda e, tc=tc: e.copy(
            sb["dtrb"][:, ts(tc, 512)], ps["d%d" % tc][:, 0:512]),
            reads=["ps_d%d" % tc], writes=["dtrb"])
    if stop_after == "passX":
        p.add("gpsimd", lambda e: e.nop(),
              reads=[("xis", ct) for ct in range(NCT)]
                    + [("zss", ct) for ct in range(NCT)] + ["dtrb"])
        p.finalize()
        return nc, es

    route_ctr = [0]

    def route():
        route_ctr[0] += 1
        return "vector" if (route_ctr[0] * 29) % 50 < 29 else "gpsimd"

    for g in range(NCT // CTG):
        cts = list(range(g * CTG, (g + 1) * CTG))
        for ci, ct in enumerate(cts):
            p.dma("sync", xi4_sl(ci), dr["xis"][ct * 128:(ct + 1) * 128, :],
                  reads=[("xis", ct)],
                  writes=[("xi4", ci)] + ([("hTb", q) for q in range(4)]
                                          if g == 0 else []))
            wq = ("wct0", "wct1")[ci % 2]
            p.dma("gpsimd", sb[wq][:, 0:128],
                  dr["dtwT"][:, ct * 128:(ct + 1) * 128], writes=[(wq,)])
            for tc in range(4):
                xb = ("x0", "x1")[tc % 2]
                p.add("tensor", lambda e, tc=tc, xb=xb, wq=wq: e.matmul(
                    ps[xb][:, 0:512], sb[wq][:, 0:128],
                    sb["dtrb"][:, ts(tc, 512)], start=True, stop=True),
                    reads=[(wq,), "dtrb"], writes=["ps_" + xb])
                dtbc = sb["dtbs"][:, ct:ct + 1]
                p.add("scalar", lambda e, xb=xb, dtbc=dtbc: e.activation(
                    out=sb["lsa"][:], in_=ps[xb][:, 0:512], func=AF.Abs, bias=dtbc),
                    reads=["ps_" + xb, "dtb"], writes=["lsa"])
                p.add("scalar", lambda e: e.activation(
                    out=sb["lsb"], in_=sb["lsa"][:], func=AF.Exp, scale=-1.0),
                    reads=["lsa"], writes=["htm"])
                p.add("scalar", lambda e: e.activation(
                    out=sb["lsc"], in_=sb["lsb"], func=AF.Ln, bias=1.0),
                    reads=["htm"], writes=["htm"])
                p.add("scalar", lambda e, xb=xb, dtbc=dtbc: e.activation(
                    out=sb["lsa"][:], in_=ps[xb][:, 0:512], func=AF.Relu, bias=dtbc),
                    reads=["ps_" + xb, "dtb"], writes=["lsa"])
                p.add("vector", lambda e, ci=ci, tc=tc: e.tensor_add(
                    dt4_sl(ci)[:, ts(tc, 512)], sb["lsa"][:], sb["lsc"]),
                    reads=["lsa", "htm"],
                    writes=[("dt4", ci)] + ([("hTb", q) for q in range(4)]
                                            if g == 0 else []))
            p.add("vector", lambda e, ci=ci: e.tensor_mul(
                sb["dx4"][:, ts(ci, L)], dt4_sl(ci), xi4_sl(ci)),
                reads=[("dt4", ci), ("xi4", ci)], writes=[("dx4", ci)])
        for s in range(DS):
            p.add("vector", lambda e, s=s: e.tensor_single_scalar(
                sb["selm"][:, 257:258], sb["selm"][:, 256:257], float(s),
                ALU.is_equal),
                reads=["pidx"], writes=["selc0"])
            p.add("vector", lambda e, s=s: e.tensor_single_scalar(
                sb["selm"][:, 258:259], sb["selm"][:, 256:257], float(16 + s),
                ALU.is_equal),
                reads=["pidx"], writes=["selc1"])
            p.add("vector", lambda e: e.tensor_copy(
                sb["selm"][:, 0:128], sb["selm"][:, 257:258].broadcast_to([32, 128])),
                reads=["selc0"], writes=["selm"])
            p.add("vector", lambda e: e.tensor_copy(
                sb["selm"][:, 128:256], sb["selm"][:, 258:259].broadcast_to([32, 128])),
                reads=["selc1"], writes=["selm"])
            for tc in range(4):
                p.add("tensor", lambda e, s=s, tc=tc: e.matmul(
                    ps["d%d" % tc][:, 0:512], sb["selm"][:, 0:128],
                    sb["bc"][0:32, ts(tc, 512)], start=True, stop=True),
                    reads=[("bc", tc), "selm"], writes=["ps_d%d" % tc]),
                p.add("tensor", lambda e, s=s, tc=tc: e.matmul(
                    ps["c"][:, 0:512], sb["selm"][:, 128:256],
                    sb["bc"][0:32, ts(tc, 512)], start=True, stop=True),
                    reads=[("bc", tc), "selm"], writes=["ps_c"])
                p.add("scalar", lambda e, tc=tc: e.copy(
                    sb["crep"][:, ts(tc, 512)], ps["c"][:, 0:512]),
                    reads=["ps_c", "crep"], writes=["crep"])
            for ci, ct in enumerate(cts):
                dab = da_sl[ci % 2]
                datok = "da%d" % (ci % 2)
                p.add("scalar", lambda e, ci=ci, ct=ct, s=s, dab=dab: e.activation(
                    out=dab, in_=dt4_sl(ci), func=AF.Exp,
                    scale=sb["acol"][:, ct * DS + s: ct * DS + s + 1]),
                    reads=[("dt4", ci), "acol"],
                    writes=[datok] + ([("xz", t) for t in range(4)]
                                      if (ci % 2) == 0 else ["cv"]))
                for tc in range(4):
                    p.add("vector", lambda e, ci=ci, tc=tc: e.tensor_mul(
                        dbx_sl[:, ts(tc, 512)],
                        sb["dx4"][:, ci * L + tc * 512: ci * L + (tc + 1) * 512],
                        ps["d%d" % tc][:, 0:512]),
                        reads=[("dx4", ci), "ps_d%d" % tc], writes=["dbx"])
                p.add("vector", lambda e, dab=dab: e.tensor_tensor_scan(
                    sb["hsc"][:], dab, dbx_sl, 0.0, ALU.mult, ALU.add),
                    reads=[datok, "dbx"], writes=["hsc"])
                if s == 0:
                    p.add(route(), lambda e, ci=ci: e.tensor_mul(
                        sb["y4"][:, ts(ci, L)], sb["hsc"][:], sb["crep"][:]),
                        reads=["hsc", "crep"], writes=[("y4", ci)])
                else:
                    p.add(route(), lambda e: e.tensor_mul(
                        htm_sl, sb["hsc"][:], sb["crep"][:]),
                        reads=["hsc", "crep"], writes=["htm"])
                    p.add(route(), lambda e, ci=ci: e.tensor_add(
                        sb["y4"][:, ts(ci, L)], sb["y4"][:, ts(ci, L)], htm_sl),
                        reads=["htm", ("y4", ci)], writes=[("y4", ci)])
        for ci, ct in enumerate(cts):
            p.add("vector", lambda e, ci=ci, ct=ct: e.scalar_tensor_tensor(
                sb["y4"][:, ts(ci, L)], xi4_sl(ci), sb["dparms"][:, ct:ct + 1],
                sb["y4"][:, ts(ci, L)], ALU.mult, ALU.add),
                reads=[("xi4", ci), "dparm", ("y4", ci)], writes=[("y4", ci)])
            p.dma("sync", zst_sl, dr["zss"][ct * 128:(ct + 1) * 128, :],
                  reads=[("zss", ct)], writes=["dbx"])
            p.add("vector", lambda e, ci=ci: e.tensor_mul(
                sb["xibf"][:], sb["y4"][:, ts(ci, L)], zst_sl),
                reads=[("y4", ci), "dbx", "xibf"], writes=["xibf"])
            p.dma("gpsimd", dr["ygs"][ct * 128:(ct + 1) * 128, :], sb["xibf"][:],
                  reads=["xibf"], writes=[("ygs", ct)])
        if stop_after == "g%d" % g:
            p.add("gpsimd", lambda e: e.nop(),
                  reads=[("ygs", ct) for ct in cts])
            p.finalize()
            return nc, es

    ygs_toks = [("ygs", ct) for ct in range(NCT)]
    for tsb in range(4):
        for ct in range(NCT):
            p.dma("sync", ygres[:, ts(ct, 512)],
                  dr["ygs"][ct * 128:(ct + 1) * 128, ts(tsb, 512)],
                  reads=ygs_toks,
                  writes=[("ygr", ct)] + ([("xi4", ci) for ci in range(4)] +
                                          [("dt4", ci) for ci in range(4)]
                                          if tsb == 0 else []))
        ygr_toks = [("ygr", ct) for ct in range(NCT)]
        for oc in range(4):
            p.dma("sync", obstr[:, 0:NCT * 512].rearrange("p (kt o) -> p kt o", kt=NCT),
                  dr["obw"][:, ts(oc, 512)].rearrange("(kt p) o -> p kt o", p=128),
                  reads=[("obw",)],
                  writes=["obstr"] + ([("dx4", ci) for ci in range(4)]
                                      if tsb == 0 and oc == 0 else []))
            for tt in range(4):
                xb = ("x0", "x1")[tt % 2]
                for ct in range(NCT):
                    p.add("tensor", lambda e, ct=ct, tt=tt, xb=xb: e.matmul(
                        ps[xb][:, 0:512],
                        ygres[:, ct * 512 + tt * 128: ct * 512 + (tt + 1) * 128],
                        obstr[:, ts(ct, 512)], start=(ct == 0), stop=(ct == NCT - 1)),
                        reads=["obstr"] + ygr_toks, writes=["ps_" + xb])
                p.add("scalar", lambda e, xb=xb: e.copy(
                    sb["crep"][:, 0:512], ps[xb][:, 0:512]),
                    reads=["ps_" + xb], writes=["most", "crep"])
                p.dma("gpsimd",
                      dr["mo"][tsb * 512 + tt * 128: tsb * 512 + (tt + 1) * 128,
                               ts(oc, 512)],
                      sb["crep"][:, 0:512], reads=["most"], writes=[("moo", tsb, tt, oc)])
    p.add("gpsimd", lambda e: e.nop(),
          reads=[("moo", a, b, c_) for a in range(4) for b in range(4)
                 for c_ in range(4)]
                + [("qw", "qw1"), ("qw", "qw2"), "sc1o", "sc2o"])
    p.finalize()
    return nc, es


def _mamba_layout(prm):
    r32 = lambda v: np.ascontiguousarray(v.reshape(32, 128).T)
    return dict(
        in_wT=np.ascontiguousarray(prm["in_w"].T),
        convw=np.ascontiguousarray(
            prm["conv_w"].reshape(32, 128, 4).transpose(1, 0, 2).reshape(128, 128)),
        convb=r32(prm["conv_b"]),
        xprojT=np.ascontiguousarray(prm["xproj_w"].T),
        dtwT=np.ascontiguousarray(prm["dt_w"].T),
        dtb=r32(prm["dt_b"]),
        alog=np.ascontiguousarray(
            prm["A_log"].reshape(32, 128, DS).transpose(1, 0, 2).reshape(128, 32 * DS)),
        dparm=r32(prm["Dp"]),
        outwT=np.ascontiguousarray(prm["out_w"].T),
    )


def run_launch_D(h_full, fm, bm, w1, w2):
    nc, es = build_D()
    w1T = np.ascontiguousarray(w1.T)
    w2T = np.ascontiguousarray(w2.T)
    lay = {0: _mamba_layout(fm), 1: _mamba_layout(bm)}
    in_maps = []
    for core in range(8):
        b, dd = core // 2, core % 2
        hh = h_full[b] if dd == 0 else h_full[b][::-1]
        m = dict(lay[dd])
        m["hT"] = np.ascontiguousarray(hh.T)
        m["w1T"] = w1T
        m["w2T"] = w2T
        m["pidx"] = np.arange(32, dtype=np.float32).reshape(32, 1)
        in_maps.append(m)
    res = run_bass_kernel_spmd(nc, in_maps, core_ids=list(range(8)))
    es.__exit__(None, None, None)
    fwd = np.stack([res.results[2 * b]["mo"] for b in range(B)])
    bwd = np.stack([res.results[2 * b + 1]["mo"][::-1] for b in range(B)])
    r0 = res.results[0]
    return fwd, bwd, r0["qw1"], r0["qw2"], r0["sc1"], r0["sc2"]

def build_E():
    nc = bass.Bass("TRN2", target_bir_lowering=False, debug=False)
    es = contextlib.ExitStack()
    es.__enter__()
    dr = {}
    for nm, shp, dt, kind in [
        ("x3", [1024, D], F32, "ExternalInput"),
        ("hloc", [1024, D], F32, "ExternalInput"),
        ("fwd", [1024, D], F32, "ExternalInput"),
        ("bwdf", [1024, D], F32, "ExternalInput"),
        ("lngc", [128, 16], F32, "ExternalInput"),
        ("lnbc", [128, 16], F32, "ExternalInput"),
        ("w_outT", [D, D], F32, "ExternalInput"),
        ("borow", [1, D], F32, "ExternalInput"),
        ("qw1", [D, 4 * D], BF16, "ExternalInput"),
        ("qw2", [4 * D, D], BF16, "ExternalInput"),
        ("sc1", [128, 1], F32, "ExternalInput"),
        ("sc2", [128, 1], F32, "ExternalInput"),
        ("eyebf", [128, 128], BF16, "ExternalInput"),
        ("out", [1024, D], F32, "ExternalOutput"),
        ("x4s", [1024, D], F32, "Internal"),
    ]:
        dr[nm] = nc.dram_tensor(nm, shp, dt, kind=kind).ap()

    sb = {}

    def A(nm, shp, dt=F32):
        sb[nm] = es.enter_context(nc.sbuf_tensor(nm, shp, dt))
        return sb[nm]

    A("eye", [128, 128], BF16)
    A("onesb", [1, 128], BF16)
    A("lng", [128, 16]); A("lnb", [128, 16], BF16)
    A("scc", [128, 8])
    A("bo", [1, D], BF16)
    A("bw", [1, D], BF16)
    A("xt0", [128, D]); A("xt1", [128, D])
    A("stat", [128, 24])
    A("xln", [128, D], BF16)
    A("rT", [128, 16 * 1024], BF16)
    A("wstr0", [128, 16 * 512], BF16); A("wstr1", [128, 16 * 512], BF16)
    A("at2", [128, 2 * D])
    A("dmy", [128, D])
    A("hst", [128, 4 * D])
    A("hT", [128, 64 * 128], BF16)
    A("q0", [128, 512]); A("q1", [128, 512], I32); A("q2", [128, 512])
    A("q3", [128, 512]); A("q4", [128, 512])
    A("xqb", [128, D], BF16)
    A("o2", [128, D])
    ps = {}
    for nm in ("p0", "p1", "r1", "c"):
        ps[nm] = es.enter_context(nc.psum_tensor("psE_" + nm, [128, 512], F32))
    ps["tr"] = es.enter_context(nc.psum_tensor("psE_tr", [128, 512], BF16))

    p = Prog(nc, es)
    p.dma("sync", sb["eye"][:], dr["eyebf"], writes=["eye"])
    p.dma("sync", sb["lng"][:], dr["lngc"], writes=["lng"])
    p.dma("gpsimd", sb["lnb"][:], dr["lnbc"], writes=["lnb"])
    p.dma("sync", sb["scc"][:, 0:1], dr["sc1"], writes=["sc1"])
    p.dma("sync", sb["scc"][:, 1:2], dr["sc2"], writes=["sc2"])
    p.dma("gpsimd", sb["bo"][:], dr["borow"], writes=["bo"])
    p.add("vector", lambda e: e.memset(sb["onesb"][:], 1.0), writes=["onesb"])

    def ln_tile(xt_, xtoks, eps, outsl, outtok, sidx, nchunk=1):
        if not isinstance(xtoks, list):
            xtoks = [xtoks]
        st = sb["stat"]
        c0, c1, c2, c3 = (st[:, 4 * sidx + j:4 * sidx + j + 1] for j in range(4))
        n = xt_.shape[1]
        nf = 1.0 / n
        p.add("vector", lambda e, x=xt_, o=c0: e.reduce_sum(o, x, AX.X),
              reads=xtoks, writes=[("s0", sidx)])
        cw = n // nchunk
        for hc in range(nchunk):
            acc = c1 if hc == 0 else sb["stat"][:, 20 + hc:21 + hc]
            p.add("scalar", lambda e, x=xt_, a=acc, hc=hc, cw=cw: e.activation(
                out=sb["dmy"][:, 0:cw], in_=x[:, hc * cw:(hc + 1) * cw],
                func=AF.Square, accum_out=a),
                reads=xtoks, writes=[("sq", sidx, hc), "lndmy"])
        for hc in range(1, nchunk):
            p.add("vector", lambda e, a=c1, hc=hc: e.tensor_add(
                a, a, sb["stat"][:, 20 + hc:21 + hc]),
                reads=[("sq", sidx, 0), ("sq", sidx, hc)], writes=[("sq", sidx, 0)])
        p.add("vector", lambda e, a=c2, b=c0, nf=nf: e.tensor_scalar_mul(a, b, nf),
              reads=[("s0", sidx)], writes=[("sm", sidx)])
        p.add("vector", lambda e, a=c1, nf=nf, eps=eps: e.tensor_scalar(
            a, a, nf, eps, ALU.mult, ALU.add),
            reads=[("sq", sidx, 0)], writes=[("s1", sidx)])
        p.add("vector", lambda e, a=c3, b=c2: e.scalar_tensor_tensor(
            a, b, 1.0, b, ALU.mult, ALU.mult),
            reads=[("sm", sidx)], writes=[("smm", sidx)])
        p.add("vector", lambda e, a=c1, b=c3: e.tensor_sub(a, a, b),
              reads=[("s1", sidx), ("smm", sidx)], writes=[("s1", sidx)])
        p.add("scalar", lambda e, a=c3, b=c1: e.activation(out=a, in_=b, func=AF.Sqrt),
              reads=[("s1", sidx)], writes=[("ssq", sidx)])
        p.add("vector", lambda e, a=c1, b=c3: e.reciprocal(a, b),
              reads=[("ssq", sidx)], writes=[("srs", sidx)])
        p.add("vector", lambda e, x=xt_, m=c2, r=c1, o=outsl: e.tensor_scalar(
            o, x, m, r, ALU.subtract, ALU.mult),
            reads=xtoks + [("sm", sidx), ("srs", sidx)], writes=[outtok])

    def transposes(n, src_fn, srctoks, dst_fn, dtok):
        for d in range(n):
            p.add("tensor", lambda e, d=d: e.transpose(
                ps["tr"][:, 0:128], src_fn(d), sb["eye"][:]),
                reads=srctoks + ["eye"], writes=["ps_tr"])
            p.add("scalar", lambda e, d=d: e.copy(dst_fn(d), ps["tr"][:, 0:128]),
                  reads=["ps_tr"], writes=[dtok])

    def quant_chunk(xsl, xtoks, scol, stok, osl, otok):
        p.add("vector", lambda e, x=xsl, s=scol: e.tensor_scalar_mul(
            sb["q0"][:], x, s), reads=xtoks + [stok], writes=["q0"])
        p.add("vector", lambda e: e.tensor_copy(sb["q1"][:], sb["q0"][:]),
              reads=["q0"], writes=["q1"])
        p.add("vector", lambda e: e.tensor_copy(sb["q2"][:], sb["q1"][:]),
              reads=["q1"], writes=["q2"])
        p.add("vector", lambda e: e.tensor_sub(sb["q3"][:], sb["q0"][:], sb["q2"][:]),
              reads=["q0", "q2"], writes=["q3"])
        p.add("vector", lambda e: e.tensor_single_scalar(
            sb["q4"][:], sb["q3"][:], 0.5, ALU.is_ge),
            reads=["q3"], writes=["q4"])
        p.add("vector", lambda e: e.scalar_tensor_tensor(
            sb["q3"][:], sb["q3"][:], -0.5, sb["q4"][:], ALU.is_le, ALU.subtract),
            reads=["q3", "q4"], writes=["q3"])
        p.add("vector", lambda e, o=osl: e.tensor_sub(o, sb["q2"][:], sb["q3"][:]),
              reads=["q2", "q3"], writes=[otok])

    for tt in range(8):
        xt_ = sb["xt0"][:]
        p.dma("sync", xt_, dr["fwd"][tt * 128:(tt + 1) * 128, :], writes=["xld"])
        p.dma("sync", sb["xt1"][:], dr["bwdf"][tt * 128:(tt + 1) * 128, :],
              writes=["xt1"])
        p.add("vector", lambda e: e.tensor_add(sb["xt0"][:], sb["xt0"][:], sb["xt1"][:]),
              reads=["xld", "xt1"], writes=["xld"])
        p.dma("sync", sb["xt1"][:], dr["hloc"][tt * 128:(tt + 1) * 128, :],
              writes=["xt1"])
        p.add("vector", lambda e: e.tensor_add(sb["xt0"][:], sb["xt0"][:], sb["xt1"][:]),
              reads=["xld", "xt1"], writes=["xld"])
        ln_tile(xt_, "xld", 1e-5, sb["xln"][:], "xln", tt % 4)
        transposes(16, lambda d: sb["xln"][:, ts(d, 128)], ["xln"],
                   lambda d, tt=tt: sb["rT"][:, d * 1024 + tt * 128:
                                            d * 1024 + (tt + 1) * 128],
                   ("rT", tt))
    rT_toks = [("rT", t) for t in range(8)]
    for sbi in range(4):
        for oc in range(4):
            wbuf = ("wstr0", "wstr1")[oc % 2]
            p.dma("gpsimd", sb[wbuf].ap().rearrange("p (kt o) -> p kt o", kt=16),
                  dr["w_outT"][:, ts(oc, 512)].rearrange("(kt p) o -> p kt o", p=128),
                  writes=[(wbuf,)])
            if sbi == 0:
                for ki in range(16):
                    p.add("tensor", lambda e, ki=ki, wbuf=wbuf: e.matmul(
                        ps["r1"][0:1, 0:512], sb["lnb"][:, ki:ki + 1],
                        sb[wbuf][:, ts(ki, 512)], start=(ki == 0), stop=(ki == 15)),
                        reads=[(wbuf,), "lnb"], writes=["ps_r1"])
                p.add("scalar", lambda e, oc=oc: e.copy(
                    sb["bw"][0:1, ts(oc, 512)], ps["r1"][0:1, 0:512]),
                    reads=["ps_r1"], writes=["bw"])
            for ki in range(16):
                p.add("vector", lambda e, ki=ki, wbuf=wbuf: e.tensor_scalar_mul(
                    sb[wbuf][:, ts(ki, 512)], sb[wbuf][:, ts(ki, 512)],
                    sb["lng"][:, ki:ki + 1]),
                    reads=[(wbuf,), "lng"] + (["bw"] if sbi == 0 else []),
                    writes=[(wbuf,)])
            for tt in range(2):
                gt = sbi * 2 + tt
                for ki in range(16):
                    p.add("tensor", lambda e, ki=ki, wbuf=wbuf, gt=gt: e.matmul(
                        ps["p0"][:, 0:512],
                        sb["rT"][:, ki * 1024 + gt * 128: ki * 1024 + (gt + 1) * 128],
                        sb[wbuf][:, ts(ki, 512)], start=(ki == 0), stop=False),
                        reads=[(wbuf,)] + rT_toks, writes=["ps_p0"])
                p.add("tensor", lambda e, oc=oc: e.matmul(
                    ps["p0"][:, 0:512], sb["onesb"][0:1, :],
                    sb["bo"][0:1, ts(oc, 512)], start=False, stop=False),
                    reads=["bo", "onesb"], writes=["ps_p0"])
                p.add("tensor", lambda e, oc=oc: e.matmul(
                    ps["p0"][:, 0:512], sb["onesb"][0:1, :],
                    sb["bw"][0:1, ts(oc, 512)], start=False, stop=True),
                    reads=["bw", "onesb"], writes=["ps_p0"])
                p.add("scalar", lambda e, tt=tt, oc=oc: e.copy(
                    sb["at2"][:, tt * D + oc * 512: tt * D + (oc + 1) * 512],
                    ps["p0"][:, 0:512]),
                    reads=["ps_p0"], writes=[("a2", tt, oc)])
        for tt in range(2):
            gt = sbi * 2 + tt
            xt_ = sb["xt0"][:]
            p.dma("sync", xt_, dr["x3"][gt * 128:(gt + 1) * 128, :], writes=["xld"])
            a2sl = sb["at2"][:, ts(tt, D)]
            a2toks = [("a2", tt, o) for o in range(4)]
            ln_tile(a2sl, a2toks, 1e-5, sb["xt1"][:], "xt1", tt)
            p.add("vector", lambda e, x=xt_: e.tensor_add(
                sb["xt1"][:], sb["xt1"][:], x),
                reads=["xt1", "xld"], writes=["xt1"])
            p.dma("gpsimd", dr["x4s"][gt * 128:(gt + 1) * 128, :], sb["xt1"][:],
                  reads=["xt1"], writes=[("x4s", gt)])
    for gt in range(8):
        xt_ = sb["xt0"][:]
        p.dma("sync", xt_, dr["x4s"][gt * 128:(gt + 1) * 128, :],
              reads=[("x4s", gt)], writes=["xld"])
        ln_tile(xt_, "xld", 1e-5, sb["at2"][:, 0:D], "xf", 0)
        p.add("vector", lambda e: e.reduce_max(
            sb["scc"][:, 2:3], sb["at2"][:, 0:D], AX.X, apply_absolute_value=True),
            reads=["xf"], writes=["am"])
        p.add("vector", lambda e: e.tensor_single_scalar(
            sb["scc"][:, 2:3], sb["scc"][:, 2:3], 1e-5, ALU.max),
            reads=["am"], writes=["am"])
        p.add("vector", lambda e: e.reciprocal(sb["scc"][:, 3:4], sb["scc"][:, 2:3]),
              reads=["am"], writes=["sm1"])
        p.add("vector", lambda e: e.tensor_scalar_mul(
            sb["scc"][:, 3:4], sb["scc"][:, 3:4], 127.0),
            reads=["sm1"], writes=["sm1"])
        p.add("vector", lambda e: e.scalar_tensor_tensor(
            sb["scc"][:, 4:5], sb["scc"][:, 2:3], 1.0 / 127, sb["scc"][:, 0:1],
            ALU.mult, ALU.mult),
            reads=["am", "sc1"], writes=["es1"])
        for cc in range(4):
            quant_chunk(sb["at2"][:, ts(cc, 512)], ["xf"], sb["scc"][:, 3:4], "sm1",
                        sb["xqb"][:, ts(cc, 512)], "xqb")
        transposes(16, lambda d: sb["xqb"][:, ts(d, 128)], ["xqb"],
                   lambda d: sb["hT"][:, ts(d, 128)], "xqT")
        for oc in range(16):
            wbuf = ("wstr0", "wstr1")[oc % 2]
            p.dma("sync", sb[wbuf].ap().rearrange("p (kt o) -> p kt o", kt=16),
                  dr["qw1"][:, ts(oc, 512)].rearrange("(kt p) o -> p kt o", p=128),
                  writes=[(wbuf,)])
            for ki in range(16):
                p.add("tensor", lambda e, ki=ki, wbuf=wbuf: e.matmul(
                    ps["p0"][:, 0:512], sb["hT"][:, ts(ki, 128)],
                    sb[wbuf][:, ts(ki, 512)], start=(ki == 0), stop=(ki == 15)),
                    reads=[(wbuf,), "xqT"], writes=["ps_p0"])
            p.add("scalar", lambda e, oc=oc: e.activation(
                out=sb["hst"][:, ts(oc, 512)], in_=ps["p0"][:, 0:512], func=AF.Gelu,
                scale=sb["scc"][:, 4:5]),
                reads=["ps_p0", "es1"], writes=[("hs", oc)])
        hs_toks = [("hs", o) for o in range(16)]
        p.add("vector", lambda e: e.reduce_max(
            sb["scc"][:, 2:3], sb["hst"][:], AX.X, apply_absolute_value=True),
            reads=hs_toks, writes=["am"])
        p.add("vector", lambda e: e.tensor_single_scalar(
            sb["scc"][:, 2:3], sb["scc"][:, 2:3], 1e-5, ALU.max),
            reads=["am"], writes=["am"])
        p.add("vector", lambda e: e.reciprocal(sb["scc"][:, 3:4], sb["scc"][:, 2:3]),
              reads=["am"], writes=["sm1"])
        p.add("vector", lambda e: e.tensor_scalar_mul(
            sb["scc"][:, 3:4], sb["scc"][:, 3:4], 127.0),
            reads=["sm1"], writes=["sm1"])
        p.add("vector", lambda e: e.scalar_tensor_tensor(
            sb["scc"][:, 4:5], sb["scc"][:, 2:3], 1.0 / 127, sb["scc"][:, 1:2],
            ALU.mult, ALU.mult),
            reads=["am", "sc2"], writes=["es1"])
        for cc in range(16):
            quant_chunk(sb["hst"][:, ts(cc, 512)], hs_toks, sb["scc"][:, 3:4], "sm1",
                        sb["xqb"][:, 0:512], "xqb")
            transposes(4, lambda d, cc=cc: sb["xqb"][:, ts(d, 128)][:, 0:128],
                       ["xqb"],
                       lambda d, cc=cc: sb["hT"][:, (cc * 4 + d) * 128:
                                                 (cc * 4 + d + 1) * 128],
                       ("hT2", cc))
        hT2_toks = [("hT2", c) for c in range(16)]
        for oc in range(4):
            for kg in range(4):
                wbuf = ("wstr0", "wstr1")[kg % 2]
                p.dma("sync", sb[wbuf].ap().rearrange("p (kt o) -> p kt o", kt=16),
                      dr["qw2"][kg * 2048:(kg + 1) * 2048, ts(oc, 512)].rearrange(
                          "(kt p) o -> p kt o", p=128),
                      writes=[(wbuf,)])
                for k2 in range(16):
                    ki = kg * 16 + k2
                    p.add("tensor", lambda e, ki=ki, k2=k2, wbuf=wbuf: e.matmul(
                        ps["p1"][:, 0:512], sb["hT"][:, ts(ki, 128)],
                        sb[wbuf][:, ts(k2, 512)],
                        start=(ki == 0), stop=(ki == 63)),
                        reads=[(wbuf,)] + hT2_toks, writes=["ps_p1"])
            p.add("scalar", lambda e, oc=oc: e.activation(
                out=sb["o2"][:, ts(oc, 512)], in_=ps["p1"][:, 0:512],
                func=AF.Identity, scale=sb["scc"][:, 4:5]),
                reads=["ps_p1", "es1"], writes=[("o2w", oc)])
        o2_toks = [("o2w", o) for o in range(4)]
        ln_tile(sb["o2"][:], o2_toks, 1e-5, sb["xt1"][:], "xt1", 1)
        p.add("vector", lambda e: e.tensor_add(sb["xt1"][:], sb["xt1"][:], sb["xt0"][:]),
              reads=["xt1", "xld"], writes=["xt1"])
        p.dma("gpsimd", dr["out"][gt * 128:(gt + 1) * 128, :], sb["xt1"][:],
              reads=["xt1"], writes=[("oo", gt)])
    p.add("gpsimd", lambda e: e.nop(), reads=[("oo", g) for g in range(8)])
    p.finalize()
    return nc, es


def run_launch_E(x3, hloc, fwd, bwd, ln_g, ln_b, w_out, b_out, qw1, qw2,
                 sc1, sc2):
    nc, es = build_E()
    eyebf = _bf16(np.eye(128, dtype=np.float32))
    r16 = lambda v: np.ascontiguousarray(v.reshape(16, 128).T)
    w_outT = np.ascontiguousarray(w_out.T)
    borow = b_out.reshape(1, D).astype(np.float32)
    in_maps = []
    for core in range(8):
        b, hh = core // 2, core % 2
        sl = slice(hh * 1024, (hh + 1) * 1024)
        in_maps.append(dict(
            x3=np.ascontiguousarray(x3[core]), hloc=np.ascontiguousarray(hloc[core]),
            fwd=np.ascontiguousarray(fwd[b, sl]),
            bwdf=np.ascontiguousarray(bwd[b, sl]),
            lngc=r16(ln_g), lnbc=r16(ln_b), w_outT=w_outT, borow=borow,
            qw1=qw1, qw2=qw2, sc1=sc1, sc2=sc2, eyebf=eyebf))
    res = run_bass_kernel_spmd(nc, in_maps, core_ids=list(range(8)))
    es.__exit__(None, None, None)
    return np.stack([res.results[c]["out"] for c in range(8)])




def _bf16(x):
    import ml_dtypes
    return np.asarray(x, dtype=ml_dtypes.bfloat16)


def run_launch_A(x, Wq, Wk, Wv, Wgk1, Wgk2, bgk2, wfcT=None, wprojT=None):
    nc, es = build_A()
    eyebf = _bf16(np.eye(128, dtype=np.float32))
    mask64 = np.zeros((128, C), np.float32)
    rr = np.arange(128) % 64
    mask64[rr[:, None] <= np.arange(C)[None, :]] = 1.0
    bgk2c = np.ascontiguousarray(bgk2.reshape(16, 128).T)
    zeros = np.zeros((1024, D), np.float32)
    in_maps = []
    for core in range(NC_ := 8):
        b, hh = core // 2, core % 2
        xp = zeros if hh == 0 else np.ascontiguousarray(x[b, 0:1024])
        xo = np.ascontiguousarray(x[b, hh * 1024:(hh + 1) * 1024])
        in_maps.append(dict(xp=xp, xo=xo, wq=Wq, wk=Wk, wv=Wv, wgk1=Wgk1,
                            wgk2=Wgk2, bgk2c=bgk2c, eyebf=eyebf, mask64=mask64,
                            wfcT=wfcT, wprojT=wprojT))
    res = run_bass_kernel_spmd(nc, in_maps, core_ids=list(range(8)))
    es.__exit__(None, None, None)
    og = np.zeros((B, L, D), np.float32)
    for core in range(8):
        b, hh = core // 2, core % 2
        og[b, hh * 1024:(hh + 1) * 1024] = res.results[core]["og"]
    gfc = res.results[0]["gabs_fc"]
    gproj = res.results[0]["gabs_proj"]
    return og, gfc, gproj


def kernel(**inputs):
    x = np.ascontiguousarray(inputs["L0_attn_x"], dtype=np.float32)
    wfcT = np.ascontiguousarray(inputs["L0_ffn_w_fc"].T)
    wprojT = np.ascontiguousarray(inputs["L0_ffn_w_proj"].T)
    og, gfc, gproj = run_launch_A(
        x, inputs["L0_attn_Wq"], inputs["L0_attn_Wk"], inputs["L0_attn_Wv"],
        inputs["L0_attn_Wgk1"], inputs["L0_attn_Wgk2"], inputs["L0_attn_bgk2"],
        wfcT, wprojT)
    x_half = np.stack([x[c // 2, (c % 2) * 1024:(c % 2 + 1) * 1024]
                       for c in range(8)])
    og_half = np.stack([og[c // 2, (c % 2) * 1024:(c % 2 + 1) * 1024]
                        for c in range(8)])
    x3, h = run_launch_B(
        x_half, og_half, inputs["L0_attn_Wg"], inputs["L0_attn_Wo"],
        inputs["L0_attn_gnorm_w"], inputs["L0_ffn_w_fc"],
        inputs["L0_ffn_w_proj"], inputs["L1_attn_w_in"], inputs["L1_attn_b_in"],
        gfc, gproj)
    h_full = np.stack([np.concatenate([h[2 * b], h[2 * b + 1]], axis=0)
                       for b in range(B)])
    pdict = lambda pre: dict(
        in_w=inputs[f"L1_attn_{pre}_in_w"], conv_w=inputs[f"L1_attn_{pre}_conv_w"],
        conv_b=inputs[f"L1_attn_{pre}_conv_b"],
        xproj_w=inputs[f"L1_attn_{pre}_xproj_w"],
        dt_w=inputs[f"L1_attn_{pre}_dt_w"], dt_b=inputs[f"L1_attn_{pre}_dt_b"],
        A_log=inputs[f"L1_attn_{pre}_A_log"], Dp=inputs[f"L1_attn_{pre}_D"],
        out_w=inputs[f"L1_attn_{pre}_out_w"])
    fwd, bwd, qw1, qw2, sc1, sc2 = run_launch_D(
        h_full, pdict("fm"), pdict("bm"), inputs["L1_ffn_w1"], inputs["L1_ffn_w2"])
    h_half = np.stack([h_full[c // 2, (c % 2) * 1024:(c % 2 + 1) * 1024]
                       for c in range(8)])
    out = run_launch_E(
        x3, h_half, fwd, bwd, inputs["L1_attn_ln_g"], inputs["L1_attn_ln_b"],
        inputs["L1_attn_w_out"], inputs["L1_attn_b_out"], qw1, qw2, sc1, sc2)
    full = np.zeros((B, L, D), np.float32)
    for c in range(8):
        full[c // 2, (c % 2) * 1024:(c % 2 + 1) * 1024] = out[c]
    return full


# version 02
# speedup vs baseline: 1.0080x; 1.0080x over previous
import contextlib
import numpy as np

import concourse.bass as bass
import concourse.mybir as mybir
from concourse.bass_utils import run_bass_kernel_spmd

F32 = mybir.dt.float32
BF16 = mybir.dt.bfloat16
I32 = mybir.dt.int32
AF = mybir.ActivationFunctionType
ALU = mybir.AluOpType
AX = mybir.AxisListType

B, L, D = 4, 2048, 2048
H, HD, C = 16, 128, 64
GLA_SCALE = float(HD) ** -0.5



class Prog:

    ENGINES = ("sync", "scalar", "vector", "tensor", "gpsimd")

    def __init__(self, nc, es):
        self.nc = nc
        self.es = es
        self.ops = []
        self.last_w = {}
        self.readers = {}

    def add(self, eng, fn, reads=(), writes=(), dma=False):
        i = len(self.ops)
        deps = set()
        for t in reads:
            j = self.last_w.get(t)
            if j is not None:
                deps.add(j)
        for t in writes:
            j = self.last_w.get(t)
            if j is not None:
                deps.add(j)
            for r in self.readers.get(t, ()):
                deps.add(r)
        deps.discard(i)
        self.ops.append([eng, fn, deps, dma, 0])
        for t in reads:
            self.readers.setdefault(t, []).append(i)
        for t in writes:
            self.last_w[t] = i
            self.readers[t] = []
        return i

    def dma(self, eng, out, in_, reads=(), writes=(), transpose=False):
        if transpose:
            fn = lambda e, o=out, s=in_: e.dma_start_transpose(o, s)
        else:
            fn = lambda e, o=out, s=in_: e.dma_start(out=o, in_=s)
        return self.add(eng, fn, reads=reads, writes=writes, dma=True)

    def finalize(self):
        nc, es = self.nc, self.es
        n = len(self.ops)
        needed = set()
        for eng, fn, deps, dma, _ in self.ops:
            for d in deps:
                if self.ops[d][0] == "tensor" and eng == "tensor":
                    continue
                needed.add(d)
        R = 8
        csem = {e: es.enter_context(nc.semaphore(f"c_{e}")) for e in self.ENGINES}
        dsem = {e: [es.enter_context(nc.semaphore(f"d_{e}{j}")) for j in range(R)]
                for e in self.ENGINES}
        ctot = {e: 0 for e in self.ENGINES}
        dcnt = {e: 0 for e in self.ENGINES}
        target = [None] * n
        prewait = [None] * n
        for i, op in enumerate(self.ops):
            eng, fn, deps, dma, _ = op
            if dma:
                slot = dcnt[eng] % R
                uses = dcnt[eng] // R
                dcnt[eng] += 1
                target[i] = ("d", (eng, slot), 16 * (uses + 1))
                op[4] = 16
                if uses > 0:
                    prewait[i] = ((eng, slot), 16 * uses)
            elif i in needed:
                ctot[eng] += 1
                target[i] = ("c", eng, ctot[eng])
                op[4] = 1
        waits = [[] for _ in range(n)]
        known = {e: {} for e in self.ENGINES}
        for i, (eng, fn, deps, dma, _) in enumerate(self.ops):
            req = {}
            if prewait[i] is not None:
                key, cnt = ("d", prewait[i][0]), prewait[i][1]
                req[key] = cnt
            for d in deps:
                if self.ops[d][0] == "tensor" and eng == "tensor":
                    continue
                tg = target[d]
                assert tg is not None
                key = (tg[0], tg[1])
                req[key] = max(req.get(key, 0), tg[2])
            k = known[eng]
            for key in sorted(req, key=repr):
                cnt = req[key]
                if k.get(key, -1) >= cnt:
                    continue
                k[key] = cnt
                waits[i].append((key, cnt))
        blk = es.enter_context(nc.Block())
        streams = {e: [] for e in self.ENGINES}
        for i, (eng, fn, deps, dma, inc) in enumerate(self.ops):
            streams[eng].append((fn, waits[i], inc, target[i]))

        def sem_of(kind, ident):
            if kind == "c":
                return csem[ident]
            return dsem[ident[0]][ident[1]]

        def mk_runner(items):
            def run(e):
                for fn, ws, inc, tg in items:
                    for (kind, ident), cnt in ws:
                        e.wait_ge(sem_of(kind, ident), cnt)
                    instr = fn(e)
                    if inc:
                        instr.then_inc(sem_of(tg[0], tg[1]), inc)
            return run

        for e in self.ENGINES:
            if streams[e]:
                getattr(blk, e)(mk_runner(streams[e]))


def ts(i, sz):
    return slice(i * sz, (i + 1) * sz)



SB = 512
NSB = L // 2 // SB


def build_A():
    nc = bass.Bass("TRN2", target_bir_lowering=False, debug=False)
    es = contextlib.ExitStack()
    es.__enter__()
    dr = {}
    for nm, shp, dt, kind in [
        ("xp", [1024, D], F32, "ExternalInput"),
        ("xo", [1024, D], F32, "ExternalInput"),
        ("wq", [D, D], F32, "ExternalInput"),
        ("wk", [D, D], F32, "ExternalInput"),
        ("wv", [D, D], F32, "ExternalInput"),
        ("wgk1", [D, 16], F32, "ExternalInput"),
        ("wgk2", [16, D], F32, "ExternalInput"),
        ("bgk2c", [128, 16], F32, "ExternalInput"),
        ("eyebf", [128, 128], BF16, "ExternalInput"),
        ("mask64", [128, C], F32, "ExternalInput"),
        ("wfcT", [D, 4 * D], F32, "ExternalInput"),
        ("wprojT", [4 * D, D], F32, "ExternalInput"),
        ("og", [1024, D], F32, "ExternalOutput"),
        ("gabs_fc", [128, 1], F32, "ExternalOutput"),
        ("gabs_proj", [128, 1], F32, "ExternalOutput"),
    ]:
        dr[nm] = nc.dram_tensor(nm, shp, dt, kind=kind).ap()

    sb = {}

    def A(nm, shp, dt=F32):
        sb[nm] = es.enter_context(nc.sbuf_tensor(nm, shp, dt))
        return sb[nm]

    A("xin", [128, 2 * D])
    A("xln", [128, 2 * D], BF16)
    A("stat", [128, 16])
    A("xhT", [128, 16 * SB], BF16)
    A("eye", [128, 128], BF16)
    A("mask", [128, C])
    A("wqs0", [128, 16 * 128], BF16); A("wqs1", [128, 16 * 128], BF16)
    A("wks0", [128, 16 * 128], BF16); A("wks1", [128, 16 * 128], BF16)
    A("wvs0", [128, 16 * 128], BF16); A("wvs1", [128, 16 * 128], BF16)
    A("wgk1s", [128, 16 * 16], BF16)
    A("wgk2s", [16, D], BF16)
    A("bgk2", [128, 16])
    A("nbgk2", [128, 16])
    A("qb", [128, 16 * SB], BF16)
    A("kb", [128, 16 * SB], BF16)
    A("vb", [128, 4 * D], BF16)
    A("gk1b", [16, SB], BF16)
    A("lsa", [128, SB]); A("lsb", [128, SB]); A("lsc", [128, SB])
    A("ell", [128, 8 * 16 * C])
    A("smask", [128, 16 * C])
    A("gpp", [128, 16 * C])
    A("dd", [128, 16 * C])
    A("ea", [128, 16 * C]); A("eb", [128, 16 * C])
    A("estate", [128, 16])
    A("qgb", [128, 16 * C], BF16); A("qhb", [128, 16 * C], BF16)
    A("khb", [128, 16 * C], BF16)
    A("kwb", [128, 16 * 128], BF16)
    A("atm", [128, C], BF16)
    A("kwc", [128, 128], BF16)
    A("S", [128, H * 128])
    A("Sb", [128, 128], BF16)
    A("ms", [128, 2])
    A("ogst", [128, D])
    ps = {}
    for nm in ("pj0", "pj1", "gk", "at", "o", "ds"):
        ps[nm] = es.enter_context(nc.psum_tensor("ps_" + nm, [128, 512], F32))
    ps["tr"] = es.enter_context(nc.psum_tensor("ps_tr", [128, 512], BF16))

    p = Prog(nc, es)

    p.dma("sync", sb["eye"][:], dr["eyebf"], writes=["eye"])
    p.dma("sync", sb["mask"][:], dr["mask64"], writes=["mask"])
    p.dma("sync", sb["bgk2"][:], dr["bgk2c"], writes=["bgk2"])
    p.dma("gpsimd", sb["wgk2s"][:], dr["wgk2"], writes=["wgk2s"])
    p.dma("gpsimd", sb["wgk1s"].ap().rearrange("p (kt o) -> p kt o", kt=16),
          dr["wgk1"].rearrange("(kt p) o -> p kt o", p=128), writes=["wgk1s"])
    p.add("vector", lambda e: e.tensor_scalar_mul(sb["nbgk2"][:], sb["bgk2"][:], -1.0),
          reads=["bgk2"], writes=["nbgk2"])
    p.add("vector", lambda e: e.memset(sb["smask"][:], 1.0), writes=["smask"])
    p.add("vector", lambda e: e.memset(sb["smask"][:, 0:16 * C:C], 0.0),
          reads=["smask"], writes=["smask"])
    p.add("vector", lambda e: e.memset(sb["S"][:], 0.0), writes=["S"])
    p.add("vector", lambda e: e.memset(sb["kwb"][:], 0.0), writes=["kwb"])
    p.add("vector", lambda e: e.memset(sb["atm"][:], 0.0), writes=["atm"])

    ell3 = sb["ell"].ap().rearrange("p (c h t) -> p c h t", c=8, h=16, t=C)

    def superblock(sbi, pref, xdr, row0):
        for tt in range(4):
            xt_ = sb["xin"][:, ts(tt % 2, D)]
            xtok = ("xin", tt % 2)
            p.dma("sync", xt_, xdr[row0 + tt * 128: row0 + (tt + 1) * 128, :],
                  writes=[xtok])
            st = sb["stat"]
            c0, c1, c2, c3 = (st[:, 4 * tt + j:4 * tt + j + 1] for j in range(4))
            p.add("vector", lambda e, x=xt_, o=c0: e.reduce_sum(o, x, AX.X),
                  reads=[xtok], writes=[("st0", tt)])
            p.add("scalar", lambda e, x=xt_, a=c1: e.activation(
                out=sb["ogst"][:], in_=x, func=AF.Square, accum_out=a),
                reads=[xtok], writes=[("st1", tt), "ogst"])
            p.add("vector", lambda e, a=c2, b=c0: e.tensor_scalar_mul(a, b, 1.0 / D),
                  reads=[("st0", tt)], writes=[("stm", tt)])
            p.add("vector", lambda e, a=c1: e.tensor_scalar(
                a, a, 1.0 / D, 1e-5, ALU.mult, ALU.add),
                reads=[("st1", tt)], writes=[("st1", tt)])
            p.add("vector", lambda e, a=c3, b=c2: e.scalar_tensor_tensor(
                a, b, 1.0, b, ALU.mult, ALU.mult),
                reads=[("stm", tt)], writes=[("stmm", tt)])
            p.add("vector", lambda e, a=c1, b=c3: e.tensor_sub(a, a, b),
                  reads=[("st1", tt), ("stmm", tt)], writes=[("st1", tt)])
            p.add("scalar", lambda e, a=c3, b=c1: e.activation(
                out=a, in_=b, func=AF.Sqrt),
                reads=[("st1", tt)], writes=[("stsq", tt)])
            p.add("vector", lambda e, a=c1, b=c3: e.reciprocal(a, b),
                  reads=[("stsq", tt)], writes=[("strs", tt)])
            p.add("vector", lambda e, x=xt_, tt=tt, m=c2, r=c1: e.tensor_scalar(
                sb["xln"][:, ts(tt % 2, D)], x, m, r, ALU.subtract, ALU.mult),
                reads=[xtok, ("stm", tt), ("strs", tt)], writes=[("xln", tt % 2)])
            for dt_ in range(16):
                p.add("tensor", lambda e, dt_=dt_, tt=tt: e.transpose(
                    ps["tr"][:, 0:128],
                    sb["xln"][:, dt_ * 128 + (tt % 2) * D: dt_ * 128 + (tt % 2) * D + 128],
                    sb["eye"][:]),
                    reads=[("xln", tt % 2), "eye"], writes=["ps_tr"])
                p.add("scalar", lambda e, dt_=dt_, tt=tt: e.copy(
                    sb["xhT"][:, dt_ * SB + tt * 128: dt_ * SB + (tt + 1) * 128],
                    ps["tr"][:, 0:128]),
                    reads=["ps_tr"], writes=[("xhT", dt_, tt)])
        xhT_toks = [("xhT", d, t) for d in range(16) for t in range(4)]
        for (wnm, bufs, dst) in (("wq", ("wqs0", "wqs1"), "qb"),
                                 ("wk", ("wks0", "wks1"), "kb")):
            if pref and wnm == "wq":
                continue
            for oi in range(16):
                wbuf = bufs[oi % 2]
                p.dma("gpsimd", sb[wbuf].ap().rearrange("p (kt o) -> p kt o", kt=16),
                      dr[wnm][:, ts(oi, 128)].rearrange("(kt p) o -> p kt o", p=128),
                      writes=[(wbuf,)])
                for ki in range(16):
                    p.add("tensor", lambda e, ki=ki, wbuf=wbuf: e.matmul(
                        ps["pj0"][:, 0:SB], sb[wbuf][:, ts(ki, 128)],
                        sb["xhT"][:, ts(ki, SB)], start=(ki == 0), stop=(ki == 15)),
                        reads=[(wbuf,)] + xhT_toks, writes=["ps_pj0"])
                p.add("scalar", lambda e, oi=oi, dst=dst: e.copy(
                    sb[dst][:, ts(oi, SB)], ps["pj0"][:, 0:SB]),
                    reads=["ps_pj0"], writes=[(dst, oi)])
        for ki in range(16):
            p.add("tensor", lambda e, ki=ki: e.matmul(
                ps["gk"][0:16, 0:SB], sb["wgk1s"][:, ts(ki, 16)],
                sb["xhT"][:, ts(ki, SB)], start=(ki == 0), stop=(ki == 15)),
                reads=["wgk1s"] + xhT_toks, writes=["ps_gk"])
        p.add("vector", lambda e: e.tensor_copy(sb["gk1b"][:], ps["gk"][0:16, 0:SB]),
              reads=["ps_gk"], writes=["gk1b"])
        for h in range(16):
            p.add("tensor", lambda e, h=h: e.matmul(
                ps["pj1"][:, 0:SB], sb["wgk2s"][0:16, ts(h, 128)], sb["gk1b"][:],
                start=True, stop=True),
                reads=["wgk2s", "gk1b"], writes=["ps_pj1"])
            bcol = sb["bgk2"][:, h:h + 1]
            nbcol = sb["nbgk2"][:, h:h + 1]
            p.add("scalar", lambda e, b=bcol: e.activation(
                out=sb["lsa"][:], in_=ps["pj1"][:, 0:SB], func=AF.Abs, bias=b),
                reads=["ps_pj1", "bgk2"], writes=["lsa"])
            p.add("scalar", lambda e: e.activation(
                out=sb["lsb"][:], in_=sb["lsa"][:], func=AF.Exp, scale=-1.0),
                reads=["lsa"], writes=["lsb"])
            p.add("scalar", lambda e: e.activation(
                out=sb["lsc"][:], in_=sb["lsb"][:], func=AF.Ln, bias=1.0),
                reads=["lsb"], writes=["lsc"])
            p.add("scalar", lambda e, nb=nbcol: e.activation(
                out=sb["lsa"][:], in_=ps["pj1"][:, 0:SB], func=AF.Relu,
                bias=nb, scale=-1.0),
                reads=["ps_pj1", "nbgk2"], writes=["lsa"])
            out3 = ell3[:, :, h, :]
            in3a = sb["lsa"].ap().rearrange("p (c t) -> p c t", c=8, t=C)
            in3b = sb["lsc"].ap().rearrange("p (c t) -> p c t", c=8, t=C)
            p.add("vector", lambda e, o=out3, a=in3a, b=in3b: e.tensor_add(o, a, b),
                  reads=["lsa", "lsc"], writes=[("ell", h)])
        for ni in range(16):
            wv_buf = ("wvs0", "wvs1")[ni % 2]
            p.dma("gpsimd", sb[wv_buf].ap().rearrange("p (kt o) -> p kt o", kt=16),
                  dr["wv"][:, ts(ni, 128)].rearrange("(kt p) o -> p kt o", p=128),
                  writes=[(wv_buf,)])
            for tt in range(4):
                for ki in range(16):
                    p.add("tensor", lambda e, tt=tt, ki=ki, wv_buf=wv_buf: e.matmul(
                        ps["pj1"][:, 0:128],
                        sb["xhT"][:, ki * SB + tt * 128: ki * SB + (tt + 1) * 128],
                        sb[wv_buf][:, ts(ki, 128)], start=(ki == 0), stop=(ki == 15)),
                        reads=[(wv_buf,)] + xhT_toks, writes=["ps_pj1"])
                p.add("scalar", lambda e, ni=ni, tt=tt: e.copy(
                    sb["vb"][:, tt * D + ni * 128: tt * D + (ni + 1) * 128],
                    ps["pj1"][:, 0:128]),
                    reads=["ps_pj1"], writes=[("vb", tt, ni)])
        vb_toks = [("vb", t, n) for t in range(4) for n in range(16)]
        r16 = lambda t_: sb[t_].ap().rearrange("p (h t) -> p h t", h=16, t=C)
        for ch in range(8):
            tt, par = ch // 2, ch % 2
            r0 = par * 64
            p.add("vector", lambda e, ch=ch: e.tensor_tensor_scan(
                sb["gpp"][:], sb["smask"][:], sb["ell"][:, ts(ch, 16 * C)],
                0.0, ALU.mult, ALU.add),
                reads=["smask"] + [("ell", h) for h in range(16)], writes=["gpp"])
            gb = r16("gpp")
            gb0 = gb[:, :, 0:1].broadcast_to([128, 16, C])
            gbl = gb[:, :, C - 1:C].broadcast_to([128, 16, C])
            qb_toks = [("qb", oi) for oi in range(16)]
            kb_toks = [("kb", oi) for oi in range(16)]
            ch_sl = lambda t_: sb[t_].ap().rearrange(
                "p (h t) -> p h t", h=16, t=SB)[:, :, ch * C:(ch + 1) * C]
            if not pref:
                p.add("gpsimd", lambda e, a=gb, b=gb0: e.tensor_sub(r16("dd"), a, b),
                      reads=["gpp"], writes=["dd"])
                p.add("scalar", lambda e: e.activation(
                    out=sb["ea"][:], in_=sb["dd"][:], func=AF.Exp, scale=-1.0 / 16),
                    reads=["dd"], writes=["ea"])
                p.add("scalar", lambda e: e.activation(
                    out=sb["eb"][:], in_=sb["dd"][:], func=AF.Exp, scale=1.0 / 16),
                    reads=["dd"], writes=["eb"])
                p.add("vector", lambda e, q=ch_sl("qb"): e.scalar_tensor_tensor(
                    r16("qhb"), q, GLA_SCALE, r16("ea"), ALU.mult, ALU.mult),
                    reads=qb_toks + ["ea"], writes=["qhb"])
                p.add("gpsimd", lambda e, k=ch_sl("kb"): e.tensor_mul(
                    r16("khb"), k, r16("eb")),
                    reads=kb_toks + ["eb"], writes=["khb"])
            p.add("gpsimd", lambda e, a=gb, b=gbl: e.tensor_sub(r16("dd"), a, b),
                  reads=["gpp", "ea", "eb"] if not pref else ["gpp"], writes=["dd"])
            p.add("scalar", lambda e: e.activation(
                out=sb["ea"][:], in_=sb["dd"][:], func=AF.Exp, scale=1.0 / 16),
                reads=["dd"], writes=["ea"])
            kw3out = sb["kwb"].ap().rearrange("p (h t) -> p h t", h=16, t=128)[
                :, :, r0:r0 + 64]
            p.add("gpsimd", lambda e, k=ch_sl("kb"), o=kw3out: e.tensor_mul(
                o, k, r16("ea")),
                reads=kb_toks + ["ea", "kwb"], writes=["kwb"])
            if not pref:
                p.add("scalar", lambda e: e.activation(
                    out=sb["eb"][:], in_=sb["gpp"][:], func=AF.Exp, scale=-1.0 / 16),
                    reads=["gpp", "khb"], writes=["eb"])
                p.add("vector", lambda e, q=ch_sl("qb"): e.scalar_tensor_tensor(
                    r16("qgb"), q, GLA_SCALE, r16("eb"), ALU.mult, ALU.mult),
                    reads=qb_toks + ["eb"], writes=["qgb"])
            p.add("scalar", lambda e: e.activation(
                out=sb["estate"][:], in_=sb["gpp"][:, C - 1:16 * C:C], func=AF.Exp,
                scale=-1.0 / 16),
                reads=["gpp"], writes=["estate"])
            for h in range(16):
                vsl = sb["vb"][r0:r0 + 64, tt * D + h * 128: tt * D + (h + 1) * 128]
                if not pref:
                    p.add("vector", lambda e, h=h: e.tensor_copy(
                        sb["Sb"][:], sb["S"][:, ts(h, 128)]),
                        reads=[("S", h)], writes=["Sb"])
                    p.add("tensor", lambda e, h=h, r0=r0: e.matmul(
                        ps["at"][r0:r0 + 64, 0:64], sb["khb"][:, ts(h, C)],
                        sb["qhb"][:, ts(h, C)], start=True, stop=True,
                        tile_position=(0, r0)),
                        reads=["khb", "qhb"], writes=["ps_at"])
                    p.add("vector", lambda e, r0=r0: e.tensor_mul(
                        sb["atm"][r0:r0 + 64, :], ps["at"][r0:r0 + 64, 0:64],
                        sb["mask"][r0:r0 + 64, :]),
                        reads=["ps_at", "mask", "atm"], writes=["atm"])
                    p.add("tensor", lambda e, h=h, r0=r0: e.matmul(
                        ps["o"][r0:r0 + 64, 0:128], sb["qgb"][:, ts(h, C)],
                        sb["Sb"][:], start=True, stop=False, tile_position=(0, r0)),
                        reads=["qgb", "Sb"], writes=["ps_o"])
                    p.add("tensor", lambda e, r0=r0, v=vsl: e.matmul(
                        ps["o"][r0:r0 + 64, 0:128], sb["atm"][r0:r0 + 64, :],
                        v, start=False, stop=True, tile_position=(r0, r0)),
                        reads=["atm"] + vb_toks, writes=["ps_o"])
                p.add("tensor", lambda e, h=h: e.transpose(
                    ps["tr"][:, 0:128], sb["kwb"][:, ts(h, 128)], sb["eye"][:]),
                    reads=["kwb", "eye"], writes=["ps_tr"])
                p.add("scalar", lambda e, r0=r0: e.copy(
                    sb["kwc"][r0:r0 + 64, :], ps["tr"][r0:r0 + 64, 0:128]),
                    reads=["ps_tr"], writes=["kwc"])
                p.add("tensor", lambda e, r0=r0, v=vsl: e.matmul(
                    ps["ds"][:, 0:128], sb["kwc"][r0:r0 + 64, :], v,
                    start=True, stop=True, tile_position=(r0, 0)),
                    reads=["kwc"] + vb_toks, writes=["ps_ds"])
                p.add("vector", lambda e, h=h: e.scalar_tensor_tensor(
                    sb["S"][:, ts(h, 128)], sb["S"][:, ts(h, 128)],
                    sb["estate"][:, h:h + 1], ps["ds"][:, 0:128], ALU.mult, ALU.add),
                    reads=[("S", h), "estate", "ps_ds"], writes=[("S", h)])
                if not pref:
                    p.add("scalar", lambda e, r0=r0: e.activation(
                        out=sb["lsa"][r0:r0 + 64, 0:128], in_=ps["o"][r0:r0 + 64, 0:128],
                        func=AF.Square, accum_out=sb["ms"][r0:r0 + 64, 0:1]),
                        reads=["ps_o"], writes=["lsa", "ms0"])
                    p.add("vector", lambda e, r0=r0: e.tensor_scalar(
                        sb["ms"][r0:r0 + 64, 1:2], sb["ms"][r0:r0 + 64, 0:1],
                        1.0 / 128, 1e-5, ALU.mult, ALU.add),
                        reads=["ms0"], writes=["ms1"])
                    p.add("vector", lambda e, r0=r0: e.reciprocal(
                        sb["ms"][r0:r0 + 64, 0:1], sb["ms"][r0:r0 + 64, 1:2]),
                        reads=["ms1"], writes=["ms0"])
                    p.add("scalar", lambda e, r0=r0: e.activation(
                        out=sb["ms"][r0:r0 + 64, 1:2], in_=sb["ms"][r0:r0 + 64, 0:1],
                        func=AF.Sqrt),
                        reads=["ms0"], writes=["ms1"])
                    p.add("vector", lambda e, h=h, r0=r0: e.tensor_scalar_mul(
                        sb["ogst"][r0:r0 + 64, ts(h, 128)],
                        ps["o"][r0:r0 + 64, 0:128], sb["ms"][r0:r0 + 64, 1:2]),
                        reads=["ps_o", "ms1"], writes=["ogst"])
            if (not pref) and par == 1:
                p.dma("gpsimd", dr["og"][row0 + tt * 128: row0 + (tt + 1) * 128, :],
                      sb["ogst"][:],
                      reads=["ogst"], writes=[("og_out", sbi, tt)])

    A2 = lambda nm, shp, dt=F32: A(nm, shp, dt)
    A2("gw0", [128, 512]); A2("gw1", [128, 512])
    A2("gacc", [128, 4])
    p.add("vector", lambda e: e.memset(sb["gacc"][:], 0.0),
          writes=["gacc3", ("gacca", 0), ("gacca", 1)])
    for gi, (wnm, onm, acol) in enumerate(
            [("wfcT", "gabs_fc", 0), ("wprojT", "gabs_proj", 1)]):
        nrow, ncol = dr[wnm].shape
        nt = (nrow // 128) * (ncol // 512)
        cpr = ncol // 512
        for t_ in range(nt):
            r, c_ = t_ // cpr, t_ % cpr
            buf = ("gw0", "gw1")[t_ % 2]
            p.dma("sync", sb[buf][:],
                  dr[wnm][r * 128:(r + 1) * 128, c_ * 512:(c_ + 1) * 512],
                  writes=[(buf,)])
            p.add("vector", lambda e, buf=buf: e.tensor_reduce(
                sb["gacc"][:, 3:4], sb[buf][:], AX.X, ALU.add,
                apply_absolute_value=True),
                reads=[(buf,)], writes=["gacc3"])
            p.add("vector", lambda e, acol=acol: e.tensor_add(
                sb["gacc"][:, acol:acol + 1], sb["gacc"][:, acol:acol + 1],
                sb["gacc"][:, 3:4]),
                reads=["gacc3", ("gacca", acol)], writes=[("gacca", acol)])
        p.dma("gpsimd", dr[onm], sb["gacc"][:, acol:acol + 1],
              reads=[("gacca", acol)], writes=[("gout", gi)])

    for i in range(NSB):
        superblock(i, True, dr["xp"], i * SB)
    for i in range(NSB):
        superblock(NSB + i, False, dr["xo"], i * SB)

    p.add("gpsimd", lambda e: e.nop(),
          reads=[("og_out", NSB + s, t) for s in range(NSB) for t in range(4)]
                + [("gout", 0), ("gout", 1)])
    p.finalize()
    return nc, es




def build_B():
    nc = bass.Bass("TRN2", target_bir_lowering=False, debug=False)
    es = contextlib.ExitStack()
    es.__enter__()
    dr = {}
    for nm, shp, dt, kind in [
        ("x", [1024, D], F32, "ExternalInput"),
        ("og", [1024, D], F32, "ExternalInput"),
        ("wg", [D, D], F32, "ExternalInput"),
        ("wo", [D, D], F32, "ExternalInput"),
        ("gnormc", [128, 1], F32, "ExternalInput"),
        ("wfcT", [D, 4 * D], F32, "ExternalInput"),
        ("wprojT", [4 * D, D], F32, "ExternalInput"),
        ("w_inT", [D, D], F32, "ExternalInput"),
        ("binrow", [1, D], F32, "ExternalInput"),
        ("gfc", [128, 1], F32, "ExternalInput"),
        ("gproj", [128, 1], F32, "ExternalInput"),
        ("eyebf", [128, 128], BF16, "ExternalInput"),
        ("x3", [1024, D], F32, "ExternalOutput"),
        ("h", [1024, D], F32, "ExternalOutput"),
        ("tfc", [D, 4 * D], BF16, "Internal"),
        ("tproj", [4 * D, D], BF16, "Internal"),
        ("x2s", [1024, D], F32, "Internal"),
    ]:
        dr[nm] = nc.dram_tensor(nm, shp, dt, kind=kind).ap()

    sb = {}

    def A(nm, shp, dt=F32):
        sb[nm] = es.enter_context(nc.sbuf_tensor(nm, shp, dt))
        return sb[nm]

    A("eye", [128, 128], BF16)
    A("ones", [128, 1])
    A("half", [128, 1]); A("nhalf", [128, 1])
    A("gcol", [128, 4])
    A("gnc", [128, 1])
    A("cell", [1, 4])
    A("wt0", [128, 512]); A("wt1", [128, 512])
    A("sg1", [128, 512], BF16); A("sg2", [128, 512], BF16)
    A("tau0", [128, 512], BF16); A("tau1", [128, 512], BF16)
    A("xt0", [128, D])
    A("stat", [128, 24])
    A("xln", [128, D], BF16)
    A("onesb", [1, 128], BF16)
    A("xhT", [128, 16 * 1024], BF16)
    A("wstr0", [128, 16 * 512], BF16); A("wstr1", [128, 16 * 512], BF16)
    A("ogt", [128, 2 * D])
    A("sgt", [128, 512])
    A("oggT", [128, 16 * 256], BF16)
    A("ao", [128, 2 * D])
    A("dmy", [128, D])
    A("hst", [128, 4 * D])
    A("hT", [128, 64 * 128], BF16)
    A("x2t", [128, D])
    A("o2", [128, D])
    A("bin", [1, D], BF16)
    ps = {}
    for nm in ("p0", "p1", "c"):
        ps[nm] = es.enter_context(nc.psum_tensor("psB_" + nm, [128, 512], F32))
    ps["tr"] = es.enter_context(nc.psum_tensor("psB_tr", [128, 512], BF16))

    p = Prog(nc, es)
    p.dma("sync", sb["eye"][:], dr["eyebf"], writes=["eye"])
    p.dma("sync", sb["gcol"][:, 0:1], dr["gfc"], writes=["gin0"])
    p.dma("sync", sb["gcol"][:, 2:3], dr["gproj"], writes=["gin2"])
    p.dma("sync", sb["gnc"][:], dr["gnormc"], writes=["gnc"])
    p.dma("gpsimd", sb["bin"][:], dr["binrow"], writes=["bin"])
    p.add("vector", lambda e: e.memset(sb["ones"][:], 1.0), writes=["ones"])
    p.add("vector", lambda e: e.memset(sb["onesb"][:], 1.0), writes=["onesb"])
    p.add("vector", lambda e: e.memset(sb["half"][:], 0.5), writes=["half"])
    p.add("vector", lambda e: e.memset(sb["nhalf"][:], -0.5), writes=["nhalf"])

    NW = float(D * 4 * D)
    for gi, incol in enumerate((0, 2)):
        gtok = "gin0" if incol == 0 else "gin2"
        p.add("tensor", lambda e, incol=incol: e.matmul(
            ps["c"][0:1, 0:1], sb["gcol"][:, incol:incol + 1], sb["ones"][:],
            start=True, stop=True),
            reads=[gtok, "ones"], writes=["ps_c"])
        p.add("scalar", lambda e, gi=gi: e.copy(
            sb["cell"][0:1, gi * 2:gi * 2 + 1], ps["c"][0:1, 0:1]),
            reads=["ps_c"], writes=[("cl", gi, 0)])
        p.add("vector", lambda e, gi=gi: e.tensor_scalar(
            sb["cell"][0:1, gi * 2 + 1:gi * 2 + 2],
            sb["cell"][0:1, gi * 2:gi * 2 + 1], 1.0 / NW, 1e-6, ALU.mult, ALU.add),
            reads=[("cl", gi, 0)], writes=[("cl", gi, 1)])
        p.add("vector", lambda e, gi=gi: e.reciprocal(
            sb["cell"][0:1, gi * 2 + 1:gi * 2 + 2],
            sb["cell"][0:1, gi * 2 + 1:gi * 2 + 2]),
            reads=[("cl", gi, 1)], writes=[("cl", gi, 1)])
        p.add("vector", lambda e, gi=gi: e.tensor_scalar_mul(
            sb["cell"][0:1, gi * 2:gi * 2 + 1],
            sb["cell"][0:1, gi * 2:gi * 2 + 1], 0.5 / NW),
            reads=[("cl", gi, 0)], writes=[("cl", gi, 0)])
        p.add("tensor", lambda e, gi=gi: e.matmul(
            ps["c"][:, 0:2], sb["ones"][0:1, 0:1].broadcast_to([1, 128]),
            sb["cell"][0:1, gi * 2:gi * 2 + 2], start=True, stop=True),
            reads=[("cl", gi, 0), ("cl", gi, 1), "ones"], writes=["ps_c"])
        p.add("scalar", lambda e, incol=incol: e.copy(
            sb["gcol"][:, incol:incol + 2], ps["c"][:, 0:2]),
            reads=["ps_c"], writes=[("g2", incol), gtok])

    def ternarize(wnm, onm, rgcol):
        nrow, ncol = dr[wnm].shape
        for r in range(nrow // 128):
            for c_ in range(ncol // 512):
                buf, tau = (("wt0", "tau0"),
                            ("wt1", "tau1"))[(r * (ncol // 512) + c_) % 2]
                p.dma("sync", sb[buf][:],
                      dr[wnm][r * 128:(r + 1) * 128, c_ * 512:(c_ + 1) * 512],
                      writes=[(buf,)])
                p.add("scalar", lambda e, buf=buf, rgcol=rgcol: e.activation(
                    out=sb["sg1"][:], in_=sb[buf][:], func=AF.Sign,
                    bias=sb["nhalf"][:, 0:1], scale=sb["gcol"][:, rgcol:rgcol + 1]),
                    reads=[(buf,), "nhalf", ("g2", rgcol - 1)], writes=["sg1"])
                p.add("scalar", lambda e, buf=buf, rgcol=rgcol: e.activation(
                    out=sb["sg2"][:], in_=sb[buf][:], func=AF.Sign,
                    bias=sb["half"][:, 0:1], scale=sb["gcol"][:, rgcol:rgcol + 1]),
                    reads=[(buf,), "half", ("g2", rgcol - 1)], writes=["sg2"])
                p.add("vector", lambda e, tau=tau: e.tensor_add(
                    sb[tau][:], sb["sg1"][:], sb["sg2"][:]),
                    reads=["sg1", "sg2"], writes=[(tau,)])
                p.dma("gpsimd",
                      dr[onm][r * 128:(r + 1) * 128, c_ * 512:(c_ + 1) * 512],
                      sb[tau][:], reads=[(tau,)], writes=[("tw", onm)])
    ternarize("wfcT", "tfc", 1)
    ternarize("wprojT", "tproj", 3)

    def ln_tile(xt_, xtoks, eps, outsl, outtok, sidx):
        if not isinstance(xtoks, list):
            xtoks = [xtoks]
        st = sb["stat"]
        c0, c1, c2, c3 = (st[:, 4 * sidx + j:4 * sidx + j + 1] for j in range(4))
        nf = 1.0 / xt_.shape[1]
        p.add("vector", lambda e, x=xt_, o=c0: e.reduce_sum(o, x, AX.X),
              reads=xtoks, writes=[("s0", sidx)])
        p.add("scalar", lambda e, x=xt_, a=c1: e.activation(
            out=sb["dmy"][:, 0:x.shape[1]], in_=x, func=AF.Square, accum_out=a),
            reads=xtoks, writes=[("s1", sidx), "lndmy"])
        p.add("vector", lambda e, a=c2, b=c0, nf=nf: e.tensor_scalar_mul(a, b, nf),
              reads=[("s0", sidx)], writes=[("sm", sidx)])
        p.add("vector", lambda e, a=c1, nf=nf, eps=eps: e.tensor_scalar(
            a, a, nf, eps, ALU.mult, ALU.add),
            reads=[("s1", sidx)], writes=[("s1", sidx)])
        p.add("vector", lambda e, a=c3, b=c2: e.scalar_tensor_tensor(
            a, b, 1.0, b, ALU.mult, ALU.mult),
            reads=[("sm", sidx)], writes=[("smm", sidx)])
        p.add("vector", lambda e, a=c1, b=c3: e.tensor_sub(a, a, b),
              reads=[("s1", sidx), ("smm", sidx)], writes=[("s1", sidx)])
        p.add("scalar", lambda e, a=c3, b=c1: e.activation(out=a, in_=b, func=AF.Sqrt),
              reads=[("s1", sidx)], writes=[("ssq", sidx)])
        p.add("vector", lambda e, a=c1, b=c3: e.reciprocal(a, b),
              reads=[("ssq", sidx)], writes=[("srs", sidx)])
        p.add("vector", lambda e, x=xt_, m=c2, r=c1, o=outsl: e.tensor_scalar(
            o, x, m, r, ALU.subtract, ALU.mult),
            reads=xtoks + [("sm", sidx), ("srs", sidx)], writes=[outtok])

    def tile_transposes(ntile, src_fn, srctoks, dst_fn, dsttok_fn):
        for dt_ in range(ntile):
            p.add("tensor", lambda e, dt_=dt_: e.transpose(
                ps["tr"][:, 0:128], src_fn(dt_), sb["eye"][:]),
                reads=srctoks + ["eye"], writes=["ps_tr"])
            p.add("scalar", lambda e, dt_=dt_: e.copy(dst_fn(dt_), ps["tr"][:, 0:128]),
                  reads=["ps_tr"], writes=[dsttok_fn(dt_)])

    for tt in range(8):
        xt_ = sb["xt0"][:]
        xtok = "xld"
        p.dma("sync", xt_, dr["x"][tt * 128:(tt + 1) * 128, :], writes=[xtok])
        ln_tile(xt_, xtok, 1e-5, sb["xln"][:], "xln", tt % 4)
        tile_transposes(
            16, lambda d: sb["xln"][:, ts(d, 128)], ["xln"],
            lambda d, tt=tt: sb["xhT"][:, d * 1024 + tt * 128: d * 1024 + (tt + 1) * 128],
            lambda d, tt=tt: ("xhT", tt))
    xhT_toks = [("xhT", t) for t in range(8)]

    for sbi in range(4):
        for tt in range(2):
            p.dma("sync", sb["ogt"][:, ts(tt, D)],
                  dr["og"][sbi * 256 + tt * 128: sbi * 256 + (tt + 1) * 128, :],
                  reads=["lndmy2"], writes=[("og4", tt)])
        for oc in range(4):
            wbuf = ("wstr0", "wstr1")[oc % 2]
            p.dma("gpsimd", sb[wbuf].ap().rearrange("p (kt o) -> p kt o", kt=16),
                  dr["wg"][:, ts(oc, 512)].rearrange("(kt p) o -> p kt o", p=128),
                  writes=[(wbuf,)])
            for tt in range(2):
                gt = sbi * 2 + tt
                for ki in range(16):
                    p.add("tensor", lambda e, ki=ki, wbuf=wbuf, gt=gt: e.matmul(
                        ps["p0"][:, 0:512],
                        sb["xhT"][:, ki * 1024 + gt * 128: ki * 1024 + (gt + 1) * 128],
                        sb[wbuf][:, ts(ki, 512)], start=(ki == 0), stop=(ki == 15)),
                        reads=[(wbuf,)] + xhT_toks, writes=["ps_p0"])
                p.add("scalar", lambda e: e.activation(
                    out=sb["sgt"][:], in_=ps["p0"][:, 0:512], func=AF.Silu),
                    reads=["ps_p0"], writes=["sgt"])
                p.add("vector", lambda e, tt=tt, oc=oc: e.tensor_mul(
                    sb["ao"].ap().bitcast(BF16)[:, tt * D + oc * 512:
                                                tt * D + (oc + 1) * 512],
                    sb["ogt"][:, tt * D + oc * 512: tt * D + (oc + 1) * 512],
                    sb["sgt"][:]),
                    reads=["sgt", ("og4", tt)], writes=[("aox",)])
        ogg_toks = [("aox",)]
        tile_transposes(
            32, lambda d: sb["ao"].ap().bitcast(BF16)[
                :, (d % 16) * 128 + (d // 16) * D:
                (d % 16) * 128 + (d // 16) * D + 128],
            ogg_toks,
            lambda d: sb["oggT"][:, (d % 16) * 256 + (d // 16) * 128:
                                 (d % 16) * 256 + (d // 16) * 128 + 128],
            lambda d: ("oggT", d % 16))
        oggT_toks = [("oggT", d) for d in range(16)]
        for oc in range(4):
            wbuf = ("wstr0", "wstr1")[oc % 2]
            p.dma("gpsimd", sb[wbuf].ap().rearrange("p (kt o) -> p kt o", kt=16),
                  dr["wo"][:, ts(oc, 512)].rearrange("(kt p) o -> p kt o", p=128),
                  writes=[(wbuf,)])
            p.add("vector", lambda e, wbuf=wbuf: e.tensor_scalar_mul(
                sb[wbuf][:], sb[wbuf][:], sb["gnc"][:, 0:1]),
                reads=[(wbuf,), "gnc"], writes=[(wbuf,)])
            for tt in range(2):
                for ki in range(16):
                    p.add("tensor", lambda e, ki=ki, wbuf=wbuf, tt=tt: e.matmul(
                        ps["p1"][:, 0:512],
                        sb["oggT"][:, ki * 256 + tt * 128: ki * 256 + (tt + 1) * 128],
                        sb[wbuf][:, ts(ki, 512)], start=(ki == 0), stop=(ki == 15)),
                        reads=[(wbuf,)] + oggT_toks, writes=["ps_p1"])
                p.add("scalar", lambda e, tt=tt, oc=oc: e.copy(
                    sb["ao"][:, tt * D + oc * 512: tt * D + (oc + 1) * 512],
                    ps["p1"][:, 0:512]),
                    reads=["ps_p1"], writes=[("aox",), ("aow", tt, oc)])
        for tt in range(2):
            gt = sbi * 2 + tt
            xt_ = sb["xt0"][:]
            xtok = "xld"
            p.dma("sync", xt_, dr["x"][gt * 128:(gt + 1) * 128, :], writes=[xtok])
            aosl = sb["ao"][:, ts(tt, D)]
            ao_toks = [("aow", tt, o) for o in range(4)] + [("aox",)]
            ln_tile(aosl, ao_toks, 1e-5, sb["x2t"][:], "x2w", tt)
            p.add("vector", lambda e, x=xt_: e.tensor_add(
                sb["x2t"][:], sb["x2t"][:], x),
                reads=["x2w", xtok], writes=["x2w"])
            p.dma("gpsimd", dr["x2s"][gt * 128:(gt + 1) * 128, :], sb["x2t"][:],
                  reads=["x2w"], writes=[("x2s", gt)])

    for gt in range(8):
        xt_ = sb["xt0"][:]
        xtok = "xld"
        p.dma("sync", xt_, dr["x2s"][gt * 128:(gt + 1) * 128, :],
              reads=[("x2s", gt)], writes=[xtok])
        ln_tile(xt_, xtok, 1e-6, sb["xln"][:], "xln", 0)
        tile_transposes(
            16, lambda d: sb["xln"][:, ts(d, 128)], ["xln"],
            lambda d: sb["oggT"][:, ts(d, 128)][:, 0:128] if False else
                      sb["oggT"][:, d * 128: (d + 1) * 128],
            lambda d: ("oggT", d))
        x2T_toks = [("oggT", d) for d in range(16)]
        for oc in range(16):
            wbuf = ("wstr0", "wstr1")[oc % 2]
            p.dma("sync", sb[wbuf].ap().rearrange("p (kt o) -> p kt o", kt=16),
                  dr["tfc"][:, ts(oc, 512)].rearrange("(kt p) o -> p kt o", p=128),
                  reads=[("tw", "tfc")], writes=[(wbuf,)])
            for ki in range(16):
                p.add("tensor", lambda e, ki=ki, wbuf=wbuf: e.matmul(
                    ps["p0"][:, 0:512], sb["oggT"][:, ts(ki, 128)],
                    sb[wbuf][:, ts(ki, 512)], start=(ki == 0), stop=(ki == 15)),
                    reads=[(wbuf,)] + x2T_toks, writes=["ps_p0"])
            p.add("scalar", lambda e, oc=oc: e.activation(
                out=sb["hst"][:, ts(oc, 512)], in_=ps["p0"][:, 0:512], func=AF.Gelu,
                scale=sb["gcol"][:, 0:1]),
                reads=["ps_p0", ("g2", 0)], writes=[("hs", oc)])
        hs_toks = [("hs", o) for o in range(16)]
        st = sb["stat"]
        c0, c1, c2, c3 = (st[:, 16 + j:16 + j + 1] for j in range(4))
        nf = 1.0 / (4 * D)
        p.add("vector", lambda e, o=c0: e.reduce_sum(o, sb["hst"][:], AX.X),
              reads=hs_toks, writes=["h0"])
        for hc in range(4):
            p.add("scalar", lambda e, hc=hc, a=c1: e.activation(
                out=sb["dmy"][:, 0:D],
                in_=sb["hst"][:, hc * D:(hc + 1) * D],
                func=AF.Square,
                accum_out=(sb["stat"][:, 20 + hc:21 + hc] if hc > 0 else a)),
                reads=hs_toks, writes=[("hsq4", hc), "lndmy"])
        for hc in range(1, 4):
            p.add("vector", lambda e, hc=hc, a=c1: e.tensor_add(
                a, a, sb["stat"][:, 20 + hc:21 + hc]),
                reads=[("hsq4", 0), ("hsq4", hc)], writes=[("hsq4", 0)])
        p.add("vector", lambda e, a=c1: e.tensor_copy(a, a),
              reads=[("hsq4", 0)], writes=["h1"])
        p.add("vector", lambda e, a=c2, b=c0, nf=nf: e.tensor_scalar_mul(a, b, nf),
              reads=["h0"], writes=["hm"])
        p.add("vector", lambda e, a=c1, nf=nf: e.tensor_scalar(
            a, a, nf, 1e-6, ALU.mult, ALU.add), reads=["h1"], writes=["h1"])
        p.add("vector", lambda e, a=c3, b=c2: e.scalar_tensor_tensor(
            a, b, 1.0, b, ALU.mult, ALU.mult), reads=["hm"], writes=["hmm"])
        p.add("vector", lambda e, a=c1, b=c3: e.tensor_sub(a, a, b),
              reads=["h1", "hmm"], writes=["h1"])
        p.add("scalar", lambda e, a=c3, b=c1: e.activation(out=a, in_=b, func=AF.Sqrt),
              reads=["h1"], writes=["hsq"])
        p.add("vector", lambda e, a=c1, b=c3: e.reciprocal(a, b),
              reads=["hsq"], writes=["hrs"])
        for dt_ in range(64):
            p.add("vector", lambda e, dt_=dt_, m=c2, r=c1: e.tensor_scalar(
                sb["xln"][:, 0:128], sb["hst"][:, ts(dt_, 128)], m, r,
                ALU.subtract, ALU.mult),
                reads=hs_toks + ["hm", "hrs", "lndmy"], writes=["xln"])
            p.add("tensor", lambda e: e.transpose(
                ps["tr"][:, 0:128], sb["xln"][:, 0:128], sb["eye"][:]),
                reads=["xln", "eye"], writes=["ps_tr"])
            p.add("scalar", lambda e, dt_=dt_: e.copy(
                sb["hT"][:, ts(dt_, 128)], ps["tr"][:, 0:128]),
                reads=["ps_tr"], writes=[("hT", dt_)])
        hT_toks = [("hT", d) for d in range(64)]
        for oc in range(4):
            for kg in range(4):
                wbuf = ("wstr0", "wstr1")[kg % 2]
                p.dma("sync", sb[wbuf].ap().rearrange("p (kt o) -> p kt o", kt=16),
                      dr["tproj"][kg * 2048:(kg + 1) * 2048, ts(oc, 512)].rearrange(
                          "(kt p) o -> p kt o", p=128),
                      reads=[("tw", "tproj")], writes=[(wbuf,)])
                for k2 in range(16):
                    ki = kg * 16 + k2
                    p.add("tensor", lambda e, ki=ki, k2=k2, wbuf=wbuf, kg=kg: e.matmul(
                        ps["p1"][:, 0:512], sb["hT"][:, ts(ki, 128)],
                        sb[wbuf][:, ts(k2, 512)],
                        start=(ki == 0), stop=(ki == 63)),
                        reads=[(wbuf,)] + hT_toks, writes=["ps_p1"])
            p.add("scalar", lambda e, oc=oc: e.activation(
                out=sb["o2"][:, ts(oc, 512)], in_=ps["p1"][:, 0:512],
                func=AF.Identity, scale=sb["gcol"][:, 2:3]),
                reads=["ps_p1", ("g2", 2)], writes=[("o2w", oc)])
        o2_toks = [("o2w", o) for o in range(4)]
        ln_tile(sb["o2"][:], o2_toks, 1e-5, sb["x2t"][:], "x2w", 1)
        p.add("vector", lambda e: e.tensor_add(sb["x2t"][:], sb["x2t"][:], sb["xt0"][:]),
              reads=["x2w", "xld"], writes=["x2w"])
        p.dma("gpsimd", dr["x3"][gt * 128:(gt + 1) * 128, :], sb["x2t"][:],
              reads=["x2w"], writes=[("x3out", gt)])
        ln_tile(sb["x2t"][:], "x2w", 1e-5, sb["xln"][:], "xln", 2)
        tile_transposes(
            16, lambda d: sb["xln"][:, ts(d, 128)], ["xln"],
            lambda d, gt=gt: sb["xhT"][:, d * 1024 + gt * 128: d * 1024 + (gt + 1) * 128],
            lambda d, gt=gt: ("xhT", gt))
    x3T_toks = [("xhT", t) for t in range(8)]
    for oc in range(4):
        wbuf = ("wstr0", "wstr1")[oc % 2]
        p.dma("gpsimd", sb[wbuf].ap().rearrange("p (kt o) -> p kt o", kt=16),
              dr["w_inT"][:, ts(oc, 512)].rearrange("(kt p) o -> p kt o", p=128),
              writes=[(wbuf,)])
        for tt in range(8):
            for ki in range(16):
                p.add("tensor", lambda e, ki=ki, wbuf=wbuf, tt=tt: e.matmul(
                    ps["p0"][:, 0:512],
                    sb["xhT"][:, ki * 1024 + tt * 128: ki * 1024 + (tt + 1) * 128],
                    sb[wbuf][:, ts(ki, 512)], start=(ki == 0), stop=False),
                    reads=[(wbuf,)] + x3T_toks, writes=["ps_p0"])
            p.add("tensor", lambda e, oc=oc: e.matmul(
                ps["p0"][:, 0:512], sb["onesb"][0:1, :],
                sb["bin"][0:1, ts(oc, 512)], start=False, stop=True),
                reads=["bin", "onesb"], writes=["ps_p0"])
            p.add("scalar", lambda e: e.activation(
                out=sb["sgt"][:], in_=ps["p0"][:, 0:512], func=AF.Gelu),
                reads=["ps_p0"], writes=["sgt"])
            p.dma("gpsimd", dr["h"][tt * 128:(tt + 1) * 128, ts(oc, 512)],
                  sb["sgt"][:], reads=["sgt"], writes=[("hout", tt, oc)])
    p.add("gpsimd", lambda e: e.nop(),
          reads=[("x3out", g) for g in range(8)]
                + [("hout", t, o) for t in range(8) for o in range(4)])
    p.finalize()
    return nc, es


def run_launch_B(x_half, og_half, Wg, Wo, gnorm_w, w_fc, w_proj, w_in, b_in,
                 gfc, gproj):
    nc, es = build_B()
    eyebf = _bf16(np.eye(128, dtype=np.float32))
    wfcT = np.ascontiguousarray(w_fc.T)
    wprojT = np.ascontiguousarray(w_proj.T)
    w_inT = np.ascontiguousarray(w_in.T)
    gnormc = np.ascontiguousarray(gnorm_w.reshape(128, 1))
    binrow = np.ascontiguousarray(b_in.reshape(1, D))
    in_maps = []
    for core in range(8):
        in_maps.append(dict(
            x=np.ascontiguousarray(x_half[core]),
            og=np.ascontiguousarray(og_half[core]),
            wg=Wg, wo=Wo, gnormc=gnormc, wfcT=wfcT, wprojT=wprojT,
            w_inT=w_inT, binrow=binrow,
            gfc=gfc, gproj=gproj, eyebf=eyebf))
    res = run_bass_kernel_spmd(nc, in_maps, core_ids=list(range(8)))
    es.__exit__(None, None, None)
    x3 = np.stack([res.results[c]["x3"] for c in range(8)])
    h = np.stack([res.results[c]["h"] for c in range(8)])
    return x3, h



DI, DS = 4096, 16
NCT = DI // 128
CTG = 4


def build_D(stop_after=None):
    nc = bass.Bass("TRN2", target_bir_lowering=False, debug=False)
    es = contextlib.ExitStack()
    es.__enter__()
    dr = {}
    for nm, shp, dt, kind in [
        ("hT", [D, L], F32, "ExternalInput"),
        ("in_wT", [D, 2 * DI], F32, "ExternalInput"),
        ("convw", [128, NCT * 4], F32, "ExternalInput"),
        ("convb", [128, NCT], F32, "ExternalInput"),
        ("xprojT", [DI, 160], F32, "ExternalInput"),
        ("dtwT", [128, DI], F32, "ExternalInput"),
        ("dtb", [128, NCT], F32, "ExternalInput"),
        ("alog", [128, NCT * DS], F32, "ExternalInput"),
        ("dparm", [128, NCT], F32, "ExternalInput"),
        ("outwT", [DI, D], F32, "ExternalInput"),
        ("w1T", [D, 4 * D], F32, "ExternalInput"),
        ("w2T", [4 * D, D], F32, "ExternalInput"),
        ("pidx", [32, 1], F32, "ExternalInput"),
        ("mo", [L, D], F32, "ExternalOutput"),
        ("qw1", [D, 4 * D], BF16, "ExternalOutput"),
        ("qw2", [4 * D, D], BF16, "ExternalOutput"),
        ("sc1", [128, 1], F32, "ExternalOutput"),
        ("sc2", [128, 1], F32, "ExternalOutput"),
        ("xis", [DI, L], F32, "Internal"),
        ("zss", [DI, L], F32, "Internal"),
        ("ygs", [DI, L], BF16, "Internal"),
        ("obw", [DI, D], BF16, "Internal"),
    ]:
        dr[nm] = nc.dram_tensor(nm, shp, dt, kind=kind).ap()

    sb = {}

    def A(nm, shp, dt=F32):
        sb[nm] = es.enter_context(nc.sbuf_tensor(nm, shp, dt))
        return sb[nm]

    A("hTb", [128, 16 * L], BF16)
    A("arena0", [128, 4 * L])
    A("y4", [128, CTG * L])
    A("dx4", [128, CTG * L])
    A("hsc", [128, L])
    A("crep", [128, L])
    A("xibf", [128, L], BF16)
    A("wct0", [128, 16 * 128], BF16); A("wct1", [128, 16 * 128], BF16)
    A("xpj0", [128, 160], BF16); A("xpj1", [128, 160], BF16)
    A("bc", [32, L])
    A("dtrb", [128, L], BF16)
    A("acol", [128, NCT * DS])
    A("lsa", [128, 512])
    A("onescol", [128, 1])
    A("half", [128, 1]); A("nhalf", [128, 1])
    A("qacc", [128, 4])
    A("qcell", [1, 4])
    A("qcol", [128, 4])
    A("selm", [32, 260])
    for nm in ("convw", "convb", "dtb", "dparm"):
        A(nm + "s", list(dr[nm].shape), F32)
    ps = {}
    for nm in ("x0", "x1", "d0", "d1", "d2", "d3", "c"):
        ps[nm] = es.enter_context(nc.psum_tensor("psD_" + nm, [128, 512], F32))
    ps["bcp"] = es.enter_context(nc.psum_tensor("psD_bcp", [32, 512], F32))

    crb = sb["crep"].ap().bitcast(BF16)
    sb["wt0"] = sb["hsc"][:, 0:512]; sb["wt1"] = sb["hsc"][:, 512:1024]
    sb["qdmy"] = sb["hsc"][:, 1024:1536]
    sb["sg1"] = crb[:, 0:512]; sb["sg2"] = crb[:, 512:1024]
    sb["tau0"] = crb[:, 1024:1536]; sb["tau1"] = crb[:, 1536:2048]
    ar0 = sb["arena0"]
    xz_sl = ar0[:, 0:L]
    cv_sl = ar0[:, L:2 * L]
    da_sl = [ar0[:, 0:L], ar0[:, L:2 * L]]
    dbx_sl = ar0[:, 2 * L:3 * L]
    zst_sl = ar0[:, 2 * L:3 * L]
    htm_sl = ar0[:, 3 * L:4 * L]
    sb["lsc"] = ar0[:, 3 * L:3 * L + 512]
    sb["lsb"] = ar0[:, 3 * L + 512:3 * L + 1024]
    hTbv = sb["hTb"].ap().bitcast(F32)
    xi4_sl = lambda ci: hTbv[:, ci * L:(ci + 1) * L]
    dt4_sl = lambda ci: hTbv[:, (4 + ci) * L:(5 + ci) * L]
    ygres = sb["hTb"]
    obstr = sb["dx4"].ap().bitcast(BF16)

    p = Prog(nc, es)
    p.add("vector", lambda e: e.memset(sb["onescol"][:], 1.0), writes=["onescol"])
    p.add("vector", lambda e: e.memset(sb["half"][:], 0.5), writes=["half"])
    p.add("vector", lambda e: e.memset(sb["nhalf"][:], -0.5), writes=["nhalf"])
    p.add("vector", lambda e: e.memset(sb["qacc"][:], 0.0), writes=["qa0", "qa1", "qac3"])
    for nm in ("convw", "convb", "dtb", "dparm"):
        p.dma("sync", sb[nm + "s"][:], dr[nm], writes=[nm])
    p.dma("sync", sb["selm"][:, 256:257], dr["pidx"], writes=["pidx"])
    p.dma("sync", sb["hsc"][:, 0:NCT * DS], dr["alog"], writes=["alog", "hsc"])
    p.add("scalar", lambda e: e.activation(
        out=sb["acol"][:], in_=sb["hsc"][:, 0:NCT * DS], func=AF.Exp),
        reads=["alog", "hsc"], writes=["acol"])
    p.add("vector", lambda e: e.tensor_scalar_mul(sb["acol"][:], sb["acol"][:], -1.0),
          reads=["acol"], writes=["acol"])

    def qstats(wnm, ai):
        nrow, ncol = dr[wnm].shape
        for t_ in range((nrow // 128) * (ncol // 512)):
            r, c_ = t_ // (ncol // 512), t_ % (ncol // 512)
            buf = ("wt0", "wt1")[t_ % 2]
            p.dma("scalar", sb[buf],
                  dr[wnm][r * 128:(r + 1) * 128, c_ * 512:(c_ + 1) * 512],
                  writes=[(buf,), "hsc"])
            p.add("scalar", lambda e, buf=buf: e.activation(
                out=sb["qdmy"], in_=sb[buf], func=AF.Square,
                accum_out=sb["qacc"][:, 3:4]),
                reads=[(buf,)], writes=["hsc", "qac3"])
            p.add("vector", lambda e, ai=ai: e.tensor_add(
                sb["qacc"][:, ai:ai + 1], sb["qacc"][:, ai:ai + 1],
                sb["qacc"][:, 3:4]),
                reads=["qac3", "qa%d" % ai], writes=["qa%d" % ai])
    qstats("w1T", 0)
    qstats("w2T", 1)
    NW = float(D * 4 * D)
    for gi in range(2):
        p.add("tensor", lambda e, gi=gi: e.matmul(
            ps["c"][0:1, 0:1], sb["qacc"][:, gi:gi + 1], sb["onescol"][:],
            start=True, stop=True),
            reads=["qa%d" % gi, "onescol"], writes=["ps_c"])
        p.add("scalar", lambda e, gi=gi: e.copy(
            sb["qcell"][0:1, gi * 2:gi * 2 + 1], ps["c"][0:1, 0:1]),
            reads=["ps_c"], writes=[("qc", gi, 0)])
        p.add("vector", lambda e, gi=gi: e.tensor_scalar(
            sb["qcell"][0:1, gi * 2:gi * 2 + 1],
            sb["qcell"][0:1, gi * 2:gi * 2 + 1], 1.0 / NW, 1e-5, ALU.mult, ALU.max),
            reads=[("qc", gi, 0)], writes=[("qc", gi, 0)])
        p.add("scalar", lambda e, gi=gi: e.activation(
            out=sb["qcell"][0:1, gi * 2 + 1:gi * 2 + 2],
            in_=sb["qcell"][0:1, gi * 2:gi * 2 + 1], func=AF.Sqrt),
            reads=[("qc", gi, 0)], writes=[("qc", gi, 1)])
        p.add("vector", lambda e, gi=gi: e.reciprocal(
            sb["qcell"][0:1, gi * 2:gi * 2 + 1],
            sb["qcell"][0:1, gi * 2 + 1:gi * 2 + 2]),
            reads=[("qc", gi, 1)], writes=[("qc", gi, 0)])
        p.add("vector", lambda e, gi=gi: e.tensor_scalar_mul(
            sb["qcell"][0:1, gi * 2 + 1:gi * 2 + 2],
            sb["qcell"][0:1, gi * 2 + 1:gi * 2 + 2], 0.5),
            reads=[("qc", gi, 1)], writes=[("qc", gi, 1)])
        p.add("tensor", lambda e, gi=gi: e.matmul(
            ps["c"][:, 0:2], sb["onescol"][0:1, 0:1].broadcast_to([1, 128]),
            sb["qcell"][0:1, gi * 2:gi * 2 + 2], start=True, stop=True),
            reads=[("qc", gi, 0), ("qc", gi, 1), "onescol"], writes=["ps_c"])
        p.add("scalar", lambda e, gi=gi: e.copy(
            sb["qcol"][:, gi * 2:gi * 2 + 2], ps["c"][:, 0:2]),
            reads=["ps_c"], writes=[("qr", gi)])
    p.dma("gpsimd", dr["sc1"], sb["qcol"][:, 1:2], reads=[("qr", 0)],
          writes=["sc1o"])
    p.dma("gpsimd", dr["sc2"], sb["qcol"][:, 3:4], reads=[("qr", 1)],
          writes=["sc2o"])

    def qquant(wnm, onm, gi):
        nrow, ncol = dr[wnm].shape
        for t_ in range((nrow // 128) * (ncol // 512)):
            r, c_ = t_ // (ncol // 512), t_ % (ncol // 512)
            buf, tau = (("wt0", "tau0"), ("wt1", "tau1"))[t_ % 2]
            p.dma("scalar", sb[buf],
                  dr[wnm][r * 128:(r + 1) * 128, c_ * 512:(c_ + 1) * 512],
                  writes=[(buf,), "hsc"])
            p.add("scalar", lambda e, buf=buf, gi=gi: e.activation(
                out=sb["sg1"], in_=sb[buf], func=AF.Sign,
                bias=sb["nhalf"][:, 0:1], scale=sb["qcol"][:, gi * 2:gi * 2 + 1]),
                reads=[(buf,), "nhalf", ("qr", gi)], writes=["sg1", "crep"])
            p.add("scalar", lambda e, buf=buf, gi=gi: e.activation(
                out=sb["sg2"], in_=sb[buf], func=AF.Sign,
                bias=sb["half"][:, 0:1], scale=sb["qcol"][:, gi * 2:gi * 2 + 1]),
                reads=[(buf,), "half", ("qr", gi)], writes=["sg2", "crep"])
            p.add("vector", lambda e, tau=tau: e.tensor_add(
                sb[tau], sb["sg1"], sb["sg2"]),
                reads=["sg1", "sg2"], writes=[(tau,), "crep"])
            p.dma("gpsimd",
                  dr[onm][r * 128:(r + 1) * 128, c_ * 512:(c_ + 1) * 512],
                  sb[tau], reads=[(tau,)], writes=[("qw", onm)])
    qquant("w1T", "qw1", 0)
    qquant("w2T", "qw2", 1)

    for t_ in range((DI // 128) * (D // 512)):
        r, c_ = t_ // (D // 512), t_ % (D // 512)
        tau = ("tau0", "tau1")[t_ % 2]
        p.dma("gpsimd", sb[tau],
              dr["outwT"][r * 128:(r + 1) * 128, c_ * 512:(c_ + 1) * 512],
              writes=[(tau,), "crep"])
        p.dma("gpsimd", dr["obw"][r * 128:(r + 1) * 128, c_ * 512:(c_ + 1) * 512],
              sb[tau], reads=[(tau,)], writes=[("obw",)])

    for q in range(4):
        p.dma("gpsimd",
              sb["hTb"].ap().rearrange("p (ft t) -> p ft t", ft=16)[:, q * 4:(q + 1) * 4, :],
              dr["hT"].rearrange("(ft p) t -> p ft t", p=128)[:, q * 4:(q + 1) * 4, :],
              writes=[("hTb", q)])
    hTb_toks = [("hTb", q) for q in range(4)]

    for ct in range(NCT):
        wbuf = "wct0"
        p.dma("gpsimd", sb[wbuf].ap().rearrange("p (kt o) -> p kt o", kt=16),
              dr["in_wT"][:, ct * 128:(ct + 1) * 128].rearrange(
                  "(kt p) o -> p kt o", p=128),
              writes=[(wbuf,)])
        for tc in range(4):
            xb = ("x0", "x1")[tc % 2]
            for ki in range(16):
                p.add("tensor", lambda e, ki=ki, tc=tc, xb=xb, wbuf=wbuf: e.matmul(
                    ps[xb][:, 0:512], sb[wbuf][:, ts(ki, 128)],
                    sb["hTb"][:, ki * L + tc * 512: ki * L + (tc + 1) * 512],
                    start=(ki == 0), stop=(ki == 15)),
                    reads=[(wbuf,)] + hTb_toks, writes=["ps_" + xb]),
            p.add("scalar", lambda e, tc=tc, xb=xb: e.copy(
                xz_sl[:, ts(tc, 512)], ps[xb][:, 0:512]),
                reads=["ps_" + xb], writes=[("xz", tc)])
        xztoks = [("xz", tc) for tc in range(4)]
        w3c = sb["convws"][:, ct * 4 + 3: ct * 4 + 4]
        p.add("vector", lambda e, w3c=w3c, ct=ct: e.tensor_scalar(
            cv_sl, xz_sl, w3c, sb["convbs"][:, ct:ct + 1], ALU.mult, ALU.add),
            reads=xztoks + ["convw", "convb"], writes=["cv"])
        for kk in (2, 1, 0):
            sh = 3 - kk
            wc = sb["convws"][:, ct * 4 + kk: ct * 4 + kk + 1]
            p.add("vector", lambda e, wc=wc, sh=sh: e.scalar_tensor_tensor(
                cv_sl[:, sh:L], xz_sl[:, 0:L - sh], wc, cv_sl[:, sh:L],
                ALU.mult, ALU.add),
                reads=xztoks + ["cv", "convw"], writes=["cv"])
        p.add("scalar", lambda e: e.activation(out=xz_sl, in_=cv_sl, func=AF.Silu),
              reads=["cv"], writes=[("xz", 0), ("xz", 1), ("xz", 2), ("xz", 3)])
        p.dma("gpsimd", dr["xis"][ct * 128:(ct + 1) * 128, :], xz_sl,
              reads=xztoks, writes=[("xis", ct)])
        p.add("vector", lambda e: e.tensor_copy(sb["xibf"][:], xz_sl),
              reads=xztoks, writes=["xibf"])
        xpj = ("xpj0", "xpj1")[ct % 2]
        p.dma("gpsimd", sb[xpj][:], dr["xprojT"][ct * 128:(ct + 1) * 128, :],
              writes=[(xpj,)])
        for tc in range(4):
            p.add("tensor", lambda e, tc=tc, xpj=xpj, ct=ct: e.matmul(
                ps["d%d" % tc][:, 0:512], sb[xpj][:, 0:128],
                sb["xibf"][:, ts(tc, 512)], start=(ct == 0), stop=(ct == NCT - 1)),
                reads=[(xpj,), "xibf"], writes=["ps_d%d" % tc])
            p.add("tensor", lambda e, tc=tc, xpj=xpj: e.matmul(
                ps["bcp"][0:32, 0:512], sb[xpj][:, 128:160],
                sb["xibf"][:, ts(tc, 512)], start=True, stop=True),
                reads=[(xpj,), "xibf"], writes=["ps_bcp"])
            if ct == 0:
                p.add("vector", lambda e, tc=tc: e.tensor_copy(
                    sb["bc"][:, ts(tc, 512)], ps["bcp"][0:32, 0:512]),
                    reads=["ps_bcp"], writes=[("bc", tc)])
            else:
                p.add("vector", lambda e, tc=tc: e.tensor_add(
                    sb["bc"][:, ts(tc, 512)], sb["bc"][:, ts(tc, 512)],
                    ps["bcp"][0:32, 0:512]),
                    reads=["ps_bcp", ("bc", tc)], writes=[("bc", tc)])
        p.dma("gpsimd", sb["wct1"].ap().rearrange("p (kt o) -> p kt o", kt=16),
              dr["in_wT"][:, DI + ct * 128: DI + (ct + 1) * 128].rearrange(
                  "(kt p) o -> p kt o", p=128),
              writes=[("wct1",)])
        for tc in range(4):
            xb = ("x0", "x1")[tc % 2]
            for ki in range(16):
                p.add("tensor", lambda e, ki=ki, tc=tc, xb=xb: e.matmul(
                    ps[xb][:, 0:512], sb["wct1"][:, ts(ki, 128)],
                    sb["hTb"][:, ki * L + tc * 512: ki * L + (tc + 1) * 512],
                    start=(ki == 0), stop=(ki == 15)),
                    reads=[("wct1",)] + hTb_toks, writes=["ps_" + xb]),
            p.add("scalar", lambda e, tc=tc, xb=xb: e.activation(
                out=cv_sl[:, ts(tc, 512)], in_=ps[xb][:, 0:512], func=AF.Silu),
                reads=["ps_" + xb], writes=["cv"])
        p.dma("gpsimd", dr["zss"][ct * 128:(ct + 1) * 128, :], cv_sl,
              reads=["cv"], writes=[("zss", ct)])
    for tc in range(4):
        p.add("scalar", lambda e, tc=tc: e.copy(
            sb["dtrb"][:, ts(tc, 512)], ps["d%d" % tc][:, 0:512]),
            reads=["ps_d%d" % tc], writes=["dtrb"])
    if stop_after == "passX":
        p.add("gpsimd", lambda e: e.nop(),
              reads=[("xis", ct) for ct in range(NCT)]
                    + [("zss", ct) for ct in range(NCT)] + ["dtrb"])
        p.finalize()
        return nc, es

    route_ctr = [0]

    def route():
        route_ctr[0] += 1
        return "vector" if (route_ctr[0] * 29) % 50 < 29 else "gpsimd"

    for g in range(NCT // CTG):
        cts = list(range(g * CTG, (g + 1) * CTG))
        for ci, ct in enumerate(cts):
            p.dma("sync", xi4_sl(ci), dr["xis"][ct * 128:(ct + 1) * 128, :],
                  reads=[("xis", ct)],
                  writes=[("xi4", ci)] + ([("hTb", q) for q in range(4)]
                                          if g == 0 else []))
            wq = ("wct0", "wct1")[ci % 2]
            p.dma("gpsimd", sb[wq][:, 0:128],
                  dr["dtwT"][:, ct * 128:(ct + 1) * 128], writes=[(wq,)])
            for tc in range(4):
                xb = ("x0", "x1")[tc % 2]
                p.add("tensor", lambda e, tc=tc, xb=xb, wq=wq: e.matmul(
                    ps[xb][:, 0:512], sb[wq][:, 0:128],
                    sb["dtrb"][:, ts(tc, 512)], start=True, stop=True),
                    reads=[(wq,), "dtrb"], writes=["ps_" + xb])
                dtbc = sb["dtbs"][:, ct:ct + 1]
                p.add("scalar", lambda e, xb=xb, dtbc=dtbc: e.activation(
                    out=sb["lsa"][:], in_=ps[xb][:, 0:512], func=AF.Abs, bias=dtbc),
                    reads=["ps_" + xb, "dtb"], writes=["lsa"])
                p.add("scalar", lambda e: e.activation(
                    out=sb["lsb"], in_=sb["lsa"][:], func=AF.Exp, scale=-1.0),
                    reads=["lsa"], writes=["htm"])
                p.add("scalar", lambda e: e.activation(
                    out=sb["lsc"], in_=sb["lsb"], func=AF.Ln, bias=1.0),
                    reads=["htm"], writes=["htm"])
                p.add("scalar", lambda e, xb=xb, dtbc=dtbc: e.activation(
                    out=sb["lsa"][:], in_=ps[xb][:, 0:512], func=AF.Relu, bias=dtbc),
                    reads=["ps_" + xb, "dtb"], writes=["lsa"])
                p.add("vector", lambda e, ci=ci, tc=tc: e.tensor_add(
                    dt4_sl(ci)[:, ts(tc, 512)], sb["lsa"][:], sb["lsc"]),
                    reads=["lsa", "htm"],
                    writes=[("dt4", ci)] + ([("hTb", q) for q in range(4)]
                                            if g == 0 else []))
            p.add("vector", lambda e, ci=ci: e.tensor_mul(
                sb["dx4"][:, ts(ci, L)], dt4_sl(ci), xi4_sl(ci)),
                reads=[("dt4", ci), ("xi4", ci)], writes=[("dx4", ci)])
        for s in range(DS):
            p.add("vector", lambda e, s=s: e.tensor_single_scalar(
                sb["selm"][:, 257:258], sb["selm"][:, 256:257], float(s),
                ALU.is_equal),
                reads=["pidx"], writes=["selc0"])
            p.add("vector", lambda e, s=s: e.tensor_single_scalar(
                sb["selm"][:, 258:259], sb["selm"][:, 256:257], float(16 + s),
                ALU.is_equal),
                reads=["pidx"], writes=["selc1"])
            p.add("vector", lambda e: e.tensor_copy(
                sb["selm"][:, 0:128], sb["selm"][:, 257:258].broadcast_to([32, 128])),
                reads=["selc0"], writes=["selm"])
            p.add("vector", lambda e: e.tensor_copy(
                sb["selm"][:, 128:256], sb["selm"][:, 258:259].broadcast_to([32, 128])),
                reads=["selc1"], writes=["selm"])
            for tc in range(4):
                p.add("tensor", lambda e, s=s, tc=tc: e.matmul(
                    ps["d%d" % tc][:, 0:512], sb["selm"][:, 0:128],
                    sb["bc"][0:32, ts(tc, 512)], start=True, stop=True),
                    reads=[("bc", tc), "selm"], writes=["ps_d%d" % tc]),
                p.add("tensor", lambda e, s=s, tc=tc: e.matmul(
                    ps["c"][:, 0:512], sb["selm"][:, 128:256],
                    sb["bc"][0:32, ts(tc, 512)], start=True, stop=True),
                    reads=[("bc", tc), "selm"], writes=["ps_c"])
                p.add("scalar", lambda e, tc=tc: e.copy(
                    sb["crep"][:, ts(tc, 512)], ps["c"][:, 0:512]),
                    reads=["ps_c", "crep"], writes=["crep"])
            for ci, ct in enumerate(cts):
                dab = da_sl[ci % 2]
                datok = "da%d" % (ci % 2)
                p.add("scalar", lambda e, ci=ci, ct=ct, s=s, dab=dab: e.activation(
                    out=dab, in_=dt4_sl(ci), func=AF.Exp,
                    scale=sb["acol"][:, ct * DS + s: ct * DS + s + 1]),
                    reads=[("dt4", ci), "acol"],
                    writes=[datok] + ([("xz", t) for t in range(4)]
                                      if (ci % 2) == 0 else ["cv"]))
                for tc in range(4):
                    p.add("vector", lambda e, ci=ci, tc=tc: e.tensor_mul(
                        dbx_sl[:, ts(tc, 512)],
                        sb["dx4"][:, ci * L + tc * 512: ci * L + (tc + 1) * 512],
                        ps["d%d" % tc][:, 0:512]),
                        reads=[("dx4", ci), "ps_d%d" % tc], writes=["dbx"])
                p.add("vector", lambda e, dab=dab: e.tensor_tensor_scan(
                    sb["hsc"][:], dab, dbx_sl, 0.0, ALU.mult, ALU.add),
                    reads=[datok, "dbx"], writes=["hsc"])
                if s == 0:
                    p.add(route(), lambda e, ci=ci: e.tensor_mul(
                        sb["y4"][:, ts(ci, L)], sb["hsc"][:], sb["crep"][:]),
                        reads=["hsc", "crep"], writes=[("y4", ci)])
                else:
                    p.add(route(), lambda e: e.tensor_mul(
                        htm_sl, sb["hsc"][:], sb["crep"][:]),
                        reads=["hsc", "crep"], writes=["htm"])
                    p.add(route(), lambda e, ci=ci: e.tensor_add(
                        sb["y4"][:, ts(ci, L)], sb["y4"][:, ts(ci, L)], htm_sl),
                        reads=["htm", ("y4", ci)], writes=[("y4", ci)])
        for ci, ct in enumerate(cts):
            p.add("vector", lambda e, ci=ci, ct=ct: e.scalar_tensor_tensor(
                sb["y4"][:, ts(ci, L)], xi4_sl(ci), sb["dparms"][:, ct:ct + 1],
                sb["y4"][:, ts(ci, L)], ALU.mult, ALU.add),
                reads=[("xi4", ci), "dparm", ("y4", ci)], writes=[("y4", ci)])
            p.dma("sync", zst_sl, dr["zss"][ct * 128:(ct + 1) * 128, :],
                  reads=[("zss", ct)], writes=["dbx"])
            p.add("vector", lambda e, ci=ci: e.tensor_mul(
                sb["xibf"][:], sb["y4"][:, ts(ci, L)], zst_sl),
                reads=[("y4", ci), "dbx", "xibf"], writes=["xibf"])
            p.dma("gpsimd", dr["ygs"][ct * 128:(ct + 1) * 128, :], sb["xibf"][:],
                  reads=["xibf"], writes=[("ygs", ct)])
        if stop_after == "g%d" % g:
            p.add("gpsimd", lambda e: e.nop(),
                  reads=[("ygs", ct) for ct in cts])
            p.finalize()
            return nc, es

    ygs_toks = [("ygs", ct) for ct in range(NCT)]
    for tsb in range(4):
        for ct in range(NCT):
            p.dma("sync", ygres[:, ts(ct, 512)],
                  dr["ygs"][ct * 128:(ct + 1) * 128, ts(tsb, 512)],
                  reads=ygs_toks,
                  writes=[("ygr", ct)] + ([("xi4", ci) for ci in range(4)] +
                                          [("dt4", ci) for ci in range(4)]
                                          if tsb == 0 else []))
        ygr_toks = [("ygr", ct) for ct in range(NCT)]
        for oc in range(4):
            p.dma("sync", obstr[:, 0:NCT * 512].rearrange("p (kt o) -> p kt o", kt=NCT),
                  dr["obw"][:, ts(oc, 512)].rearrange("(kt p) o -> p kt o", p=128),
                  reads=[("obw",)],
                  writes=["obstr"] + ([("dx4", ci) for ci in range(4)]
                                      if tsb == 0 and oc == 0 else []))
            for tt in range(4):
                xb = ("x0", "x1")[tt % 2]
                for ct in range(NCT):
                    p.add("tensor", lambda e, ct=ct, tt=tt, xb=xb: e.matmul(
                        ps[xb][:, 0:512],
                        ygres[:, ct * 512 + tt * 128: ct * 512 + (tt + 1) * 128],
                        obstr[:, ts(ct, 512)], start=(ct == 0), stop=(ct == NCT - 1)),
                        reads=["obstr"] + ygr_toks, writes=["ps_" + xb])
                p.add("scalar", lambda e, xb=xb: e.copy(
                    sb["crep"][:, 0:512], ps[xb][:, 0:512]),
                    reads=["ps_" + xb], writes=["most", "crep"])
                p.dma("gpsimd",
                      dr["mo"][tsb * 512 + tt * 128: tsb * 512 + (tt + 1) * 128,
                               ts(oc, 512)],
                      sb["crep"][:, 0:512], reads=["most"], writes=[("moo", tsb, tt, oc)])
    p.add("gpsimd", lambda e: e.nop(),
          reads=[("moo", a, b, c_) for a in range(4) for b in range(4)
                 for c_ in range(4)]
                + [("qw", "qw1"), ("qw", "qw2"), "sc1o", "sc2o"])
    p.finalize()
    return nc, es


def _mamba_layout(prm):
    r32 = lambda v: np.ascontiguousarray(v.reshape(32, 128).T)
    return dict(
        in_wT=np.ascontiguousarray(prm["in_w"].T),
        convw=np.ascontiguousarray(
            prm["conv_w"].reshape(32, 128, 4).transpose(1, 0, 2).reshape(128, 128)),
        convb=r32(prm["conv_b"]),
        xprojT=np.ascontiguousarray(prm["xproj_w"].T),
        dtwT=np.ascontiguousarray(prm["dt_w"].T),
        dtb=r32(prm["dt_b"]),
        alog=np.ascontiguousarray(
            prm["A_log"].reshape(32, 128, DS).transpose(1, 0, 2).reshape(128, 32 * DS)),
        dparm=r32(prm["Dp"]),
        outwT=np.ascontiguousarray(prm["out_w"].T),
    )


def run_launch_D(h_full, fm, bm, w1, w2):
    nc, es = build_D()
    w1T = np.ascontiguousarray(w1.T)
    w2T = np.ascontiguousarray(w2.T)
    lay = {0: _mamba_layout(fm), 1: _mamba_layout(bm)}
    in_maps = []
    for core in range(8):
        b, dd = core // 2, core % 2
        hh = h_full[b] if dd == 0 else h_full[b][::-1]
        m = dict(lay[dd])
        m["hT"] = np.ascontiguousarray(hh.T)
        m["w1T"] = w1T
        m["w2T"] = w2T
        m["pidx"] = np.arange(32, dtype=np.float32).reshape(32, 1)
        in_maps.append(m)
    res = run_bass_kernel_spmd(nc, in_maps, core_ids=list(range(8)))
    es.__exit__(None, None, None)
    fwd = np.stack([res.results[2 * b]["mo"] for b in range(B)])
    bwd = np.stack([res.results[2 * b + 1]["mo"][::-1] for b in range(B)])
    r0 = res.results[0]
    return fwd, bwd, r0["qw1"], r0["qw2"], r0["sc1"], r0["sc2"]

def build_E():
    nc = bass.Bass("TRN2", target_bir_lowering=False, debug=False)
    es = contextlib.ExitStack()
    es.__enter__()
    dr = {}
    for nm, shp, dt, kind in [
        ("x3", [1024, D], F32, "ExternalInput"),
        ("hloc", [1024, D], F32, "ExternalInput"),
        ("fwd", [1024, D], F32, "ExternalInput"),
        ("bwdf", [1024, D], F32, "ExternalInput"),
        ("lngc", [128, 16], F32, "ExternalInput"),
        ("lnbc", [128, 16], F32, "ExternalInput"),
        ("w_outT", [D, D], F32, "ExternalInput"),
        ("borow", [1, D], F32, "ExternalInput"),
        ("qw1", [D, 4 * D], BF16, "ExternalInput"),
        ("qw2", [4 * D, D], BF16, "ExternalInput"),
        ("sc1", [128, 1], F32, "ExternalInput"),
        ("sc2", [128, 1], F32, "ExternalInput"),
        ("eyebf", [128, 128], BF16, "ExternalInput"),
        ("out", [1024, D], F32, "ExternalOutput"),
        ("x4s", [1024, D], F32, "Internal"),
    ]:
        dr[nm] = nc.dram_tensor(nm, shp, dt, kind=kind).ap()

    sb = {}

    def A(nm, shp, dt=F32):
        sb[nm] = es.enter_context(nc.sbuf_tensor(nm, shp, dt))
        return sb[nm]

    A("eye", [128, 128], BF16)
    A("onesb", [1, 128], BF16)
    A("lng", [128, 16]); A("lnb", [128, 16], BF16)
    A("scc", [128, 8])
    A("bo", [1, D], BF16)
    A("bw", [1, D], BF16)
    A("xt0", [128, D]); A("xt1", [128, D])
    A("stat", [128, 24])
    A("xln", [128, D], BF16)
    A("rT", [128, 16 * 1024], BF16)
    A("wstr0", [128, 16 * 512], BF16); A("wstr1", [128, 16 * 512], BF16)
    A("at2", [128, 2 * D])
    A("dmy", [128, D])
    A("hst", [128, 4 * D])
    A("hT", [128, 64 * 128], BF16)
    A("q0", [128, 512]); A("q1", [128, 512], I32); A("q2", [128, 512])
    A("q3", [128, 512]); A("q4", [128, 512])
    A("xqb", [128, D], BF16)
    A("o2", [128, D])
    ps = {}
    for nm in ("p0", "p1", "r1", "c"):
        ps[nm] = es.enter_context(nc.psum_tensor("psE_" + nm, [128, 512], F32))
    ps["tr"] = es.enter_context(nc.psum_tensor("psE_tr", [128, 512], BF16))

    p = Prog(nc, es)
    p.dma("sync", sb["eye"][:], dr["eyebf"], writes=["eye"])
    p.dma("sync", sb["lng"][:], dr["lngc"], writes=["lng"])
    p.dma("gpsimd", sb["lnb"][:], dr["lnbc"], writes=["lnb"])
    p.dma("sync", sb["scc"][:, 0:1], dr["sc1"], writes=["sc1"])
    p.dma("sync", sb["scc"][:, 1:2], dr["sc2"], writes=["sc2"])
    p.dma("gpsimd", sb["bo"][:], dr["borow"], writes=["bo"])
    p.add("vector", lambda e: e.memset(sb["onesb"][:], 1.0), writes=["onesb"])

    def ln_tile(xt_, xtoks, eps, outsl, outtok, sidx, nchunk=1):
        if not isinstance(xtoks, list):
            xtoks = [xtoks]
        st = sb["stat"]
        c0, c1, c2, c3 = (st[:, 4 * sidx + j:4 * sidx + j + 1] for j in range(4))
        n = xt_.shape[1]
        nf = 1.0 / n
        p.add("vector", lambda e, x=xt_, o=c0: e.reduce_sum(o, x, AX.X),
              reads=xtoks, writes=[("s0", sidx)])
        cw = n // nchunk
        for hc in range(nchunk):
            acc = c1 if hc == 0 else sb["stat"][:, 20 + hc:21 + hc]
            p.add("scalar", lambda e, x=xt_, a=acc, hc=hc, cw=cw: e.activation(
                out=sb["dmy"][:, 0:cw], in_=x[:, hc * cw:(hc + 1) * cw],
                func=AF.Square, accum_out=a),
                reads=xtoks, writes=[("sq", sidx, hc), "lndmy"])
        for hc in range(1, nchunk):
            p.add("vector", lambda e, a=c1, hc=hc: e.tensor_add(
                a, a, sb["stat"][:, 20 + hc:21 + hc]),
                reads=[("sq", sidx, 0), ("sq", sidx, hc)], writes=[("sq", sidx, 0)])
        p.add("vector", lambda e, a=c2, b=c0, nf=nf: e.tensor_scalar_mul(a, b, nf),
              reads=[("s0", sidx)], writes=[("sm", sidx)])
        p.add("vector", lambda e, a=c1, nf=nf, eps=eps: e.tensor_scalar(
            a, a, nf, eps, ALU.mult, ALU.add),
            reads=[("sq", sidx, 0)], writes=[("s1", sidx)])
        p.add("vector", lambda e, a=c3, b=c2: e.scalar_tensor_tensor(
            a, b, 1.0, b, ALU.mult, ALU.mult),
            reads=[("sm", sidx)], writes=[("smm", sidx)])
        p.add("vector", lambda e, a=c1, b=c3: e.tensor_sub(a, a, b),
              reads=[("s1", sidx), ("smm", sidx)], writes=[("s1", sidx)])
        p.add("scalar", lambda e, a=c3, b=c1: e.activation(out=a, in_=b, func=AF.Sqrt),
              reads=[("s1", sidx)], writes=[("ssq", sidx)])
        p.add("vector", lambda e, a=c1, b=c3: e.reciprocal(a, b),
              reads=[("ssq", sidx)], writes=[("srs", sidx)])
        p.add("vector", lambda e, x=xt_, m=c2, r=c1, o=outsl: e.tensor_scalar(
            o, x, m, r, ALU.subtract, ALU.mult),
            reads=xtoks + [("sm", sidx), ("srs", sidx)], writes=[outtok])

    def transposes(n, src_fn, srctoks, dst_fn, dtok):
        for d in range(n):
            p.add("tensor", lambda e, d=d: e.transpose(
                ps["tr"][:, 0:128], src_fn(d), sb["eye"][:]),
                reads=srctoks + ["eye"], writes=["ps_tr"])
            p.add("scalar", lambda e, d=d: e.copy(dst_fn(d), ps["tr"][:, 0:128]),
                  reads=["ps_tr"], writes=[dtok])

    def quant_chunk(xsl, xtoks, scol, stok, osl, otok):
        p.add("vector", lambda e, x=xsl, s=scol: e.tensor_scalar_mul(
            sb["q0"][:], x, s), reads=xtoks + [stok], writes=["q0"])
        p.add("vector", lambda e: e.tensor_copy(sb["q1"][:], sb["q0"][:]),
              reads=["q0"], writes=["q1"])
        p.add("vector", lambda e: e.tensor_copy(sb["q2"][:], sb["q1"][:]),
              reads=["q1"], writes=["q2"])
        p.add("vector", lambda e: e.tensor_sub(sb["q3"][:], sb["q0"][:], sb["q2"][:]),
              reads=["q0", "q2"], writes=["q3"])
        p.add("vector", lambda e: e.tensor_single_scalar(
            sb["q4"][:], sb["q3"][:], 0.5, ALU.is_ge),
            reads=["q3"], writes=["q4"])
        p.add("vector", lambda e: e.scalar_tensor_tensor(
            sb["q3"][:], sb["q3"][:], -0.5, sb["q4"][:], ALU.is_le, ALU.subtract),
            reads=["q3", "q4"], writes=["q3"])
        p.add("vector", lambda e, o=osl: e.tensor_sub(o, sb["q2"][:], sb["q3"][:]),
              reads=["q2", "q3"], writes=[otok])

    for tt in range(8):
        xt_ = sb["xt0"][:]
        p.dma("sync", xt_, dr["fwd"][tt * 128:(tt + 1) * 128, :], writes=["xld"])
        p.dma("sync", sb["xt1"][:], dr["bwdf"][tt * 128:(tt + 1) * 128, :],
              writes=["xt1"])
        p.add("vector", lambda e: e.tensor_add(sb["xt0"][:], sb["xt0"][:], sb["xt1"][:]),
              reads=["xld", "xt1"], writes=["xld"])
        p.dma("sync", sb["xt1"][:], dr["hloc"][tt * 128:(tt + 1) * 128, :],
              writes=["xt1"])
        p.add("vector", lambda e: e.tensor_add(sb["xt0"][:], sb["xt0"][:], sb["xt1"][:]),
              reads=["xld", "xt1"], writes=["xld"])
        ln_tile(xt_, "xld", 1e-5, sb["xln"][:], "xln", tt % 4)
        transposes(16, lambda d: sb["xln"][:, ts(d, 128)], ["xln"],
                   lambda d, tt=tt: sb["rT"][:, d * 1024 + tt * 128:
                                            d * 1024 + (tt + 1) * 128],
                   ("rT", tt))
    rT_toks = [("rT", t) for t in range(8)]
    for sbi in range(4):
        for oc in range(4):
            wbuf = ("wstr0", "wstr1")[oc % 2]
            p.dma("gpsimd", sb[wbuf].ap().rearrange("p (kt o) -> p kt o", kt=16),
                  dr["w_outT"][:, ts(oc, 512)].rearrange("(kt p) o -> p kt o", p=128),
                  writes=[(wbuf,)])
            if sbi == 0:
                for ki in range(16):
                    p.add("tensor", lambda e, ki=ki, wbuf=wbuf: e.matmul(
                        ps["r1"][0:1, 0:512], sb["lnb"][:, ki:ki + 1],
                        sb[wbuf][:, ts(ki, 512)], start=(ki == 0), stop=(ki == 15)),
                        reads=[(wbuf,), "lnb"], writes=["ps_r1"])
                p.add("scalar", lambda e, oc=oc: e.copy(
                    sb["bw"][0:1, ts(oc, 512)], ps["r1"][0:1, 0:512]),
                    reads=["ps_r1"], writes=["bw"])
            for ki in range(16):
                p.add("vector", lambda e, ki=ki, wbuf=wbuf: e.tensor_scalar_mul(
                    sb[wbuf][:, ts(ki, 512)], sb[wbuf][:, ts(ki, 512)],
                    sb["lng"][:, ki:ki + 1]),
                    reads=[(wbuf,), "lng"] + (["bw"] if sbi == 0 else []),
                    writes=[(wbuf,)])
            for tt in range(2):
                gt = sbi * 2 + tt
                for ki in range(16):
                    p.add("tensor", lambda e, ki=ki, wbuf=wbuf, gt=gt: e.matmul(
                        ps["p0"][:, 0:512],
                        sb["rT"][:, ki * 1024 + gt * 128: ki * 1024 + (gt + 1) * 128],
                        sb[wbuf][:, ts(ki, 512)], start=(ki == 0), stop=False),
                        reads=[(wbuf,)] + rT_toks, writes=["ps_p0"])
                p.add("tensor", lambda e, oc=oc: e.matmul(
                    ps["p0"][:, 0:512], sb["onesb"][0:1, :],
                    sb["bo"][0:1, ts(oc, 512)], start=False, stop=False),
                    reads=["bo", "onesb"], writes=["ps_p0"])
                p.add("tensor", lambda e, oc=oc: e.matmul(
                    ps["p0"][:, 0:512], sb["onesb"][0:1, :],
                    sb["bw"][0:1, ts(oc, 512)], start=False, stop=True),
                    reads=["bw", "onesb"], writes=["ps_p0"])
                p.add("scalar", lambda e, tt=tt, oc=oc: e.copy(
                    sb["at2"][:, tt * D + oc * 512: tt * D + (oc + 1) * 512],
                    ps["p0"][:, 0:512]),
                    reads=["ps_p0"], writes=[("a2", tt, oc)])
        for tt in range(2):
            gt = sbi * 2 + tt
            xt_ = sb["xt0"][:]
            p.dma("sync", xt_, dr["x3"][gt * 128:(gt + 1) * 128, :], writes=["xld"])
            a2sl = sb["at2"][:, ts(tt, D)]
            a2toks = [("a2", tt, o) for o in range(4)]
            ln_tile(a2sl, a2toks, 1e-5, sb["xt1"][:], "xt1", tt)
            p.add("vector", lambda e, x=xt_: e.tensor_add(
                sb["xt1"][:], sb["xt1"][:], x),
                reads=["xt1", "xld"], writes=["xt1"])
            p.dma("gpsimd", dr["x4s"][gt * 128:(gt + 1) * 128, :], sb["xt1"][:],
                  reads=["xt1"], writes=[("x4s", gt)])
    for gt in range(8):
        xt_ = sb["xt0"][:]
        p.dma("sync", xt_, dr["x4s"][gt * 128:(gt + 1) * 128, :],
              reads=[("x4s", gt)], writes=["xld"])
        ln_tile(xt_, "xld", 1e-5, sb["at2"][:, 0:D], "xf", 0)
        p.add("vector", lambda e: e.reduce_max(
            sb["scc"][:, 2:3], sb["at2"][:, 0:D], AX.X, apply_absolute_value=True),
            reads=["xf"], writes=["am"])
        p.add("vector", lambda e: e.tensor_single_scalar(
            sb["scc"][:, 2:3], sb["scc"][:, 2:3], 1e-5, ALU.max),
            reads=["am"], writes=["am"])
        p.add("vector", lambda e: e.reciprocal(sb["scc"][:, 3:4], sb["scc"][:, 2:3]),
              reads=["am"], writes=["sm1"])
        p.add("vector", lambda e: e.tensor_scalar_mul(
            sb["scc"][:, 3:4], sb["scc"][:, 3:4], 127.0),
            reads=["sm1"], writes=["sm1"])
        p.add("vector", lambda e: e.scalar_tensor_tensor(
            sb["scc"][:, 4:5], sb["scc"][:, 2:3], 1.0 / 127, sb["scc"][:, 0:1],
            ALU.mult, ALU.mult),
            reads=["am", "sc1"], writes=["es1"])
        for cc in range(4):
            quant_chunk(sb["at2"][:, ts(cc, 512)], ["xf"], sb["scc"][:, 3:4], "sm1",
                        sb["xqb"][:, ts(cc, 512)], "xqb")
        transposes(16, lambda d: sb["xqb"][:, ts(d, 128)], ["xqb"],
                   lambda d: sb["hT"][:, ts(d, 128)], "xqT")
        for oc in range(16):
            wbuf = ("wstr0", "wstr1")[oc % 2]
            p.dma("sync", sb[wbuf].ap().rearrange("p (kt o) -> p kt o", kt=16),
                  dr["qw1"][:, ts(oc, 512)].rearrange("(kt p) o -> p kt o", p=128),
                  writes=[(wbuf,)])
            for ki in range(16):
                p.add("tensor", lambda e, ki=ki, wbuf=wbuf: e.matmul(
                    ps["p0"][:, 0:512], sb["hT"][:, ts(ki, 128)],
                    sb[wbuf][:, ts(ki, 512)], start=(ki == 0), stop=(ki == 15)),
                    reads=[(wbuf,), "xqT"], writes=["ps_p0"])
            p.add("scalar", lambda e, oc=oc: e.activation(
                out=sb["hst"][:, ts(oc, 512)], in_=ps["p0"][:, 0:512], func=AF.Gelu,
                scale=sb["scc"][:, 4:5]),
                reads=["ps_p0", "es1"], writes=[("hs", oc)])
        hs_toks = [("hs", o) for o in range(16)]
        p.add("vector", lambda e: e.reduce_max(
            sb["scc"][:, 2:3], sb["hst"][:], AX.X, apply_absolute_value=True),
            reads=hs_toks, writes=["am"])
        p.add("vector", lambda e: e.tensor_single_scalar(
            sb["scc"][:, 2:3], sb["scc"][:, 2:3], 1e-5, ALU.max),
            reads=["am"], writes=["am"])
        p.add("vector", lambda e: e.reciprocal(sb["scc"][:, 3:4], sb["scc"][:, 2:3]),
              reads=["am"], writes=["sm1"])
        p.add("vector", lambda e: e.tensor_scalar_mul(
            sb["scc"][:, 3:4], sb["scc"][:, 3:4], 127.0),
            reads=["sm1"], writes=["sm1"])
        p.add("vector", lambda e: e.scalar_tensor_tensor(
            sb["scc"][:, 4:5], sb["scc"][:, 2:3], 1.0 / 127, sb["scc"][:, 1:2],
            ALU.mult, ALU.mult),
            reads=["am", "sc2"], writes=["es1"])
        for cc in range(16):
            quant_chunk(sb["hst"][:, ts(cc, 512)], hs_toks, sb["scc"][:, 3:4], "sm1",
                        sb["xqb"][:, 0:512], "xqb")
            transposes(4, lambda d, cc=cc: sb["xqb"][:, ts(d, 128)][:, 0:128],
                       ["xqb"],
                       lambda d, cc=cc: sb["hT"][:, (cc * 4 + d) * 128:
                                                 (cc * 4 + d + 1) * 128],
                       ("hT2", cc))
        hT2_toks = [("hT2", c) for c in range(16)]
        for oc in range(4):
            for kg in range(4):
                wbuf = ("wstr0", "wstr1")[kg % 2]
                p.dma("sync", sb[wbuf].ap().rearrange("p (kt o) -> p kt o", kt=16),
                      dr["qw2"][kg * 2048:(kg + 1) * 2048, ts(oc, 512)].rearrange(
                          "(kt p) o -> p kt o", p=128),
                      writes=[(wbuf,)])
                for k2 in range(16):
                    ki = kg * 16 + k2
                    p.add("tensor", lambda e, ki=ki, k2=k2, wbuf=wbuf: e.matmul(
                        ps["p1"][:, 0:512], sb["hT"][:, ts(ki, 128)],
                        sb[wbuf][:, ts(k2, 512)],
                        start=(ki == 0), stop=(ki == 63)),
                        reads=[(wbuf,)] + hT2_toks, writes=["ps_p1"])
            p.add("scalar", lambda e, oc=oc: e.activation(
                out=sb["o2"][:, ts(oc, 512)], in_=ps["p1"][:, 0:512],
                func=AF.Identity, scale=sb["scc"][:, 4:5]),
                reads=["ps_p1", "es1"], writes=[("o2w", oc)])
        o2_toks = [("o2w", o) for o in range(4)]
        ln_tile(sb["o2"][:], o2_toks, 1e-5, sb["xt1"][:], "xt1", 1)
        p.add("vector", lambda e: e.tensor_add(sb["xt1"][:], sb["xt1"][:], sb["xt0"][:]),
              reads=["xt1", "xld"], writes=["xt1"])
        p.dma("gpsimd", dr["out"][gt * 128:(gt + 1) * 128, :], sb["xt1"][:],
              reads=["xt1"], writes=[("oo", gt)])
    p.add("gpsimd", lambda e: e.nop(), reads=[("oo", g) for g in range(8)])
    p.finalize()
    return nc, es


def run_launch_E(x3, hloc, fwd, bwd, ln_g, ln_b, w_out, b_out, qw1, qw2,
                 sc1, sc2):
    nc, es = build_E()
    eyebf = _bf16(np.eye(128, dtype=np.float32))
    r16 = lambda v: np.ascontiguousarray(v.reshape(16, 128).T)
    w_outT = np.ascontiguousarray(w_out.T)
    borow = b_out.reshape(1, D).astype(np.float32)
    in_maps = []
    for core in range(8):
        b, hh = core // 2, core % 2
        sl = slice(hh * 1024, (hh + 1) * 1024)
        in_maps.append(dict(
            x3=np.ascontiguousarray(x3[core]), hloc=np.ascontiguousarray(hloc[core]),
            fwd=np.ascontiguousarray(fwd[b, sl]),
            bwdf=np.ascontiguousarray(bwd[b, sl]),
            lngc=r16(ln_g), lnbc=r16(ln_b), w_outT=w_outT, borow=borow,
            qw1=qw1, qw2=qw2, sc1=sc1, sc2=sc2, eyebf=eyebf))
    res = run_bass_kernel_spmd(nc, in_maps, core_ids=list(range(8)))
    es.__exit__(None, None, None)
    return np.stack([res.results[c]["out"] for c in range(8)])




def _bf16(x):
    import ml_dtypes
    return np.asarray(x, dtype=ml_dtypes.bfloat16)


def run_launch_A(x, Wq, Wk, Wv, Wgk1, Wgk2, bgk2, wfcT=None, wprojT=None):
    nc, es = build_A()
    eyebf = _bf16(np.eye(128, dtype=np.float32))
    mask64 = np.zeros((128, C), np.float32)
    rr = np.arange(128) % 64
    mask64[rr[:, None] <= np.arange(C)[None, :]] = 1.0
    bgk2c = np.ascontiguousarray(bgk2.reshape(16, 128).T)
    zeros = np.zeros((1024, D), np.float32)
    in_maps = []
    for core in range(NC_ := 8):
        b, hh = core // 2, core % 2
        xp = zeros if hh == 0 else np.ascontiguousarray(x[b, 0:1024])
        xo = np.ascontiguousarray(x[b, hh * 1024:(hh + 1) * 1024])
        in_maps.append(dict(xp=xp, xo=xo, wq=Wq, wk=Wk, wv=Wv, wgk1=Wgk1,
                            wgk2=Wgk2, bgk2c=bgk2c, eyebf=eyebf, mask64=mask64,
                            wfcT=wfcT, wprojT=wprojT))
    res = run_bass_kernel_spmd(nc, in_maps, core_ids=list(range(8)))
    es.__exit__(None, None, None)
    og = np.zeros((B, L, D), np.float32)
    for core in range(8):
        b, hh = core // 2, core % 2
        og[b, hh * 1024:(hh + 1) * 1024] = res.results[core]["og"]
    gfc = res.results[0]["gabs_fc"]
    gproj = res.results[0]["gabs_proj"]
    return og, gfc, gproj


def kernel(**inputs):
    x = np.ascontiguousarray(inputs["L0_attn_x"], dtype=np.float32)
    wfcT = np.ascontiguousarray(inputs["L0_ffn_w_fc"].T)
    wprojT = np.ascontiguousarray(inputs["L0_ffn_w_proj"].T)
    og, gfc, gproj = run_launch_A(
        x, inputs["L0_attn_Wq"], inputs["L0_attn_Wk"], inputs["L0_attn_Wv"],
        inputs["L0_attn_Wgk1"], inputs["L0_attn_Wgk2"], inputs["L0_attn_bgk2"],
        wfcT, wprojT)
    x_half = np.stack([x[c // 2, (c % 2) * 1024:(c % 2 + 1) * 1024]
                       for c in range(8)])
    og_half = np.stack([og[c // 2, (c % 2) * 1024:(c % 2 + 1) * 1024]
                        for c in range(8)])
    x3, h = run_launch_B(
        x_half, og_half, inputs["L0_attn_Wg"], inputs["L0_attn_Wo"],
        inputs["L0_attn_gnorm_w"], inputs["L0_ffn_w_fc"],
        inputs["L0_ffn_w_proj"], inputs["L1_attn_w_in"], inputs["L1_attn_b_in"],
        gfc, gproj)
    h_full = np.stack([np.concatenate([h[2 * b], h[2 * b + 1]], axis=0)
                       for b in range(B)])
    pdict = lambda pre: dict(
        in_w=inputs[f"L1_attn_{pre}_in_w"], conv_w=inputs[f"L1_attn_{pre}_conv_w"],
        conv_b=inputs[f"L1_attn_{pre}_conv_b"],
        xproj_w=inputs[f"L1_attn_{pre}_xproj_w"],
        dt_w=inputs[f"L1_attn_{pre}_dt_w"], dt_b=inputs[f"L1_attn_{pre}_dt_b"],
        A_log=inputs[f"L1_attn_{pre}_A_log"], Dp=inputs[f"L1_attn_{pre}_D"],
        out_w=inputs[f"L1_attn_{pre}_out_w"])
    fwd, bwd, qw1, qw2, sc1, sc2 = run_launch_D(
        h_full, pdict("fm"), pdict("bm"), inputs["L1_ffn_w1"], inputs["L1_ffn_w2"])
    h_half = np.stack([h_full[c // 2, (c % 2) * 1024:(c % 2 + 1) * 1024]
                       for c in range(8)])
    out = run_launch_E(
        x3, h_half, fwd, bwd, inputs["L1_attn_ln_g"], inputs["L1_attn_ln_b"],
        inputs["L1_attn_w_out"], inputs["L1_attn_b_out"], qw1, qw2, sc1, sc2)
    full = np.zeros((B, L, D), np.float32)
    for c in range(8):
        full[c // 2, (c % 2) * 1024:(c % 2 + 1) * 1024] = out[c]
    return full
